# version 09
# baseline (speedup 1.0000x reference)
import math
import numpy as np
import ml_dtypes
import concourse.bass as bass
import concourse.mybir as mybir
from concourse.bass_utils import run_bass_kernel_spmd
from concourse.bass_core_rust import IndirectOffsetOnAxis

F32 = mybir.dt.float32
BF16 = mybir.dt.bfloat16
I32 = mybir.dt.int32
U32 = mybir.dt.uint32
ALU = mybir.AluOpType
ACT = mybir.ActivationFunctionType
AX = mybir.AxisListType

NEG = -1.0e30


class Cfg:
    def __init__(self, tiny=False, mid=False):
        if mid:
            self.NCORE, self.D, self.L = 2, 1024, 512
            self.NSEQ, self.NH, self.NKV = 1, 4, 2
            self.E, self.F, self.SLAB = 8, 256, 256
        elif tiny:
            self.NCORE, self.D, self.L = 2, 256, 256
            self.NSEQ, self.NH, self.NKV = 2, 2, 1
            self.E, self.F, self.SLAB = 8, 128, 128
        else:
            self.NCORE, self.D, self.L = 8, 1024, 2048
            self.NSEQ, self.NH, self.NKV = 4, 4, 2
            self.E, self.F, self.SLAB = 8, 2048, 512
        self.DC = self.D // 128
        self.HD = self.D // self.NH
        self.HDC = self.HD // 128
        self.NREP = self.NH // self.NKV
        self.TOWN = self.NSEQ * self.L
        self.T = self.NCORE * self.TOWN
        self.NSLAB_OWN = self.TOWN // self.SLAB
        self.NSLAB_ALL = self.T // self.SLAB
        self.NTILE_OWN = self.TOWN // 128
        self.TPS = self.SLAB // 128
        self.KT = self.L // 128
        self.NQS = self.L // self.SLAB
        self.F2 = 2 * self.F
        self.FC = self.F // 128
        self.CGLOB = int(math.ceil(2 * self.T / self.E * 1.25))
        cl = int(math.ceil(2 * self.TOWN / self.E * 1.25))
        self.CLOC = ((cl + 127) // 128) * 128
        self.TPE = self.CLOC // 128
        self.NSLOT = self.E * self.CLOC
        self.NT_IMG = (self.NSLOT + 2 * self.TOWN) // 128
        self.NHALF = 2 if (not tiny or mid) else 1
        assert self.TPE % self.NHALF == 0
        self.TPH = self.TPE // self.NHALF
        self.HSLOT = self.TPH * 128
        self.HSLABS = []
        r = self.HSLOT
        while r > 0:
            w = min(self.SLAB, r)
            self.HSLABS.append(w)
            r -= w


class Ch:
    def __init__(self, name, engine, step):
        self.name, self.engine, self.step = name, engine, step
        self.is_dma = step == 16
        self.n = 0
        self.sem = None


class Prog:
    def __init__(self):
        self.ops = []
        self.chans = []

    def chan(self, name, engine, step=1):
        ch = Ch(name, engine, step)
        self.chans.append(ch)
        return ch

    def add(self, engine, fn, waits=(), inc=None):
        w = [(cc, v) for (cc, v) in waits if cc is not None and v > 0]
        self.ops.append((engine, fn, w, inc))
        if inc is not None:
            inc.n += inc.step
            return (inc, inc.n)
        return None

    def barrier(self):
        return [(cc, cc.n) for cc in self.chans]

    def emit(self, nc):
        for cc in self.chans:
            cc.sem = nc.alloc_semaphore(name="ch_" + cc.name)
        streams = {}
        for op in self.ops:
            streams.setdefault(op[0], []).append(op)

        def make_replay(ename, myops):
            def replay(eng):
                waited = {}
                for (_, fn, waits, inc) in myops:
                    for (cc, v) in waits:
                        if waited.get(cc.name, 0) >= v:
                            continue
                        eng.wait_ge(cc.sem, v)
                        waited[cc.name] = v
                    inst = fn(eng)
                    if inc is not None:
                        inst.then_inc(inc.sem, inc.step)
            return replay

        with nc.Block() as block:
            decs = {"sync": block.sync, "scalar": block.scalar,
                    "vector": block.vector, "tensor": block.tensor,
                    "gpsimd": block.gpsimd}
            for ename, myops in streams.items():
                decs[ename](make_replay(ename, myops))


class B:
    def __init__(self, cfg):
        self.c = cfg
        self.nc = bass.Bass("TRN2", target_bir_lowering=False,
                            detect_race_conditions=False)
        self.P = Prog()
        P = self.P
        self.LD = P.chan("ld", "sync", 16)
        self.SD = P.chan("sd", "scalar", 16)
        self.GD = P.chan("gd", "gpsimd", 16)
        self.PE = P.chan("pe", "tensor", 1)
        self.DV = P.chan("dv", "vector", 1)
        self.SC = P.chan("sc", "scalar", 1)
        self.GP = P.chan("gp", "gpsimd", 1)
        used = self.nc.SBUF_PARTITION_SIZE_BYTES - self.nc.sbuf_bytes_remaining
        self._sb_off = (used + 31) // 32 * 32
        self.bk_w = [None] * 8
        self.bk_r = [None] * 8

    def dma(self, q, out, in_, waits=(), slow=False):
        ch = {"sync": self.LD, "scalar": self.SD, "gpsimd": self.GD}[q]
        return self.P.add(
            q, lambda e, o=out, i=in_, s=slow: e.dma_start(
                out=o, in_=i, allow_slow_non_contiguous=s),
            waits=waits, inc=ch)

    def mm(self, bank_i, out, lhsT, rhs, start, stop, waits=(), inc=True):
        w = list(waits)
        if start and self.bk_r[bank_i] is not None:
            w.append(self.bk_r[bank_i])
        tok = self.P.add(
            "tensor",
            lambda e, o=out, l=lhsT, r=rhs, s1=start, s2=stop:
                e.matmul(o, l, r, start=s1, stop=s2),
            waits=w, inc=self.PE if inc else None)
        if inc:
            self.bk_w[bank_i] = tok
        return tok

    def tp(self, bank_i, out, in_, ident, waits=()):
        w = list(waits)
        if self.bk_r[bank_i] is not None:
            w.append(self.bk_r[bank_i])
        tok = self.P.add(
            "tensor", lambda e, o=out, i=in_, d=ident: e.transpose(o, i, d),
            waits=w, inc=self.PE)
        self.bk_w[bank_i] = tok
        return tok

    def rd(self, bank_i, engine, fn, waits=()):
        w = list(waits)
        if self.bk_w[bank_i] is not None:
            w.append(self.bk_w[bank_i])
        ch = self.DV if engine == "vector" else self.SC
        tok = self.P.add(engine, fn, waits=w, inc=ch)
        self.bk_r[bank_i] = tok
        return tok

    def rd2(self, bank_i, bank_j, engine, fn, waits=()):
        w = list(waits)
        for bi in (bank_i, bank_j):
            if self.bk_w[bi] is not None:
                w.append(self.bk_w[bi])
        ch = self.DV if engine == "vector" else self.SC
        tok = self.P.add(engine, fn, waits=w, inc=ch)
        self.bk_r[bank_i] = tok
        self.bk_r[bank_j] = tok
        return tok

    def vec(self, fn, waits=()):
        return self.P.add("vector", fn, waits=waits, inc=self.DV)

    def sca(self, fn, waits=()):
        return self.P.add("scalar", fn, waits=waits, inc=self.SC)

    def gpd(self, fn, waits=()):
        return self.P.add("gpsimd", fn, waits=waits, inc=self.GD)

    def gpc(self, fn, waits=()):
        return self.P.add("gpsimd", fn, waits=waits, inc=self.GP)

    def sb(self, name, shape, dtype, offset=None):
        nbytes = int(np.prod(shape[1:])) * mybir.dt.size(dtype)
        if offset is None:
            offset = self._sb_off
            self._sb_off = (offset + nbytes + 31) // 32 * 32
        return self.nc.alloc_sbuf_tensor_at(name, shape, dtype, offset=offset).ap()


def build(cfg: Cfg, dbg=False, upto='F'):
    b = B(cfg)
    c = cfg
    nc, P = b.nc, b.P
    PE, DV, SC = b.PE, b.DV, b.SC

    DT = nc.dram_tensor
    xT = DT("xT", [c.D, c.T], F32, kind="ExternalInput").ap()
    own_x = DT("own_x", [c.TOWN, c.D], F32, kind="ExternalInput").ap()
    Wq = DT("Wq", [c.D, c.D], F32, kind="ExternalInput").ap()
    Wk = DT("Wk", [c.D, c.NKV * c.HD], F32, kind="ExternalInput").ap()
    Wv = DT("Wv", [c.D, c.NKV * c.HD], F32, kind="ExternalInput").ap()
    Wo = DT("Wo", [c.D, c.D], F32, kind="ExternalInput").ap()
    Wg = DT("Wg", [c.D, c.E], F32, kind="ExternalInput").ap()
    W_in = DT("W_in", [c.E * c.D, c.F2], F32, kind="ExternalInput").ap()
    W_out = DT("W_out", [c.E * c.F, c.D], F32, kind="ExternalInput").ap()
    norm_w = DT("norm_w", [c.D], F32, kind="ExternalInput").ap()
    gate_s = DT("gate_s", [1, 1], F32, kind="ExternalInput").ap()
    ltemp = DT("ltemp", [1, 1], F32, kind="ExternalInput").ap()
    below = DT("below", [c.NCORE, 2], F32, kind="ExternalInput").ap()
    ident = DT("ident", [128, 128], F32, kind="ExternalInput").ap()
    identb = DT("identb", [128, 128], BF16, kind="ExternalInput").ap()
    triu = DT("triu", [128, 128], F32, kind="ExternalInput").ap()
    trimaskT = DT("trimaskT", [128, 128], F32, kind="ExternalInput").ap()
    colpack = DT("colpack", [128, 4], F32, kind="ExternalInput").ap()
    erow = DT("erow", [128, 8], F32, kind="ExternalInput").ap()
    onesrow = DT("onesrow", [1, 128], F32, kind="ExternalInput").ap()

    WqR = Wq.rearrange("(r p) f -> p r f", p=128)
    WkR = Wk.rearrange("(r p) f -> p r f", p=128)
    WvR = Wv.rearrange("(r p) f -> p r f", p=128)
    WoR = Wo.rearrange("(r p) f -> p r f", p=128)
    WinR = W_in.rearrange("(r p) f -> p r f", p=128)
    WoutR = W_out.rearrange("(r p) f -> p r f", p=128)

    out_own = DT("out", [c.TOWN, c.D], F32, kind="ExternalOutput").ap()
    if dbg is True:
        dbg = {"xnbuf", "sbufd", "aobuf", "yslots", "slotimg"}
    elif not dbg:
        dbg = set()
    def kd(n):
        return "ExternalOutput" if n in dbg else "Internal"
    xnbuf = DT("xnbuf", [c.TOWN + 1, 2 * c.D], BF16, kind=kd("xnbuf")).ap()
    sbufd = DT("sbufd", [c.TOWN, c.D], F32, kind=kd("sbufd")).ap()
    aobuf = DT("aobuf", [c.NSEQ * 2 * c.DC * 128, c.L], BF16, kind=kd("aobuf")).ap()
    yslots = DT("yslots", [c.NSLOT + 1, c.D], F32, kind=kd("yslots")).ap()
    slotimg = DT("slotimg", [128, c.NT_IMG], U32, kind=kd("slotimg")).ap()

    sb_ident = b.sb("c_id", [128, 128], F32)
    sb_identb = b.sb("c_idb", [128, 128], BF16)
    sb_triu = b.sb("c_triu", [128, 128], F32)
    sb_trimT = b.sb("c_trimT", [128, 128], F32)
    sb_col = b.sb("c_col", [128, 4], F32)
    ones_col, p_col, p2_col = sb_col[:, 0:1], sb_col[:, 1:2], sb_col[:, 2:3]
    sb_erow = b.sb("c_erow", [128, 8], F32)
    sb_onesrow = b.sb("c_onesrow", [1, 128], F32)
    sb_below = b.sb("c_below", [c.NCORE, 2], F32)
    sb_nw = b.sb("c_nw", [128, c.DC], F32)
    sb_scal = b.sb("c_scal", [1, 8], F32)
    sb_gcol = b.sb("c_gcol", [128, 2], F32)
    sb_wg = b.sb("c_wg", [128, c.DC * 9], F32)
    sb_csum = b.sb("c_csum", [1, 16], F32)

    NT_ = c.NTILE_OWN
    st_oh = b.sb("st_oh", [128, NT_ * 16], F32)
    st_c = b.sb("st_c", [128, NT_ * 16], F32)
    st_yg = b.sb("st_yg", [128, NT_ * 2], U32)
    sb_cnt8 = b.sb("st_cnt8", [c.NCORE, 16], F32)
    sb_offs = b.sb("st_offs", [1, 16], F32)
    sb_rows = b.sb("st_rows", [1, 64], F32)
    sb_prefb = b.sb("st_prefb", [128, 24], F32)
    sb_slotcols = b.sb("st_slotc", [128, c.NT_IMG], U32)
    sb_u32 = b.sb("st_u32", [128, 4], U32)
    sb_i32 = b.sb("st_i32", [128, 4], I32)
    sb_cc = b.sb("st_cc", [128, 16], F32)
    sb_v8 = b.sb("st_v8", [128, 64], F32)


    sb_fx = b.sb("sc_fx", [128, c.SLAB], F32)
    sb_fx2 = b.sb("sc_fx2", [128, c.SLAB], F32)
    sb_fx3 = b.sb("sc_fx3", [128, c.SLAB], F32)
    sb_xn2 = b.sb("sc_xn2", [128, 2 * c.SLAB], BF16)
    sb_wf = b.sb("sc_wf", [128, 2 * 1024], F32)
    sb_wb = b.sb("sc_wb", [128, 2 * 2048], BF16)
    sb_r1 = b.sb("sc_r1", [1, c.SLAB], F32)
    sb_r2 = b.sb("sc_r2", [1, c.SLAB], F32)
    sb_r3 = b.sb("sc_r3", [1, c.SLAB], F32)
    sb_rr = sb_r3
    sb_pt = [b.sb(f"sc_pt{i}", [128, c.SLAB], BF16) for i in range(2)]
    sb_qn = [b.sb(f"sc_qn{i}", [128, c.HDC * c.SLAB], BF16) for i in range(2)]
    sb_onesb = b.sb("sc_onesb", [128, 1], BF16)
    sb_ab = b.sb("sc_ab", [128, 2 * c.SLAB], BF16)

    u0 = b._sb_off
    sb_sq = b.sb("u_sq", [128, c.DC * c.SLAB], F32, offset=u0)
    sb_tm = b.sb("u_tm", [128, c.TPS * 2 * c.D], BF16, offset=u0)
    sb_st = b.sb("u_st", [128, c.TPS * c.D], F32, offset=u0)
    ao_off = u0 + c.TPS * c.D * 4
    sb_aoin = [b.sb("u_ao0", [128, c.DC * c.SLAB], BF16, offset=ao_off),
               b.sb("u_ao1", [128, c.DC * c.SLAB], BF16,
                    offset=ao_off + c.DC * c.SLAB * 2)]
    usz = max(c.TPS * 2 * c.D * 2, c.TPS * c.D * 4 + 2 * c.DC * c.SLAB * 2,
              c.DC * c.SLAB * 4, 2 * c.KT * c.NKV * c.HD * 2,
              2 * 2 * c.D * 2 + 512 * 4)
    b._sb_off = (u0 + usz + 31) // 32 * 32

    base0 = b._sb_off
    LW = c.L
    sb_xts = [b.sb("sc_xts0", [128, c.DC * c.SLAB], F32, offset=base0),
              b.sb("sc_xts1", [128, c.DC * c.SLAB], F32,
                   offset=base0 + c.DC * c.SLAB * 4)]
    base = base0 + c.DC * c.SLAB * 4
    sb_h = [b.sb("rH_hi", [128, c.DC * LW], BF16, offset=base),
            b.sb("rH_lo", [128, c.DC * LW], BF16, offset=base + c.DC * LW * 2)]
    rK = base + 2 * (c.DC * LW * 2)
    sb_k = [b.sb("rK_hi", [128, c.NKV * c.HDC * LW], BF16, offset=rK),
            b.sb("rK_lo", [128, c.NKV * c.HDC * LW], BF16,
                 offset=rK + c.NKV * c.HDC * LW * 2)]
    sb_vv = [b.sb("rV_hi", [128, c.KT * c.NKV * c.HD], BF16, offset=u0),
             b.sb("rV_lo", [128, c.KT * c.NKV * c.HD], BF16,
                  offset=u0 + c.KT * c.NKV * c.HD * 2)]
    rEnd = rK + 2 * (c.NKV * c.HDC * LW * 2)

    mo = base0
    sb_xg = [b.sb("mo_xg_hi", [128, c.DC * c.HSLOT], BF16, offset=mo),
             b.sb("mo_xg_lo", [128, c.DC * c.HSLOT], BF16,
                  offset=mo + c.DC * c.HSLOT * 2)]
    mo2 = mo + 2 * c.DC * c.HSLOT * 2
    sb_act = [b.sb("mo_act_hi", [128, c.FC * c.HSLOT], BF16, offset=mo2),
              b.sb("mo_act_lo", [128, c.FC * c.HSLOT], BF16,
                   offset=mo2 + c.FC * c.HSLOT * 2)]
    mo3 = mo2 + 2 * c.FC * c.HSLOT * 2
    sb_gt = b.sb("mo_gt", [128, 2 * 2 * c.D], BF16, offset=u0)
    sb_y = b.sb("mo_y", [128, 512], F32, offset=u0 + 2 * 2 * c.D * 2)
    moEnd = mo3

    fo = base0
    sb_fa = b.sb("fi_a", [128, c.D], F32, offset=fo)
    sb_fb = b.sb("fi_b", [128, c.D], F32, offset=fo + c.D * 4)
    sb_fs = b.sb("fi_s", [128, c.D], F32, offset=fo + 2 * c.D * 4)
    sb_fo = b.sb("fi_o", [128, c.D], F32, offset=fo + 3 * c.D * 4)
    sb_fy = b.sb("fi_y", [128, c.D], F32, offset=fo + 4 * c.D * 4)
    sb_fy2 = b.sb("fi_y2", [128, c.D], F32, offset=fo + 5 * c.D * 4)

    peak = max(rEnd, moEnd, fo + 6 * c.D * 4)
    assert peak <= 229376, f"SBUF over budget: {peak}"

    pb = [nc.alloc_psum_tensor(f"pb{i}", [128, 512], F32).ap() for i in range(6)]
    pbbs = [nc.alloc_psum_tensor(f"pbb{i}", [128, 128], BF16).ap()
            for i in range(2)]
    pbb_r = [None, None]
    pbb_sel = [0]

    def bf16_tp(dst_fn, src, deps):
        bk = pbb_sel[0]
        pbb_sel[0] ^= 1
        w = list(deps)
        if pbb_r[bk] is not None:
            w.append(pbb_r[bk])
        th = P.add("tensor",
                   lambda e, ps=pbbs[bk], sr=src: e.transpose(ps, sr, sb_identb),
                   waits=w, inc=PE)
        te = P.add("vector", lambda e, ps=pbbs[bk]: dst_fn(e, ps), waits=[th],
                   inc=DV)
        pbb_r[bk] = te
        return te

    wrot = {"i": 0, "busy": [None, None]}

    def w_unit(src_ap, wcols, waits, q="scalar"):
        i = wrot["i"]
        wrot["i"] = (i + 1) % 2
        fslot = sb_wf[:, i * 1024:i * 1024 + wcols]
        h = sb_wb[:, i * 2048:i * 2048 + wcols]
        l = sb_wb[:, i * 2048 + 1024:i * 2048 + 1024 + wcols]
        w = list(waits)
        if wrot["busy"][i] is not None:
            w.append(wrot["busy"][i])
        t0 = b.dma(q, fslot, src_ap, waits=w)
        t1 = b.vec(lambda e, h=h, f=fslot: e.tensor_copy(h, f), waits=[t0])
        t2 = b.vec(lambda e, h=h, l=l, f=fslot: e.tensor_sub(l, f, h), waits=[t1])
        return t2, h, l, i

    def w_used(i, tok):
        wrot["busy"][i] = tok

    cl = [b.dma("sync", sb_ident, ident), b.dma("sync", sb_identb, identb),
          b.dma("sync", sb_triu, triu), b.dma("sync", sb_trimT, trimaskT),
          b.dma("sync", sb_col, colpack), b.dma("sync", sb_erow, erow),
          b.dma("sync", sb_onesrow, onesrow), b.dma("sync", sb_below, below),
          b.dma("sync", sb_nw, norm_w.rearrange("(a p) -> p a", p=128), slow=True),
          b.dma("sync", sb_scal[0:1, 0:1], gate_s),
          b.dma("sync", sb_scal[0:1, 1:2], ltemp)]
    for dc in range(c.DC):
        cl.append(b.dma("sync", sb_wg[:, dc * 9:dc * 9 + 8],
                        Wg[dc * 128:(dc + 1) * 128, :]))
    t_ts = b.sca(lambda e: e.activation(sb_scal[0:1, 2:3], sb_scal[0:1, 1:2],
                                        ACT.Exp), waits=cl)
    b.mm(1, pb[1][:, 0:1], sb_onesrow, sb_scal[0:1, 0:1], True, True, waits=cl)
    b.mm(1, pb[1][:, 1:2], sb_onesrow, sb_scal[0:1, 2:3], True, True, waits=[t_ts])
    t_gce = b.rd(1, "vector", lambda e: e.tensor_copy(sb_gcol, pb[1][:, 0:2]))

    wgo = [b.vec(lambda e, dc=dc: e.memset(sb_wg[:, dc * 9 + 8:dc * 9 + 9], 1.0),
                 waits=cl) for dc in range(c.DC)]
    for dc in range(c.DC):
        b.mm(2, pb[2][0:1, 0:8], ones_col, sb_wg[:, dc * 9:dc * 9 + 8],
             dc == 0, dc == c.DC - 1, waits=cl, inc=(dc == c.DC - 1))
    t_cs = b.rd(2, "vector", lambda e: e.tensor_scalar_mul(
        sb_csum[0:1, 0:8], pb[2][0:1, 0:8], -1.0 / c.D))
    b.mm(3, pb[3][:, 0:8], sb_onesrow, sb_csum[0:1, 0:8], True, True, waits=[t_cs])
    wg_fix = []
    for dc in range(c.DC):
        wg_fix.append(b.rd(3, "vector", lambda e, dc=dc: e.tensor_add(
            sb_wg[:, dc * 9:dc * 9 + 8], sb_wg[:, dc * 9:dc * 9 + 8], pb[3][:, 0:8])))

    t_m1 = b.vec(lambda e: e.memset(sb_slotcols, c.TOWN))
    t_si = b.dma("sync", slotimg, sb_slotcols, waits=[t_m1])
    t_m2 = b.vec(lambda e: e.memset(sb_tm[:, 0:2 * c.D], 0.0))
    t_zx = b.dma("sync", xnbuf[c.TOWN:c.TOWN + 1, :], sb_tm[0:1, 0:2 * c.D],
                 waits=[t_m2])
    t_m3 = b.vec(lambda e: e.memset(sb_fx[:, 0:c.SLAB], 0.0))
    for j in range(0, c.D, c.SLAB):
        w = min(c.SLAB, c.D - j)
        b.dma("sync", yslots[c.NSLOT:c.NSLOT + 1, j:j + w], sb_fx[0:1, 0:w],
              waits=[t_m3])
    tok_offs = b.vec(lambda e: e.memset(sb_offs, 0.0))
    t_ob = b.vec(lambda e: e.memset(sb_onesb, 1.0))
    cl.append(t_ob)

    xts_free = [None, None]
    tile_done = {}
    rr_readers = []
    sqmm_prev = []
    cnt_prev = []
    v6_prev, v10_prev = [], []
    for s in range(c.NSLAB_ALL):
        own = s < c.NSLAB_OWN
        buf = s % 2
        w = ([xts_free[buf]] if xts_free[buf] else []) + (cl if s < 2 else [])
        t_x = b.dma("sync", sb_xts[buf],
                    xT.rearrange("(a p) t -> p a t", p=128)[:, :, s * c.SLAB:(s + 1) * c.SLAB],
                    waits=w)
        tok_sq = None
        if own:
            t_sqa = b.sca(lambda e, bu=buf: e.activation(sb_sq, sb_xts[bu],
                                                         ACT.Square),
                          waits=[t_x] + ([t_zx, t_m2] if s == 0 else [])
                          + sqmm_prev)
            for dc in range(c.DC):
                b.mm(3, pb[3][0:1, 0:c.SLAB], ones_col,
                     sb_sq[:, dc * c.SLAB:(dc + 1) * c.SLAB], dc == 0,
                     dc == c.DC - 1, waits=[t_sqa], inc=(dc == c.DC - 1))
            sqmm_prev = [(PE, PE.n)]
            tok_sq = b.rd(3, "vector", lambda e: e.tensor_copy(
                sb_rr, pb[3][0:1, 0:c.SLAB]), waits=rr_readers)
            rr_readers = []
        rng = s // c.NSLAB_OWN
        last_tok = t_x
        for tt in range(c.TPS):
            tile = s * c.TPS + tt
            ncols = 9 if own else 8
            mm_w = [t_x] + (wg_fix + wgo if s == 0 else [])
            for dc in range(c.DC):
                b.mm(2, pb[2][:, 0:ncols],
                     sb_xts[buf][:, dc * c.SLAB + tt * 128:dc * c.SLAB + tt * 128 + 128],
                     sb_wg[:, dc * 9:dc * 9 + ncols], dc == 0, dc == c.DC - 1,
                     waits=mm_w, inc=(dc == c.DC - 1))
            u8 = pb[2][:, 0:8]
            ohsl = st_oh[:, tile * 16:tile * 16 + 16] if own else sb_v8[:, 32:48]
            t1 = b.rd(2, "vector", lambda e: e.reduce_max(sb_cc[:, 0:1], u8, AX.X,
                                                          negate=True))
            t2 = b.rd(2, "vector", lambda e, o=ohsl: e.tensor_scalar(
                o[:, 0:8], u8, sb_cc[:, 0:1], 0.0, ALU.add, ALU.is_equal),
                waits=[t1] + ([] if own else cnt_prev))
            t3 = b.rd(2, "vector", lambda e, o=ohsl: e.scalar_tensor_tensor(
                sb_v8[:, 0:8], o[:, 0:8], NEG, u8, ALU.mult, ALU.add), waits=[t2])
            t4 = b.vec(lambda e: e.reduce_max(sb_cc[:, 1:2], sb_v8[:, 0:8], AX.X,
                                              negate=True), waits=[t3])
            t5 = b.vec(lambda e, o=ohsl: e.tensor_scalar(
                o[:, 8:16], sb_v8[:, 0:8], sb_cc[:, 1:2], 0.0, ALU.add,
                ALU.is_equal), waits=[t4])
            start_cnt = (s % c.NSLAB_OWN) == 0 and tt == 0
            stop_cnt = (s % c.NSLAB_OWN) == c.NSLAB_OWN - 1 and tt == c.TPS - 1
            b.mm(4, pb[4][0:1, 0:16], ones_col, ohsl, start_cnt, stop_cnt,
                 waits=[t5])
            last_tok = (PE, PE.n)
            cnt_prev = [last_tok]
            if stop_cnt:
                tcv = b.rd(4, "vector", lambda e: e.tensor_copy(
                    sb_rows[0:1, 0:16], pb[4][0:1, 0:16]))
                b.dma("sync", sb_cnt8[rng:rng + 1, :], sb_rows[0:1, 0:16],
                      waits=[tcv])
                last_tok = tcv
            if own:
                cc0 = st_c[:, tile * 16:tile * 16 + 16]
                ta = b.vec(lambda e, o=ohsl: e.tensor_mul(sb_v8[:, 8:16], o[:, 0:8],
                                                          sb_erow), waits=[t5])
                b.vec(lambda e, cc=cc0: e.reduce_sum(cc[:, 3:4], sb_v8[:, 8:16],
                                                     AX.X), waits=[ta])
                tb_ = b.vec(lambda e, o=ohsl: e.tensor_mul(sb_v8[:, 8:16], o[:, 8:16],
                                                           sb_erow), waits=[ta])
                b.vec(lambda e, cc=cc0: e.reduce_sum(cc[:, 4:5], sb_v8[:, 8:16],
                                                     AX.X), waits=[tb_])
                tmc = b.rd(2, "vector", lambda e, cc=cc0: e.tensor_scalar_mul(
                    cc[:, 7:8], pb[2][:, 8:9], 1.0 / c.D))
                ttp = b.tp(5, pb[5][:, 0:1], sb_rr[0:1, tt * 128:tt * 128 + 128],
                           sb_ident[0:1, 0:1], waits=[tok_sq])
                rr_readers.append(ttp)
                v1 = b.rd(5, "vector", lambda e: e.tensor_scalar_mul(
                    sb_cc[:, 2:3], pb[5][:, 0:1], 1.0 / c.D))
                v2 = b.vec(lambda e, cc=cc0: e.tensor_mul(sb_cc[:, 3:4], cc[:, 7:8],
                                                          cc[:, 7:8]), waits=[tmc])
                v3 = b.vec(lambda e: e.tensor_sub(sb_cc[:, 4:5], sb_cc[:, 2:3],
                                                  sb_cc[:, 3:4]), waits=[v1, v2])
                v4 = b.vec(lambda e: e.tensor_scalar_add(sb_cc[:, 5:6],
                                                         sb_cc[:, 4:5], 1e-5),
                           waits=[v3])
                v5 = b.vec(lambda e: e.reciprocal(sb_cc[:, 6:7], sb_cc[:, 5:6]),
                           waits=[v4] + v6_prev)
                v6 = b.sca(lambda e, cc=cc0: e.activation(cc[:, 8:9], sb_cc[:, 6:7],
                                                          ACT.Sqrt), waits=[v5])
                v6_prev = [v6]
                v7 = b.vec(lambda e: e.tensor_mul(sb_cc[:, 7:8], sb_cc[:, 4:5],
                                                  sb_cc[:, 6:7]), waits=[v5])
                v8_ = b.vec(lambda e: e.tensor_scalar_add(sb_cc[:, 8:9],
                                                          sb_cc[:, 7:8], 1e-6),
                            waits=[v7])
                v9 = b.vec(lambda e: e.reciprocal(sb_cc[:, 9:10], sb_cc[:, 8:9]),
                           waits=[v8_] + v10_prev)
                v10 = b.sca(lambda e, cc=cc0: e.activation(
                    cc[:, 9:10], sb_cc[:, 9:10], ACT.Sqrt), waits=[v9])
                v10_prev = [v10]
                tl = b.rd(2, "vector", lambda e, cc=cc0: e.tensor_scalar_mul(
                    sb_v8[:, 16:24], u8, cc[:, 8:9]), waits=[v6, t5])
                tl2 = b.vec(lambda e: e.reduce_max(sb_cc[:, 10:11], sb_v8[:, 16:24],
                                                   AX.X, negate=True), waits=[tl])
                tl3 = b.sca(lambda e, cc=cc0: e.activation(
                    sb_v8[:, 24:32], sb_v8[:, 16:24], ACT.Exp, bias=sb_cc[:, 10:11],
                    accum_out=cc[:, 2:3]), waits=[tl2])
                tl4 = b.vec(lambda e, o=ohsl: e.tensor_mul(
                    sb_v8[:, 16:24], o[:, 0:8], sb_v8[:, 24:32]), waits=[tl3])
                b.vec(lambda e, cc=cc0: e.reduce_sum(cc[:, 0:1], sb_v8[:, 16:24],
                                                     AX.X), waits=[tl4])
                tl5 = b.vec(lambda e, o=ohsl: e.tensor_mul(
                    sb_v8[:, 16:24], o[:, 8:16], sb_v8[:, 24:32]), waits=[tl4])
                b.vec(lambda e, cc=cc0: e.reduce_sum(cc[:, 1:2], sb_v8[:, 16:24],
                                                     AX.X), waits=[tl5])
                b.mm(0, pb[0][:, 0:16], sb_triu, ohsl, True, False,
                     waits=[t5, tok_offs], inc=False)
                b.mm(0, pb[0][:, 0:16], sb_onesrow, sb_offs, False, True,
                     waits=[tok_offs])
                b.mm(1, pb[1][0:1, 0:16], ones_col, ohsl, True, True, waits=[t5])
                tok_offs = b.rd(1, "vector", lambda e: e.tensor_add(
                    sb_offs, sb_offs, pb[1][0:1, 0:16]))
                w1 = b.rd(0, "vector", lambda e, o=ohsl: e.tensor_mul(
                    sb_v8[:, 16:24], o[:, 0:8], pb[0][:, 0:8]))
                b.vec(lambda e, cc=cc0: e.reduce_sum(cc[:, 5:6], sb_v8[:, 16:24],
                                                     AX.X), waits=[w1])
                w2 = b.rd(0, "vector", lambda e, o=ohsl: e.tensor_mul(
                    sb_v8[:, 16:24], o[:, 8:16], pb[0][:, 8:16]), waits=[w1])
                tloc = b.vec(lambda e, cc=cc0: e.reduce_sum(
                    cc[:, 6:7], sb_v8[:, 16:24], AX.X), waits=[w2])
                tile_done[tile] = tloc
                last_tok = tloc
        xts_free[buf] = last_tok

    bar_r = P.barrier()
    b.mm(0, pb[0][0:1, 0:16], sb_below[:, 0:1], sb_cnt8, True, True, waits=bar_r)
    b.mm(1, pb[1][0:1, 0:16], sb_below[:, 1:2], sb_cnt8, True, True, waits=bar_r)
    tpv = b.rd(0, "vector", lambda e: e.tensor_copy(sb_rows[0:1, 0:16],
                                                    pb[0][0:1, 0:16]))
    ttv = b.rd(1, "vector", lambda e: e.tensor_copy(sb_rows[0:1, 16:32],
                                                    pb[1][0:1, 0:16]))
    tpg = b.vec(lambda e: e.tensor_add(sb_rows[0:1, 8:16], sb_rows[0:1, 8:16],
                                       sb_rows[0:1, 16:24]), waits=[tpv, ttv])
    b.mm(2, pb[2][:, 0:16], sb_onesrow, sb_rows[0:1, 0:16], True, True,
         waits=[tpg])
    tpbe = b.rd(2, "vector", lambda e: e.tensor_copy(sb_prefb[:, 0:16],
                                                     pb[2][:, 0:16]))
    b.mm(3, pb[3][:, 0:8], sb_onesrow, sb_cnt8[0:1, 0:8], True, True,
         waits=bar_r)
    tpc0 = b.rd(3, "vector", lambda e: e.tensor_copy(sb_prefb[:, 16:24],
                                                     pb[3][:, 0:8]))

    scat = []
    for tile in range(c.NTILE_OWN):
        cc0 = st_c[:, tile * 16:tile * 16 + 16]
        ohsl = st_oh[:, tile * 16:tile * 16 + 16]
        dep = [tpbe, tile_done[tile]]
        for kk in range(2):
            oh = ohsl[:, 8 * kk:8 * kk + 8]
            locw = cc0[:, 5 + kk:6 + kk]
            ek = cc0[:, 3 + kk:4 + kk]
            a1 = b.vec(lambda e, oh=oh, kk=kk: e.tensor_mul(
                sb_v8[:, 0:8], oh, sb_prefb[:, 8 * kk:8 * kk + 8]), waits=dep)
            a2 = b.vec(lambda e: e.reduce_sum(sb_cc[:, 0:1], sb_v8[:, 0:8], AX.X),
                       waits=[a1])
            a3 = b.vec(lambda e, locw=locw: e.tensor_add(sb_cc[:, 1:2], locw,
                                                         sb_cc[:, 0:1]), waits=[a2])
            a4 = b.vec(lambda e: e.tensor_scalar(sb_cc[:, 2:3], sb_cc[:, 1:2],
                                                 float(c.CGLOB), None, ALU.is_lt),
                       waits=[a3])
            if kk == 0:
                aL = b.vec(lambda e, locw=locw: e.tensor_copy(sb_cc[:, 15:16],
                                                              locw), waits=[a3])
            else:
                aL1 = b.vec(lambda e, oh=oh: e.tensor_mul(
                    sb_v8[:, 8:16], oh, sb_prefb[:, 16:24]), waits=[a3, tpc0])
                aL2 = b.vec(lambda e: e.reduce_sum(sb_cc[:, 14:15],
                                                   sb_v8[:, 8:16], AX.X),
                            waits=[aL1])
                aL = b.vec(lambda e, locw=locw: e.tensor_add(
                    sb_cc[:, 15:16], locw, sb_cc[:, 14:15]), waits=[aL2])
            a5 = b.vec(lambda e: e.tensor_scalar(
                sb_cc[:, 3:4], sb_cc[:, 15:16], float(c.CLOC), None, ALU.is_lt),
                waits=[aL])
            a6 = b.vec(lambda e: e.tensor_mul(sb_cc[:, 4:5], sb_cc[:, 2:3],
                                              sb_cc[:, 3:4]), waits=[a4, a5])
            b.vec(lambda e, cc=cc0, kk=kk: e.tensor_mul(
                cc[:, 12 + kk:13 + kk], cc[:, kk:kk + 1], sb_cc[:, 2:3]), waits=[a4])
            a8 = b.vec(lambda e, ek=ek: e.scalar_tensor_tensor(
                sb_cc[:, 5:6], ek, float(c.CLOC), sb_cc[:, 15:16], ALU.mult,
                ALU.add), waits=[a6])
            a9 = b.vec(lambda e, tile=tile, kk=kk: e.tensor_scalar_add(
                sb_cc[:, 6:7], p2_col, float(c.NSLOT + 256 * tile + kk)), waits=[a6])
            a6i = b.gpc(lambda e: e.tensor_copy(sb_i32[:, 3:4], sb_cc[:, 4:5]),
                        waits=[a6])
            a10 = b.vec(lambda e: e.select(sb_cc[:, 7:8], sb_i32[:, 3:4],
                                           sb_cc[:, 5:6], sb_cc[:, 6:7]),
                        waits=[a8, a9, a6i])
            a11 = b.vec(lambda e: e.memset(sb_cc[:, 8:9], float(c.NSLOT)),
                        waits=[a10])
            a12 = b.vec(lambda e: e.select(sb_cc[:, 9:10], sb_i32[:, 3:4],
                                           sb_cc[:, 5:6], sb_cc[:, 8:9]),
                        waits=[a11])
            b.gpc(lambda e, kk=kk, tile=tile: e.tensor_copy(
                st_yg[:, tile * 2 + kk:tile * 2 + kk + 1], sb_cc[:, 9:10]),
                waits=[a12])
            MAGIC = 12582912.0
            m1 = b.vec(lambda e: e.tensor_scalar_mul(sb_v8[:, 24:25],
                                                     sb_cc[:, 7:8], 1.0 / 128.0),
                       waits=[a10])
            m2 = b.vec(lambda e: e.tensor_scalar_add(sb_v8[:, 25:26],
                                                     sb_v8[:, 24:25], -0.49995),
                       waits=[m1])
            m3 = b.vec(lambda e: e.tensor_scalar_add(sb_v8[:, 26:27],
                                                     sb_v8[:, 25:26], MAGIC),
                       waits=[m2])
            m4 = b.vec(lambda e: e.tensor_scalar_add(sb_v8[:, 27:28],
                                                     sb_v8[:, 26:27], -MAGIC),
                       waits=[m3])
            m5 = b.vec(lambda e: e.tensor_scalar_mul(sb_v8[:, 28:29],
                                                     sb_cc[:, 7:8],
                                                     float(c.NT_IMG)),
                       waits=[a10])
            m6 = b.vec(lambda e: e.scalar_tensor_tensor(
                sb_v8[:, 29:30], sb_v8[:, 27:28], -(128.0 * c.NT_IMG - 1.0),
                sb_v8[:, 28:29], ALU.mult, ALU.add), waits=[m4, m5])
            m7 = b.gpc(lambda e: e.tensor_copy(sb_u32[:, 0:1], sb_v8[:, 29:30]),
                       waits=[m6] + scat[-1:])
            b5 = b.vec(lambda e, tile=tile: e.tensor_scalar_add(
                sb_cc[:, 10:11], p_col, float(128 * tile)), waits=[a10])
            b6 = b.gpc(lambda e: e.tensor_copy(sb_u32[:, 1:2], sb_cc[:, 10:11]),
                       waits=[b5] + scat[-1:])
            sc_t = b.gpd(lambda e: e.indirect_dma_start(
                out=slotimg.rearrange("p n -> (p n)").unsqueeze(1),
                out_offset=IndirectOffsetOnAxis(sb_u32[:, 0:1], 0),
                in_=sb_u32[:, 1:2], in_offset=None), waits=[m7, b6, t_si])
            scat.append(sc_t)
        g1 = b.vec(lambda e, cc=cc0: e.tensor_scalar_mul(sb_cc[:, 11:12],
                                                         cc[:, 2:3], 1e-9),
                   waits=[scat[-1]])
        g2 = b.vec(lambda e, cc=cc0: e.tensor_add(sb_cc[:, 12:13], cc[:, 12:13],
                                                  cc[:, 13:14]), waits=[scat[-1]])
        g3 = b.vec(lambda e: e.tensor_max(sb_cc[:, 13:14], sb_cc[:, 12:13],
                                          sb_cc[:, 11:12]), waits=[g1, g2])
        g4 = b.vec(lambda e: e.reciprocal(sb_cc[:, 14:15], sb_cc[:, 13:14]),
                   waits=[g3])
        b.vec(lambda e, cc=cc0: e.tensor_mul(cc[:, 10:11], cc[:, 12:13],
                                             sb_cc[:, 14:15]), waits=[g4])
        b.vec(lambda e, cc=cc0: e.tensor_mul(cc[:, 11:12], cc[:, 13:14],
                                             sb_cc[:, 14:15]), waits=[g4])

    bar_scat = P.barrier()
    t_slot = b.dma("sync", sb_slotcols, slotimg, waits=bar_scat)

    if upto == "R":
        sdump = DT("sdump", [128, NT_ * 48 + c.NT_IMG + 64], F32,
                   kind="ExternalOutput").ap()
        bar_d = P.barrier()
        b.dma("sync", sdump[:, 0:NT_ * 16], st_c, waits=bar_d)
        b.dma("sync", sdump[:, NT_ * 16:NT_ * 32], st_oh, waits=bar_d)
        b.dma("sync", sdump[:, NT_ * 32:NT_ * 34], st_yg.bitcast(F32), waits=bar_d)
        b.dma("sync", sdump[:, NT_ * 34:NT_ * 34 + c.NT_IMG],
              sb_slotcols.bitcast(F32), waits=[t_slot] + bar_d)
        b.dma("sync", sdump[:, NT_ * 34 + c.NT_IMG:NT_ * 34 + c.NT_IMG + 16],
              sb_prefb[:, 0:16], waits=bar_d)
        fin = P.barrier()
        for eng in ("sync", "scalar", "vector", "tensor", "gpsimd"):
            P.add(eng, lambda e: e.nop(), waits=fin)
        P.emit(nc)
        return nc

    def build_rows(stile, want_r3, waits):
        toks = []
        prev = list(waits)
        for j in range(c.TPS):
            cc0 = st_c[:, (stile + j) * 16:(stile + j) * 16 + 16]
            tmu = b.vec(lambda e, cc=cc0: e.tensor_mul(sb_cc[:, 0:1], cc[:, 7:8],
                                                       cc[:, 8:9]), waits=prev)
            b.tp(5, pb[5][0:1, 0:128], cc0[:, 8:9], sb_ident, waits=prev)
            te1 = b.rd(5, "vector", lambda e, j=j: e.tensor_copy(
                sb_r1[0:1, j * 128:(j + 1) * 128], pb[5][0:1, 0:128]))
            b.tp(5, pb[5][0:1, 0:128], sb_cc[:, 0:1], sb_ident, waits=[tmu])
            te2 = b.rd(5, "vector", lambda e, j=j: e.tensor_copy(
                sb_r2[0:1, j * 128:(j + 1) * 128], pb[5][0:1, 0:128]))
            toks += [te1, te2]
            prev = [te2]
            if want_r3:
                b.tp(5, pb[5][0:1, 0:128], cc0[:, 9:10], sb_ident, waits=[te2])
                te3 = b.rd(5, "vector", lambda e, j=j: e.tensor_copy(
                    sb_r3[0:1, j * 128:(j + 1) * 128], pb[5][0:1, 0:128]))
                toks.append(te3)
                prev = [te3]
        return toks

    xn_last = []
    s_stores = []
    a4_c1p, a4_mmp = [], []
    fxprev = []
    prev_ab = []
    for sq in range(c.NSEQ):
        t00 = sq * c.L
        h_tok = []
        xslab_free = None
        for sl in range(c.NQS):
            tok0 = t00 + sl * c.SLAB
            stile = tok0 // 128
            row_toks = build_rows(stile, True,
                                  [tile_done[stile + j] for j in range(c.TPS)])
            b.mm(0, pb[0][:, 0:c.SLAB], sb_onesrow, sb_r1[0:1, 0:c.SLAB], True,
                 True, waits=row_toks)
            b.mm(1, pb[1][:, 0:c.SLAB], sb_onesrow, sb_r2[0:1, 0:c.SLAB], True,
                 True, waits=row_toks)
            b.mm(2, pb[2][:, 0:c.SLAB], sb_onesrow, sb_r3[0:1, 0:c.SLAB], True,
                 True, waits=row_toks)
            t_x = b.dma("sync", sb_xts[0],
                        xT.rearrange("(a p) t -> p a t", p=128)[:, :, tok0:tok0 + c.SLAB],
                        waits=[xslab_free] if xslab_free else bar_scat)
            tm_toks = []
            c3p, c7p, tmp_ = [], [], []
            for dc in range(c.DC):
                xsl = sb_xts[0][:, dc * c.SLAB:(dc + 1) * c.SLAB]
                c1 = b.rd(0, "vector", lambda e, xsl=xsl: e.tensor_mul(
                    sb_fx, xsl, pb[0][:, 0:c.SLAB]), waits=[t_x] + c3p)
                c2 = b.rd(1, "vector", lambda e: e.tensor_sub(
                    sb_fx, sb_fx, pb[1][:, 0:c.SLAB]), waits=[c1])
                c3 = b.sca(lambda e: e.activation(sb_xn2[:, 0:c.SLAB], sb_fx, ACT.Copy),
                           waits=[c2] + tmp_)
                c4 = b.vec(lambda e: e.tensor_sub(sb_xn2[:, c.SLAB:], sb_fx,
                                                  sb_xn2[:, 0:c.SLAB]), waits=[c3])
                c5 = b.rd(2, "vector", lambda e: e.tensor_mul(
                    sb_fx2, sb_fx, pb[2][:, 0:c.SLAB]), waits=[c2] + c7p)
                c6 = b.vec(lambda e, dc=dc: e.tensor_scalar_mul(
                    sb_fx2, sb_fx2, sb_nw[:, dc:dc + 1]), waits=[c5])
                hh = sb_h[0][:, dc * c.L + sl * c.SLAB:dc * c.L + sl * c.SLAB + c.SLAB]
                hl = sb_h[1][:, dc * c.L + sl * c.SLAB:dc * c.L + sl * c.SLAB + c.SLAB]
                c7 = b.sca(lambda e, hh=hh: e.activation(hh, sb_fx2, ACT.Copy), waits=[c6])
                c8 = b.vec(lambda e, hh=hh, hl=hl: e.tensor_sub(hl, sb_fx2, hh),
                           waits=[c7])
                h_tok.append(c8)
                c3p, c7p = [c3], [c7]
                tmp_ = []
                for j in range(c.TPS):
                    for hilo in range(2):
                        dst = sb_tm[:, j * 2 * c.D + hilo * c.D + dc * 128:
                                    j * 2 * c.D + hilo * c.D + dc * 128 + 128]
                        te = bf16_tp(
                            (lambda e, ps, d=dst: e.tensor_copy(d, ps)),
                            sb_xn2[:, hilo * c.SLAB + j * 128:
                                   hilo * c.SLAB + j * 128 + 128],
                            [c3 if hilo == 0 else c4]
                            + ((xn_last[-c.TPS:] + s_stores[-c.TPS:])
                               if dc == 0 and j == 0 and hilo == 0 else []))
                        tmp_.append(te)
                        tm_toks.append(te)
            xslab_free = tm_toks[-1]
            for j in range(c.TPS):
                xn_last.append(b.dma("sync",
                                     xnbuf[tok0 + j * 128:tok0 + j * 128 + 128, :],
                                     sb_tm[:, j * 2 * c.D:(j + 1) * 2 * c.D],
                                     waits=tm_toks))
        bar_h = h_tok + xn_last[-c.TPS * c.NQS:]

        k_done = []
        for kvh in range(c.NKV):
            for slq in range(c.NQS):
                lastmm = {}
                for fc2 in range(c.HDC):
                    bank = fc2
                    col0 = (kvh * c.HDC + fc2) * 128
                    tw, wuh, wul, wsl = w_unit(
                        WkR[:, :, col0:col0 + 128], c.DC * 128,
                        bar_h if (slq == 0 and fc2 == 0 and kvh == 0) else [])
                    for dc in range(c.DC):
                        wh_ = wuh[:, dc * 128:dc * 128 + 128]
                        wl_ = wul[:, dc * 128:dc * 128 + 128]
                        rh = sb_h[0][:, dc * c.L + slq * c.SLAB:dc * c.L + slq * c.SLAB + c.SLAB]
                        rl = sb_h[1][:, dc * c.L + slq * c.SLAB:dc * c.L + slq * c.SLAB + c.SLAB]
                        b.mm(bank, pb[bank][:, 0:c.SLAB], wh_, rh, dc == 0, False,
                             waits=([tw] + bar_h) if dc == 0 else [], inc=False)
                        b.mm(bank, pb[bank][:, 0:c.SLAB], wh_, rl, False, False,
                             inc=False)
                        t_l = b.mm(bank, pb[bank][:, 0:c.SLAB], wl_, rh, False,
                                   dc == c.DC - 1)
                    w_used(wsl, t_l)
                    lastmm[fc2] = t_l
                sq1 = b.rd(0, "scalar", lambda e: e.activation(
                    sb_fx, pb[0][:, 0:c.SLAB], ACT.Square), waits=fxprev)
                b.mm(4, pb[4][0:1, 0:c.SLAB], ones_col, sb_fx, True, c.HDC == 1,
                     waits=[sq1], inc=(c.HDC == 1))
                if c.HDC > 1:
                    sq2 = b.rd(1, "scalar", lambda e: e.activation(
                        sb_fx2, pb[1][:, 0:c.SLAB], ACT.Square))
                    b.mm(4, pb[4][0:1, 0:c.SLAB], ones_col, sb_fx2, False, True,
                         waits=[sq2])
                n1 = b.rd(4, "scalar", lambda e: e.activation(
                    sb_rr, pb[4][0:1, 0:c.SLAB], ACT.Sqrt))
                n2 = b.vec(lambda e: e.tensor_scalar_max(sb_rr, sb_rr, 1e-12),
                           waits=[n1])
                n3 = b.vec(lambda e: e.reciprocal(sb_rr, sb_rr), waits=[n2])
                b.mm(5, pb[5][:, 0:c.SLAB], sb_onesrow, sb_rr, True, True,
                     waits=[n3])
                be = b.rd(5, "vector", lambda e: e.tensor_copy(
                    sb_fx3, pb[5][:, 0:c.SLAB]))
                hhp = []
                for fc2 in range(c.HDC):
                    base_k = kvh * c.HDC * c.L + fc2 * c.L + slq * c.SLAB
                    pr = b.rd(fc2, "vector", lambda e, fc2=fc2: e.tensor_mul(
                        sb_fx, pb[fc2][:, 0:c.SLAB], sb_fx3), waits=[be] + hhp)
                    hh = b.sca(lambda e, base_k=base_k: e.activation(
                        sb_k[0][:, base_k:base_k + c.SLAB], sb_fx, ACT.Copy), waits=[pr])
                    hhp = [hh]
                    ll = b.vec(lambda e, base_k=base_k: e.tensor_sub(
                        sb_k[1][:, base_k:base_k + c.SLAB], sb_fx,
                        sb_k[0][:, base_k:base_k + c.SLAB]), waits=[hh])
                    k_done.append(ll)
                fxprev = [ll, be, hh]
        v_done = {}
        NVG = min(4, c.KT)
        for tg in range(0, c.KT, NVG):
            ng = min(NVG, c.KT - tg)
            lastv = {}
            VW = c.NKV * c.HD
            for dp in range(0, c.DC, 2):
                tw, wuh, wul, wsl = w_unit(WvR[:, dp:dp + 2, :], 2 * VW,
                                           bar_h if tg == 0 and dp == 0 else [])
                for d2 in range(2):
                    dc = dp + d2
                    wvh = wuh[:, d2 * VW:(d2 + 1) * VW]
                    wvl = wul[:, d2 * VW:(d2 + 1) * VW]
                    for ti in range(ng):
                        tt = tg + ti
                        bank = ti
                        l0 = sb_h[0][:, dc * c.L + tt * 128:dc * c.L + tt * 128 + 128]
                        l1 = sb_h[1][:, dc * c.L + tt * 128:dc * c.L + tt * 128 + 128]
                        b.mm(bank, pb[bank][:, 0:VW], l0, wvh, dc == 0,
                             False, waits=(([tw] + bar_h) if ti == 0 and d2 == 0
                                           else []), inc=False)
                        b.mm(bank, pb[bank][:, 0:VW], l0, wvl, False, False,
                             inc=False)
                        lastv[ti] = b.mm(bank, pb[bank][:, 0:VW], l1, wvh,
                                         False, dc == c.DC - 1)
                w_used(wsl, lastv[ng - 1])
            for ti in range(ng):
                tt = tg + ti
                vh = sb_vv[0][:, tt * c.NKV * c.HD:(tt + 1) * c.NKV * c.HD]
                vl = sb_vv[1][:, tt * c.NKV * c.HD:(tt + 1) * c.NKV * c.HD]
                e1 = b.rd(ti, "scalar", lambda e, vh=vh, ti=ti: e.activation(
                    vh, pb[ti][:, 0:c.NKV * c.HD], ACT.Copy))
                e2 = b.rd(ti, "vector", lambda e, vh=vh, vl=vl, ti=ti: e.tensor_sub(
                    vl, pb[ti][:, 0:c.NKV * c.HD], vh), waits=[e1])
                v_done[tt] = e2
        v_all = list(v_done.values())

        for kvh in range(c.NKV):
            for qh in range(kvh * c.NREP, (kvh + 1) * c.NREP):
                for qs in range(c.NQS):
                    for fc2 in range(c.HDC):
                        bank = fc2
                        col0 = (qh * c.HDC + fc2) * 128
                        tw, wuh, wul, wsl = w_unit(
                            WqR[:, :, col0:col0 + 128], c.DC * 128, [])
                        for dc in range(c.DC):
                            wh_ = wuh[:, dc * 128:dc * 128 + 128]
                            wl_ = wul[:, dc * 128:dc * 128 + 128]
                            rh = sb_h[0][:, dc * c.L + qs * c.SLAB:dc * c.L + qs * c.SLAB + c.SLAB]
                            rl = sb_h[1][:, dc * c.L + qs * c.SLAB:dc * c.L + qs * c.SLAB + c.SLAB]
                            b.mm(bank, pb[bank][:, 0:c.SLAB], wh_, rh, dc == 0,
                                 False, waits=([tw] + bar_h) if dc == 0 else [],
                                 inc=False)
                            b.mm(bank, pb[bank][:, 0:c.SLAB], wh_, rl, False,
                                 False, inc=False)
                            t_l = b.mm(bank, pb[bank][:, 0:c.SLAB], wl_, rh,
                                       False, dc == c.DC - 1)
                        w_used(wsl, t_l)
                    sq1 = b.rd(0, "scalar", lambda e: e.activation(
                        sb_fx, pb[0][:, 0:c.SLAB], ACT.Square), waits=fxprev)
                    b.mm(4, pb[4][0:1, 0:c.SLAB], ones_col, sb_fx, True,
                         c.HDC == 1, waits=[sq1], inc=(c.HDC == 1))
                    if c.HDC > 1:
                        sq2 = b.rd(1, "scalar", lambda e: e.activation(
                            sb_fx2, pb[1][:, 0:c.SLAB], ACT.Square))
                        b.mm(4, pb[4][0:1, 0:c.SLAB], ones_col, sb_fx2, False,
                             True, waits=[sq2])
                    n1 = b.rd(4, "scalar", lambda e: e.activation(
                        sb_rr, pb[4][0:1, 0:c.SLAB], ACT.Sqrt))
                    n2 = b.vec(lambda e: e.tensor_scalar_max(sb_rr, sb_rr, 1e-12),
                               waits=[n1])
                    n3 = b.vec(lambda e: e.reciprocal(sb_rr, sb_rr), waits=[n2])
                    n4 = b.vec(lambda e: e.tensor_scalar_mul(
                        sb_rr, sb_rr, sb_scal[0:1, 2:3]), waits=[n3, t_ts])
                    b.mm(5, pb[5][:, 0:c.SLAB], sb_onesrow, sb_rr, True, True,
                         waits=[n4])
                    be = b.rd(5, "vector", lambda e: e.tensor_copy(
                        sb_fx3, pb[5][:, 0:c.SLAB]))
                    qn_tok = []
                    hhp = []
                    for fc2 in range(c.HDC):
                        pr = b.rd(fc2, "vector", lambda e, fc2=fc2: e.tensor_mul(
                            sb_fx, pb[fc2][:, 0:c.SLAB], sb_fx3), waits=[be] + hhp)
                        hh = b.sca(lambda e, fc2=fc2: e.activation(
                            sb_qn[0][:, fc2 * c.SLAB:(fc2 + 1) * c.SLAB], sb_fx,
                            ACT.Copy), waits=[pr])
                        hhp = [hh]
                        ll = b.vec(lambda e, fc2=fc2: e.tensor_sub(
                            sb_qn[1][:, fc2 * c.SLAB:(fc2 + 1) * c.SLAB], sb_fx,
                            sb_qn[0][:, fc2 * c.SLAB:(fc2 + 1) * c.SLAB]),
                            waits=[hh])
                        qn_tok.append(ll)
                    fxprev = [ll, be, hh]
                    ktmax = (qs + 1) * c.TPS
                    pt_free = [None, None]
                    dn = None
                    lastpv = None
                    for kt in range(ktmax):
                        first = True
                        for fc2 in range(c.HDC):
                            kb = kvh * c.HDC * c.L + fc2 * c.L + kt * 128
                            kap_h = sb_k[0][:, kb:kb + 128]
                            kap_l = sb_k[1][:, kb:kb + 128]
                            qap_h = sb_qn[0][:, fc2 * c.SLAB:(fc2 + 1) * c.SLAB]
                            qap_l = sb_qn[1][:, fc2 * c.SLAB:(fc2 + 1) * c.SLAB]
                            b.mm(0, pb[0][:, 0:c.SLAB], kap_h, qap_h, first,
                                 False, waits=qn_tok + k_done[-1:], inc=False)
                            first = False
                            b.mm(0, pb[0][:, 0:c.SLAB], kap_h, qap_l, False,
                                 False, inc=False)
                            smm = b.mm(0, pb[0][:, 0:c.SLAB], kap_l, qap_h, False,
                                       fc2 == c.HDC - 1)
                        off = kt * 128 - qs * c.SLAB
                        if 0 <= off < c.SLAB:
                            smm = b.rd(0, "vector", lambda e, off=off: e.tensor_add(
                                pb[0][:, off:off + 128], pb[0][:, off:off + 128],
                                sb_trimT))
                            b.bk_w[0] = smm
                        ex = b.rd(0, "scalar", lambda e: e.activation(
                            sb_fx, pb[0][:, 0:c.SLAB], ACT.Exp),
                            waits=(pl_prev if kt > 0 else fxprev))
                        if 0 < off < c.SLAB:
                            ex = b.vec(lambda e, off=off: e.memset(
                                sb_fx[:, 0:off], 0.0), waits=[ex])
                        bs = 0
                        wpt = [x for x in (pt_free[bs],) if x]
                        ph = b.sca(lambda e, bs=bs: e.activation(
                            sb_pt[2 * bs], sb_fx, ACT.Copy), waits=[ex] + wpt)
                        pl = b.vec(lambda e, bs=bs: e.tensor_sub(
                            sb_pt[2 * bs + 1], sb_fx, sb_pt[2 * bs]), waits=[ph])
                        pl_prev = [pl]
                        b.mm(1, pb[1][0:1, 0:c.SLAB], sb_onesb, sb_pt[2 * bs],
                             kt == 0, False, waits=[pl, t_ob], inc=False)
                        dn = b.mm(1, pb[1][0:1, 0:c.SLAB], sb_onesb,
                                  sb_pt[2 * bs + 1], False, kt == ktmax - 1,
                                  waits=[pl])
                        for fc2 in range(c.HDC):
                            vb = kt * c.NKV * c.HD + kvh * c.HD + fc2 * 128
                            vap_h = sb_vv[0][:, vb:vb + 128]
                            vap_l = sb_vv[1][:, vb:vb + 128]
                            bank = 2 + fc2
                            b.mm(bank, pb[bank][:, 0:c.SLAB], vap_h,
                                 sb_pt[2 * bs], kt == 0, False,
                                 waits=[pl] + v_all[-1:], inc=False)
                            b.mm(bank, pb[bank][:, 0:c.SLAB], vap_h,
                                 sb_pt[2 * bs + 1], False, False, inc=False)
                            lastpv = b.mm(bank, pb[bank][:, 0:c.SLAB], vap_l,
                                          sb_pt[2 * bs], False, kt == ktmax - 1)
                        pt_free[bs] = lastpv
                    rd_ = b.rd(1, "vector", lambda e: e.reciprocal(
                        sb_rr, pb[1][0:1, 0:c.SLAB]))
                    b.mm(4, pb[4][:, 0:c.SLAB], sb_onesrow, sb_rr, True, True,
                         waits=[rd_])
                    be2 = b.rd(4, "vector", lambda e: e.tensor_copy(
                        sb_fx3, pb[4][:, 0:c.SLAB]))
                    hhp = []
                    for fc2 in range(c.HDC):
                        r_ch = qh * c.HDC + fc2
                        pr = b.rd(2 + fc2, "vector", lambda e, fc2=fc2:
                                  e.tensor_mul(sb_fx, pb[2 + fc2][:, 0:c.SLAB],
                                               sb_fx3), waits=[be2] + hhp)
                        hh = b.sca(lambda e: e.activation(
                            sb_ab[:, 0:c.SLAB], sb_fx, ACT.Copy),
                            waits=[pr] + prev_ab)
                        hhp = [hh]
                        ll = b.vec(lambda e: e.tensor_sub(
                            sb_ab[:, c.SLAB:], sb_fx, sb_ab[:, 0:c.SLAB]),
                            waits=[hh])
                        ab0 = ((sq * 2 + 0) * c.DC + r_ch) * 128
                        ab1 = ((sq * 2 + 1) * c.DC + r_ch) * 128
                        d1 = b.dma("sync",
                              aobuf[ab0:ab0 + 128, qs * c.SLAB:(qs + 1) * c.SLAB],
                              sb_ab[:, 0:c.SLAB], waits=[ll])
                        d2 = b.dma("sync",
                              aobuf[ab1:ab1 + 128, qs * c.SLAB:(qs + 1) * c.SLAB],
                              sb_ab[:, c.SLAB:], waits=[ll])
                        prev_ab = [d1, d2]
                    fxprev = [ll, be2, hh]

        bar_att = P.barrier()
        for sl in range(c.NQS):
            tok0 = t00 + sl * c.SLAB
            stile = tok0 // 128
            row_toks = build_rows(stile, False, bar_att)
            b.mm(2, pb[2][:, 0:c.SLAB], sb_onesrow, sb_r1[0:1, 0:c.SLAB], True,
                 True, waits=row_toks)
            b.mm(3, pb[3][:, 0:c.SLAB], sb_onesrow, sb_r2[0:1, 0:c.SLAB], True,
                 True, waits=row_toks)
            t_x = b.dma("sync", sb_xts[0],
                        xT.rearrange("(a p) t -> p a t", p=128)[:, :, tok0:tok0 + c.SLAB],
                        waits=bar_att + a4_c1p)
            ao_toks = []
            for r in range(c.DC):
                ab0 = ((sq * 2 + 0) * c.DC + r) * 128
                ab1 = ((sq * 2 + 1) * c.DC + r) * 128
                ao_toks.append(b.dma("sync",
                                     sb_aoin[0][:, r * c.SLAB:(r + 1) * c.SLAB],
                                     aobuf[ab0:ab0 + 128, sl * c.SLAB:(sl + 1) * c.SLAB],
                                     waits=bar_att + a4_mmp))
                ao_toks.append(b.dma("sync",
                                     sb_aoin[1][:, r * c.SLAB:(r + 1) * c.SLAB],
                                     aobuf[ab1:ab1 + 128, sl * c.SLAB:(sl + 1) * c.SLAB],
                                     waits=bar_att))
            st_toks = []
            a4_trp = []
            for dout in range(c.DC):
                bank = dout % 2
                lastmm = None
                tw, wuh, wul, wsl = w_unit(
                    WoR[:, :, dout * 128:dout * 128 + 128], c.DC * 128, [])
                for r in range(c.DC):
                    woh = wuh[:, r * 128:r * 128 + 128]
                    wol = wul[:, r * 128:r * 128 + 128]
                    ah = sb_aoin[0][:, r * c.SLAB:(r + 1) * c.SLAB]
                    al = sb_aoin[1][:, r * c.SLAB:(r + 1) * c.SLAB]
                    b.mm(bank, pb[bank][:, 0:c.SLAB], woh, ah, r == 0, False,
                         waits=([tw] + ao_toks) if r == 0 else [], inc=False)
                    b.mm(bank, pb[bank][:, 0:c.SLAB], woh, al, False, False,
                         inc=False)
                    lastmm = b.mm(bank, pb[bank][:, 0:c.SLAB], wol, ah, False,
                                  r == c.DC - 1)
                w_used(wsl, lastmm)
                xsl = sb_xts[0][:, dout * c.SLAB:(dout + 1) * c.SLAB]
                c1 = b.rd(2, "vector", lambda e, xsl=xsl: e.tensor_mul(
                    sb_fx, xsl, pb[2][:, 0:c.SLAB]), waits=[t_x])
                c2 = b.rd(3, "vector", lambda e: e.tensor_sub(
                    sb_fx, sb_fx, pb[3][:, 0:c.SLAB]), waits=[c1])
                c3 = b.rd(bank, "vector", lambda e, bank=bank:
                          e.scalar_tensor_tensor(sb_fx2, pb[bank][:, 0:c.SLAB],
                                                 sb_gcol[:, 0:1], sb_fx, ALU.mult,
                                                 ALU.add), waits=[c2, t_gce] + a4_trp)
                a4_trp = []
                for j in range(c.TPS):
                    ttr = b.tp(5, pb[5][:, 0:128],
                               sb_fx2[:, j * 128:j * 128 + 128],
                               sb_ident, waits=[c3]
                               + (s_stores[-c.TPS:] if dout == 0 and j == 0
                                  else []))
                    a4_trp.append(ttr)
                    te = b.rd(5, "vector", lambda e, j=j, dout=dout: e.tensor_copy(
                        sb_st[:, j * c.D + dout * 128:j * c.D + dout * 128 + 128],
                        pb[5][:, 0:128]))
                    st_toks.append(te)
                a4_c1p = [c1]
                a4_mmp = [lastmm]
            for j in range(c.TPS):
                s_stores.append(
                    b.dma("sync", sbufd[tok0 + j * 128:tok0 + j * 128 + 128, :],
                          sb_st[:, j * c.D:(j + 1) * c.D], waits=st_toks))

    bar_a = P.barrier()
    fx3_readers = []
    m_prp, m_h1p = [], []
    y_busy = [None] * 4
    gt_free = [None, None]
    act_free = None
    nhs = len(c.HSLABS)
    g1_guard = []
    for ee in range(c.E):
        for hf in range(c.NHALF):
            tile0 = ee * c.TPE + hf * c.TPH
            xg_toks = []
            for tt in range(c.TPH):
                gb = tt % 2
                gsl = sb_gt[:, gb * 2 * c.D:(gb + 1) * 2 * c.D]
                wts = [t_slot] + (bar_a if tt < 2 else []) + \
                      ([gt_free[gb]] if gt_free[gb] else []) + \
                      (g1_guard if tt == 0 else [])
                tg = b.gpd(lambda e, gsl=gsl, ti=tile0 + tt: e.indirect_dma_start(
                    out=gsl, out_offset=None, in_=xnbuf,
                    in_offset=IndirectOffsetOnAxis(sb_slotcols[:, ti:ti + 1], 0)),
                    waits=wts)
                lastte = None
                for dc in range(c.DC):
                    for hilo in range(2):
                        dst = sb_xg[hilo][:, dc * c.HSLOT + tt * 128:
                                          dc * c.HSLOT + tt * 128 + 128]
                        te = bf16_tp(
                            (lambda e, ps, d=dst: e.tensor_copy(d, ps)),
                            gsl[:, hilo * c.D + dc * 128:
                                hilo * c.D + dc * 128 + 128],
                            [tg])
                        lastte = te
                        xg_toks.append(te)
                gt_free[gb] = lastte
            xg_last = xg_toks[-1]
            act_toks = []
            for fp in range(c.FC):
                lastg1 = {}
                for gi, fch in enumerate((fp, fp + c.FC)):
                    tw, wuh, wul, wsl = w_unit(
                        WinR[:, ee * c.DC:(ee + 1) * c.DC,
                             fch * 128:fch * 128 + 128], c.DC * 128, [])
                    for dc in range(c.DC):
                        wih = wuh[:, dc * 128:dc * 128 + 128]
                        wil = wul[:, dc * 128:dc * 128 + 128]
                        s0 = 0
                        for si, wdt in enumerate(c.HSLABS):
                            bank = gi * nhs + si
                            rhs_h = sb_xg[0][:, dc * c.HSLOT + s0:dc * c.HSLOT + s0 + wdt]
                            rhs_l = sb_xg[1][:, dc * c.HSLOT + s0:dc * c.HSLOT + s0 + wdt]
                            b.mm(bank, pb[bank][:, 0:wdt], wih, rhs_h, dc == 0,
                                 False, waits=([tw, xg_last] if dc == 0 and
                                               si == 0 else []), inc=False)
                            b.mm(bank, pb[bank][:, 0:wdt], wih, rhs_l, False,
                                 False, inc=False)
                            lastg1[(gi, si)] = b.mm(bank, pb[bank][:, 0:wdt], wil,
                                                    rhs_h, False, dc == c.DC - 1)
                            s0 += wdt
                    w_used(wsl, lastg1[(gi, nhs - 1)])
                s0 = 0
                for si, wdt in enumerate(c.HSLABS):
                    ga = b.rd(si, "scalar", lambda e, si=si, wdt=wdt: e.activation(
                        sb_fx[:, 0:wdt], pb[si][:, 0:wdt], ACT.Gelu),
                        waits=m_prp)
                    pr = b.rd(nhs + si, "vector", lambda e, si=si, wdt=wdt:
                              e.tensor_mul(sb_fx2[:, 0:wdt], sb_fx[:, 0:wdt],
                                           pb[nhs + si][:, 0:wdt]),
                              waits=[ga] + m_h1p)
                    m_prp = [pr]
                    ah = sb_act[0][:, fp * c.HSLOT + s0:fp * c.HSLOT + s0 + wdt]
                    al = sb_act[1][:, fp * c.HSLOT + s0:fp * c.HSLOT + s0 + wdt]
                    h1 = b.sca(lambda e, ah=ah, wdt=wdt: e.activation(
                        ah, sb_fx2[:, 0:wdt], ACT.Copy), waits=[pr])
                    m_h1p = [h1]
                    l1 = b.vec(lambda e, ah=ah, al=al, wdt=wdt: e.tensor_sub(
                        al, sb_fx2[:, 0:wdt], ah), waits=[h1])
                    act_toks.append(l1)
                    s0 += wdt
            g1_guard = [lastg1[(1, nhs - 1)]]
            FH = max(1, c.FC // 2)
            for dout in range(c.DC):
                lastg2 = {}
                for fh in range(c.FC // FH):
                    tw, wuh, wul, wsl = w_unit(
                        WoutR[:, ee * c.FC + fh * FH:ee * c.FC + (fh + 1) * FH,
                              dout * 128:dout * 128 + 128], FH * 128,
                        act_toks[-1:] if fh == 0 and dout == 0 else [])
                    for f8 in range(FH):
                        fc = fh * FH + f8
                        wgh = wuh[:, f8 * 128:f8 * 128 + 128]
                        wgl = wul[:, f8 * 128:f8 * 128 + 128]
                        s0 = 0
                        for si, wdt in enumerate(c.HSLABS):
                            bank = (dout * nhs + si) % 4
                            ath = sb_act[0][:, fc * c.HSLOT + s0:fc * c.HSLOT + s0 + wdt]
                            atl = sb_act[1][:, fc * c.HSLOT + s0:fc * c.HSLOT + s0 + wdt]
                            b.mm(bank, pb[bank][:, 0:wdt], wgh, ath, fc == 0,
                                 False, waits=(([tw] + (act_toks[-1:] if fh == 0
                                                        and dout == 0 else []))
                                               if f8 == 0 and si == 0 else []),
                                 inc=False)
                            b.mm(bank, pb[bank][:, 0:wdt], wgh, atl, False, False,
                                 inc=False)
                            lastg2[si] = b.mm(bank, pb[bank][:, 0:wdt], wgl, ath,
                                              False, fc == c.FC - 1)
                            s0 += wdt
                    w_used(wsl, lastg2[nhs - 1])
                s0 = 0
                for si, wdt in enumerate(c.HSLABS):
                    bank = (dout * nhs + si) % 4
                    tev = b.rd(bank, "vector", lambda e, wdt=wdt, bank=bank: e.tensor_copy(
                        sb_fx3[:, 0:wdt], pb[bank][:, 0:wdt]),
                        waits=fx3_readers)
                    fx3_readers = []
                    for j in range(wdt // 128):
                        tb = 4 + (j % 2)
                        ttp = b.tp(tb, pb[tb][:, 0:128],
                                   sb_fx3[:, j * 128:j * 128 + 128], sb_ident,
                                   waits=[tev])
                        fx3_readers.append(ttp)
                        ysl = sb_y[:, (j % 4) * 128:(j % 4) * 128 + 128]
                        te = b.rd(tb, "vector", lambda e, ysl=ysl, tb=tb:
                                  e.tensor_copy(ysl, pb[tb][:, 0:128]),
                            waits=[y_busy[j % 4]] if y_busy[j % 4] else [])
                        slot0 = ee * c.CLOC + hf * c.HSLOT + s0 + j * 128
                        y_busy[j % 4] = b.dma(
                            "sync", yslots[slot0:slot0 + 128,
                                           dout * 128:(dout + 1) * 128],
                            ysl, waits=[te])
                    s0 += wdt

    bar_m = P.barrier()
    pv = {}
    for tile in range(c.NTILE_OWN):
        tok0 = tile * 128
        cc0 = st_c[:, tile * 16:tile * 16 + 16]
        tg0 = b.gpd(lambda e, tile=tile: e.indirect_dma_start(
            out=sb_fa, out_offset=None, in_=yslots,
            in_offset=IndirectOffsetOnAxis(st_yg[:, tile * 2:tile * 2 + 1], 0)),
            waits=bar_m + [pv["f1"]] if "f1" in pv else bar_m)
        tg1 = b.gpd(lambda e, tile=tile: e.indirect_dma_start(
            out=sb_fb, out_offset=None, in_=yslots,
            in_offset=IndirectOffsetOnAxis(st_yg[:, tile * 2 + 1:tile * 2 + 2], 0)),
            waits=[pv["f2"]] if "f2" in pv else bar_m)
        ts_ = b.dma("sync", sb_fs, sbufd[tok0:tok0 + 128, :],
                    waits=[pv["f4"]] if "f4" in pv else bar_m)
        tx_ = b.dma("scalar", sb_fo, own_x[tok0:tok0 + 128, :],
                    waits=[pv["f5"]] if "f5" in pv else bar_m)
        f1 = b.vec(lambda e, cc=cc0: e.tensor_scalar_mul(sb_fy, sb_fa,
                                                         cc[:, 10:11]),
                   waits=[tg0] + ([pv["f2"], pv["f3"]] if "f2" in pv else []))
        f2 = b.vec(lambda e, cc=cc0: e.scalar_tensor_tensor(
            sb_fy, sb_fb, cc[:, 11:12], sb_fy, ALU.mult, ALU.add),
            waits=[f1, tg1])
        f3 = b.sca(lambda e: e.activation(sb_fy2, sb_fy, ACT.Silu),
                   waits=[f2] + ([pv["dma"]] if "dma" in pv else []))
        pv["f3"] = f3
        f4 = b.vec(lambda e: e.tensor_mul(sb_fy2, sb_fy2, sb_fs), waits=[f3, ts_])
        f5 = b.vec(lambda e: e.tensor_add(sb_fy2, sb_fy2, sb_fo), waits=[f4, tx_])
        pv["dma"] = b.dma("sync", out_own[tok0:tok0 + 128, :], sb_fy2, waits=[f5])
        pv["f1"], pv["f2"], pv["f4"], pv["f5"] = f1, f2, f4, f5

    fin = P.barrier()
    for eng in ("sync", "scalar", "vector", "tensor", "gpsimd"):
        P.add(eng, lambda e: e.nop(), waits=fin)

    import os
    if os.environ.get("KBUILD_STATS"):
        for cc_ in P.chans:
            print("chan", cc_.name, "final", cc_.n)
    P.emit(nc)
    return nc


def make_consts(c):
    ident = np.eye(128, dtype=np.float32)
    identb = np.eye(128).astype(ml_dtypes.bfloat16)
    triu = np.triu(np.ones((128, 128), np.float32), 1)
    trimaskT = np.where(np.arange(128)[:, None] > np.arange(128)[None, :],
                        np.float32(NEG), np.float32(0.0))
    colpack = np.zeros((128, 4), np.float32)
    colpack[:, 0] = 1.0
    colpack[:, 1] = np.arange(128)
    colpack[:, 2] = 2 * np.arange(128)
    erow = np.tile(np.arange(8, dtype=np.float32)[None, :], (128, 1))
    onesrow = np.ones((1, 128), np.float32)
    return ident, identb, triu, trimaskT, colpack, erow, onesrow


def stage_inputs(c, inputs):
    x = np.asarray(inputs["L0_attn_x"], dtype=np.float32)
    xf = np.ascontiguousarray(x.reshape(c.T, c.D))
    xTs = np.ascontiguousarray(xf.T)
    ident, identb, triu, trimaskT, colpack, erow, onesrow = make_consts(c)
    wq = np.ascontiguousarray(inputs["L0_attn_Wq"], dtype=np.float32)
    wk = np.ascontiguousarray(inputs["L0_attn_Wk"], dtype=np.float32)
    wv = np.ascontiguousarray(inputs["L0_attn_Wv"], dtype=np.float32)
    wo = np.ascontiguousarray(inputs["L0_attn_Wo"], dtype=np.float32)
    wg = np.ascontiguousarray(inputs["L0_ffn_Wg"], dtype=np.float32)
    win = np.ascontiguousarray(
        np.asarray(inputs["L0_ffn_W_in"], dtype=np.float32).reshape(c.E * c.D, c.F2))
    wout = np.ascontiguousarray(
        np.asarray(inputs["L0_ffn_W_out"], dtype=np.float32).reshape(c.E * c.F, c.D))
    nw = np.ascontiguousarray(inputs["L0_attn_norm_w"], dtype=np.float32)
    gate = np.asarray(inputs["L0_attn_gate"], dtype=np.float32).reshape(1, 1)
    lt = np.asarray(inputs["L0_attn_log_temperature"],
                    dtype=np.float32).reshape(1, 1)
    in_maps = []
    for core in range(c.NCORE):
        xr = np.roll(xTs, -core * c.TOWN, axis=1) if core else xTs
        belowv = np.zeros((c.NCORE, 2), np.float32)
        for r in range(c.NCORE):
            tru = (core + r) % c.NCORE
            belowv[r, 0] = 1.0 if tru < core else 0.0
            belowv[r, 1] = 1.0
        in_maps.append({
            "xT": np.ascontiguousarray(xr),
            "own_x": np.ascontiguousarray(xf[core * c.TOWN:(core + 1) * c.TOWN]),
            "Wq": wq, "Wk": wk, "Wv": wv, "Wo": wo, "Wg": wg,
            "W_in": win, "W_out": wout, "norm_w": nw,
            "gate_s": gate, "ltemp": lt, "below": belowv,
            "ident": ident, "identb": identb, "triu": triu,
            "trimaskT": trimaskT, "colpack": colpack, "erow": erow,
            "onesrow": onesrow,
        })
    return in_maps


def kernel(**inputs):
    c = Cfg(tiny=False)
    in_maps = stage_inputs(c, inputs)
    nc = build(c, dbg=True)
    res = run_bass_kernel_spmd(nc, in_maps, core_ids=list(range(c.NCORE)))
    out = np.concatenate([r["out"] for r in res.results], axis=0)
    return out.reshape(32, 2048, 1024)


# version 10
# speedup vs baseline: 1.0096x; 1.0096x over previous
import math
import numpy as np
import ml_dtypes
import concourse.bass as bass
import concourse.mybir as mybir
from concourse.bass_utils import run_bass_kernel_spmd
from concourse.bass_core_rust import IndirectOffsetOnAxis

F32 = mybir.dt.float32
BF16 = mybir.dt.bfloat16
I32 = mybir.dt.int32
U32 = mybir.dt.uint32
ALU = mybir.AluOpType
ACT = mybir.ActivationFunctionType
AX = mybir.AxisListType

NEG = -1.0e30


class Cfg:
    def __init__(self, tiny=False, mid=False):
        if mid:
            self.NCORE, self.D, self.L = 2, 1024, 512
            self.NSEQ, self.NH, self.NKV = 1, 4, 2
            self.E, self.F, self.SLAB = 8, 256, 256
        elif tiny:
            self.NCORE, self.D, self.L = 2, 256, 256
            self.NSEQ, self.NH, self.NKV = 2, 2, 1
            self.E, self.F, self.SLAB = 8, 128, 128
        else:
            self.NCORE, self.D, self.L = 8, 1024, 2048
            self.NSEQ, self.NH, self.NKV = 4, 4, 2
            self.E, self.F, self.SLAB = 8, 2048, 512
        self.DC = self.D // 128
        self.HD = self.D // self.NH
        self.HDC = self.HD // 128
        self.NREP = self.NH // self.NKV
        self.TOWN = self.NSEQ * self.L
        self.T = self.NCORE * self.TOWN
        self.NSLAB_OWN = self.TOWN // self.SLAB
        self.NSLAB_ALL = self.T // self.SLAB
        self.NTILE_OWN = self.TOWN // 128
        self.TPS = self.SLAB // 128
        self.KT = self.L // 128
        self.NQS = self.L // self.SLAB
        self.F2 = 2 * self.F
        self.FC = self.F // 128
        self.CGLOB = int(math.ceil(2 * self.T / self.E * 1.25))
        cl = int(math.ceil(2 * self.TOWN / self.E * 1.25))
        self.CLOC = ((cl + 127) // 128) * 128
        self.TPE = self.CLOC // 128
        self.NSLOT = self.E * self.CLOC
        self.NT_IMG = (self.NSLOT + 2 * self.TOWN) // 128
        self.NHALF = 2 if (not tiny or mid) else 1
        assert self.TPE % self.NHALF == 0
        self.TPH = self.TPE // self.NHALF
        self.HSLOT = self.TPH * 128
        self.HSLABS = []
        r = self.HSLOT
        while r > 0:
            w = min(self.SLAB, r)
            self.HSLABS.append(w)
            r -= w


class Ch:
    def __init__(self, name, engine, step):
        self.name, self.engine, self.step = name, engine, step
        self.is_dma = step == 16
        self.n = 0
        self.sem = None


class Prog:
    def __init__(self):
        self.ops = []
        self.chans = []

    def chan(self, name, engine, step=1):
        ch = Ch(name, engine, step)
        self.chans.append(ch)
        return ch

    def add(self, engine, fn, waits=(), inc=None):
        w = [(cc, v) for (cc, v) in waits if cc is not None and v > 0]
        self.ops.append((engine, fn, w, inc))
        if inc is not None:
            inc.n += inc.step
            return (inc, inc.n)
        return None

    def barrier(self):
        return [(cc, cc.n) for cc in self.chans]

    def emit(self, nc):
        for cc in self.chans:
            cc.sem = nc.alloc_semaphore(name="ch_" + cc.name)
        streams = {}
        for op in self.ops:
            streams.setdefault(op[0], []).append(op)

        def make_replay(ename, myops):
            def replay(eng):
                waited = {}
                for (_, fn, waits, inc) in myops:
                    for (cc, v) in waits:
                        if waited.get(cc.name, 0) >= v:
                            continue
                        eng.wait_ge(cc.sem, v)
                        waited[cc.name] = v
                    inst = fn(eng)
                    if inc is not None:
                        inst.then_inc(inc.sem, inc.step)
            return replay

        with nc.Block() as block:
            decs = {"sync": block.sync, "scalar": block.scalar,
                    "vector": block.vector, "tensor": block.tensor,
                    "gpsimd": block.gpsimd}
            for ename, myops in streams.items():
                decs[ename](make_replay(ename, myops))


class B:
    def __init__(self, cfg):
        self.c = cfg
        self.nc = bass.Bass("TRN2", target_bir_lowering=False,
                            detect_race_conditions=False)
        self.P = Prog()
        P = self.P
        self.LD = P.chan("ld", "sync", 16)
        self.SD = P.chan("sd", "scalar", 16)
        self.GD = P.chan("gd", "gpsimd", 16)
        self.WSEM = [P.chan("w0", "scalar", 16), P.chan("w1", "scalar", 16)]
        self.XT = [P.chan("xt0", "sync", 16), P.chan("xt1", "sync", 16)]
        self.CST = P.chan("cst", "sync", 16)
        self.YB = [P.chan(f"y{i}", "sync", 16) for i in range(4)]
        self.GB = [P.chan("g0", "gpsimd", 16), P.chan("g1", "gpsimd", 16)]
        self.PE = P.chan("pe", "tensor", 1)
        self.DV = P.chan("dv", "vector", 1)
        self.SC = P.chan("sc", "scalar", 1)
        self.GP = P.chan("gp", "gpsimd", 1)
        used = self.nc.SBUF_PARTITION_SIZE_BYTES - self.nc.sbuf_bytes_remaining
        self._sb_off = (used + 31) // 32 * 32
        self.bk_w = [None] * 8
        self.bk_r = [None] * 8

    def dma(self, q, out, in_, waits=(), slow=False, ch=None):
        if ch is None:
            ch = {"sync": self.LD, "scalar": self.SD, "gpsimd": self.GD}[q]
        return self.P.add(
            q, lambda e, o=out, i=in_, s=slow: e.dma_start(
                out=o, in_=i, allow_slow_non_contiguous=s),
            waits=waits, inc=ch)

    def mm(self, bank_i, out, lhsT, rhs, start, stop, waits=(), inc=True):
        w = list(waits)
        if start and self.bk_r[bank_i] is not None:
            w.append(self.bk_r[bank_i])
        tok = self.P.add(
            "tensor",
            lambda e, o=out, l=lhsT, r=rhs, s1=start, s2=stop:
                e.matmul(o, l, r, start=s1, stop=s2),
            waits=w, inc=self.PE if inc else None)
        if inc:
            self.bk_w[bank_i] = tok
        return tok

    def tp(self, bank_i, out, in_, ident, waits=()):
        w = list(waits)
        if self.bk_r[bank_i] is not None:
            w.append(self.bk_r[bank_i])
        tok = self.P.add(
            "tensor", lambda e, o=out, i=in_, d=ident: e.transpose(o, i, d),
            waits=w, inc=self.PE)
        self.bk_w[bank_i] = tok
        return tok

    def rd(self, bank_i, engine, fn, waits=()):
        w = list(waits)
        if self.bk_w[bank_i] is not None:
            w.append(self.bk_w[bank_i])
        ch = self.DV if engine == "vector" else self.SC
        tok = self.P.add(engine, fn, waits=w, inc=ch)
        self.bk_r[bank_i] = tok
        return tok

    def rd2(self, bank_i, bank_j, engine, fn, waits=()):
        w = list(waits)
        for bi in (bank_i, bank_j):
            if self.bk_w[bi] is not None:
                w.append(self.bk_w[bi])
        ch = self.DV if engine == "vector" else self.SC
        tok = self.P.add(engine, fn, waits=w, inc=ch)
        self.bk_r[bank_i] = tok
        self.bk_r[bank_j] = tok
        return tok

    def vec(self, fn, waits=()):
        return self.P.add("vector", fn, waits=waits, inc=self.DV)

    def sca(self, fn, waits=()):
        return self.P.add("scalar", fn, waits=waits, inc=self.SC)

    def gpd(self, fn, waits=(), ch=None):
        return self.P.add("gpsimd", fn, waits=waits, inc=ch or self.GD)

    def gpc(self, fn, waits=()):
        return self.P.add("gpsimd", fn, waits=waits, inc=self.GP)

    def sb(self, name, shape, dtype, offset=None):
        nbytes = int(np.prod(shape[1:])) * mybir.dt.size(dtype)
        if offset is None:
            offset = self._sb_off
            self._sb_off = (offset + nbytes + 31) // 32 * 32
        return self.nc.alloc_sbuf_tensor_at(name, shape, dtype, offset=offset).ap()


def build(cfg: Cfg, dbg=False, upto='F'):
    b = B(cfg)
    c = cfg
    nc, P = b.nc, b.P
    PE, DV, SC = b.PE, b.DV, b.SC

    DT = nc.dram_tensor
    xT = DT("xT", [128, c.DC * c.T], F32, kind="ExternalInput").ap()
    own_x = DT("own_x", [c.TOWN, c.D], F32, kind="ExternalInput").ap()
    Wq = DT("Wq", [128, c.NH * c.HDC * c.DC * 128], F32, kind="ExternalInput").ap()
    Wk = DT("Wk", [128, c.NKV * c.HDC * c.DC * 128], F32, kind="ExternalInput").ap()
    Wv = DT("Wv", [128, c.DC * c.NKV * c.HD], F32, kind="ExternalInput").ap()
    Wo = DT("Wo", [128, c.DC * c.DC * 128], F32, kind="ExternalInput").ap()
    Wg = DT("Wg", [c.D, c.E], F32, kind="ExternalInput").ap()
    W_in = DT("W_in", [128, c.E * 2 * c.FC * c.DC * 128], F32,
              kind="ExternalInput").ap()
    W_out = DT("W_out", [128, c.E * c.FC * c.DC * 128], F32,
               kind="ExternalInput").ap()
    norm_w = DT("norm_w", [c.D], F32, kind="ExternalInput").ap()
    gate_s = DT("gate_s", [1, 1], F32, kind="ExternalInput").ap()
    ltemp = DT("ltemp", [1, 1], F32, kind="ExternalInput").ap()
    below = DT("below", [c.NCORE, 2], F32, kind="ExternalInput").ap()
    ident = DT("ident", [128, 128], F32, kind="ExternalInput").ap()
    identb = DT("identb", [128, 128], BF16, kind="ExternalInput").ap()
    triu = DT("triu", [128, 128], F32, kind="ExternalInput").ap()
    trimaskT = DT("trimaskT", [128, 128], F32, kind="ExternalInput").ap()
    colpack = DT("colpack", [128, 4], F32, kind="ExternalInput").ap()
    erow = DT("erow", [128, 8], F32, kind="ExternalInput").ap()
    onesrow = DT("onesrow", [1, 128], F32, kind="ExternalInput").ap()

    out_own = DT("out", [c.TOWN, c.D], F32, kind="ExternalOutput").ap()
    if dbg is True:
        dbg = {"xnbuf", "sbufd", "aobuf", "yslots", "slotimg"}
    elif not dbg:
        dbg = set()
    def kd(n):
        return "ExternalOutput" if n in dbg else "Internal"
    xnbuf = DT("xnbuf", [c.TOWN + 1, 2 * c.D], BF16, kind=kd("xnbuf")).ap()
    sbufd = DT("sbufd", [c.TOWN, c.D], F32, kind=kd("sbufd")).ap()
    aobuf = DT("aobuf", [c.NSEQ * 2 * c.DC * 128, c.L], BF16, kind=kd("aobuf")).ap()
    yslots = DT("yslots", [c.NSLOT + 1, c.D], F32, kind=kd("yslots")).ap()
    slotimg = DT("slotimg", [128, c.NT_IMG], U32, kind=kd("slotimg")).ap()
    if "mdump" in dbg:
        NTIL = c.E * c.NHALF
        xgdump = DT("xgdump", [NTIL * 128, c.DC * c.HSLOT], BF16,
                    kind="ExternalOutput").ap()
        actdump = DT("actdump", [NTIL * 128, 2 * c.FC * c.HSLOT], BF16,
                     kind="ExternalOutput").ap()
        wbdump = DT("wbdump", [NTIL * 128, 2048], BF16,
                    kind="ExternalOutput").ap()

    sb_ident = b.sb("c_id", [128, 128], F32)
    sb_identb = b.sb("c_idb", [128, 128], BF16)
    sb_triu = b.sb("c_triu", [128, 128], F32)
    sb_trimT = b.sb("c_trimT", [128, 128], F32)
    sb_col = b.sb("c_col", [128, 4], F32)
    ones_col, p_col, p2_col = sb_col[:, 0:1], sb_col[:, 1:2], sb_col[:, 2:3]
    sb_erow = b.sb("c_erow", [128, 8], F32)
    sb_onesrow = b.sb("c_onesrow", [1, 128], F32)
    sb_below = b.sb("c_below", [c.NCORE, 2], F32)
    sb_nw = b.sb("c_nw", [128, c.DC], F32)
    sb_scal = b.sb("c_scal", [1, 8], F32)
    sb_gcol = b.sb("c_gcol", [128, 2], F32)
    sb_wg = b.sb("c_wg", [128, c.DC * 9], F32)
    sb_csum = b.sb("c_csum", [1, 16], F32)

    NT_ = c.NTILE_OWN
    st_oh = b.sb("st_oh", [128, NT_ * 16], F32)
    st_c = b.sb("st_c", [128, NT_ * 16], F32)
    st_yg = b.sb("st_yg", [128, NT_ * 2], U32)
    sb_cnt8 = b.sb("st_cnt8", [c.NCORE, 16], F32)
    sb_offs = b.sb("st_offs", [1, 16], F32)
    sb_rows = b.sb("st_rows", [1, 64], F32)
    sb_prefb = b.sb("st_prefb", [128, 24], F32)
    sb_slotcols = b.sb("st_slotc", [128, c.NT_IMG], U32)
    sb_u32 = b.sb("st_u32", [128, 4], U32)
    sb_i32 = b.sb("st_i32", [128, 4], I32)
    sb_cc = b.sb("st_cc", [128, 16], F32)
    sb_v8 = b.sb("st_v8", [128, 64], F32)


    sb_fx = b.sb("sc_fx", [128, c.SLAB], F32)
    sb_fx2 = b.sb("sc_fx2", [128, c.SLAB], F32)
    sb_fx3 = b.sb("sc_fx3", [128, c.SLAB], F32)
    sb_xn2 = b.sb("sc_xn2", [128, 2 * c.SLAB], BF16)
    sb_wf = b.sb("sc_wf", [128, 2 * 1024], F32)
    sb_wb = b.sb("sc_wb", [128, 2 * 2048], BF16)
    sb_r1 = b.sb("sc_r1", [1, c.SLAB], F32)
    sb_r2 = b.sb("sc_r2", [1, c.SLAB], F32)
    sb_r3 = b.sb("sc_r3", [1, c.SLAB], F32)
    sb_rr = sb_r3
    sb_pt = [b.sb(f"sc_pt{i}", [128, c.SLAB], BF16) for i in range(2)]
    sb_qn = [b.sb(f"sc_qn{i}", [128, c.HDC * c.SLAB], BF16) for i in range(2)]
    sb_onesb = b.sb("sc_onesb", [128, 1], BF16)
    sb_ab = b.sb("sc_ab", [128, 2 * c.SLAB], BF16)

    u0 = b._sb_off
    sb_sq = b.sb("u_sq", [128, c.DC * c.SLAB], F32, offset=u0)
    sb_tm = b.sb("u_tm", [128, c.TPS * 2 * c.D], BF16, offset=u0)
    sb_st = b.sb("u_st", [128, c.TPS * c.D], F32, offset=u0)
    ao_off = u0 + c.TPS * c.D * 4
    sb_aoin = [b.sb("u_ao0", [128, c.DC * c.SLAB], BF16, offset=ao_off),
               b.sb("u_ao1", [128, c.DC * c.SLAB], BF16,
                    offset=ao_off + c.DC * c.SLAB * 2)]
    usz = max(c.TPS * 2 * c.D * 2, c.TPS * c.D * 4 + 2 * c.DC * c.SLAB * 2,
              c.DC * c.SLAB * 4, 2 * c.KT * c.NKV * c.HD * 2,
              2 * 2 * c.D * 2 + 512 * 4)
    b._sb_off = (u0 + usz + 31) // 32 * 32

    base0 = b._sb_off
    LW = c.L
    sb_xts = [b.sb("sc_xts0", [128, c.DC * c.SLAB], F32, offset=base0),
              b.sb("sc_xts1", [128, c.DC * c.SLAB], F32,
                   offset=base0 + c.DC * c.SLAB * 4)]
    base = base0 + c.DC * c.SLAB * 4
    sb_h = [b.sb("rH_hi", [128, c.DC * LW], BF16, offset=base),
            b.sb("rH_lo", [128, c.DC * LW], BF16, offset=base + c.DC * LW * 2)]
    rK = base + 2 * (c.DC * LW * 2)
    sb_k = [b.sb("rK_hi", [128, c.NKV * c.HDC * LW], BF16, offset=rK),
            b.sb("rK_lo", [128, c.NKV * c.HDC * LW], BF16,
                 offset=rK + c.NKV * c.HDC * LW * 2)]
    sb_vv = [b.sb("rV_hi", [128, c.KT * c.NKV * c.HD], BF16, offset=u0),
             b.sb("rV_lo", [128, c.KT * c.NKV * c.HD], BF16,
                  offset=u0 + c.KT * c.NKV * c.HD * 2)]
    rEnd = rK + 2 * (c.NKV * c.HDC * LW * 2)

    mo = base0
    sb_xg = [b.sb("mo_xg_hi", [128, c.DC * c.HSLOT], BF16, offset=mo),
             b.sb("mo_xg_lo", [128, c.DC * c.HSLOT], BF16,
                  offset=mo + c.DC * c.HSLOT * 2)]
    mo2 = mo + 2 * c.DC * c.HSLOT * 2
    sb_act = [b.sb("mo_act_hi", [128, c.FC * c.HSLOT], BF16, offset=mo2),
              b.sb("mo_act_lo", [128, c.FC * c.HSLOT], BF16,
                   offset=mo2 + c.FC * c.HSLOT * 2)]
    mo3 = mo2 + 2 * c.FC * c.HSLOT * 2
    sb_gt = b.sb("mo_gt", [128, 2 * 2 * c.D], BF16, offset=u0)
    sb_y = b.sb("mo_y", [128, 512], F32, offset=u0 + 2 * 2 * c.D * 2)
    moEnd = mo3

    fo = base0
    sb_fa = b.sb("fi_a", [128, c.D], F32, offset=fo)
    sb_fb = b.sb("fi_b", [128, c.D], F32, offset=fo + c.D * 4)
    sb_fs = b.sb("fi_s", [128, c.D], F32, offset=fo + 2 * c.D * 4)
    sb_fo = b.sb("fi_o", [128, c.D], F32, offset=fo + 3 * c.D * 4)
    sb_fy = b.sb("fi_y", [128, c.D], F32, offset=fo + 4 * c.D * 4)
    sb_fy2 = b.sb("fi_y2", [128, c.D], F32, offset=fo + 5 * c.D * 4)

    peak = max(rEnd, moEnd, fo + 6 * c.D * 4)
    assert peak <= 229376, f"SBUF over budget: {peak}"

    pb = [nc.alloc_psum_tensor(f"pb{i}", [128, 512], F32).ap() for i in range(6)]
    pbbs = [nc.alloc_psum_tensor(f"pbb{i}", [128, 128], BF16).ap()
            for i in range(2)]
    pbb_r = [None, None]
    pbb_sel = [0]

    def bf16_tp(dst_fn, src, deps):
        bk = pbb_sel[0]
        pbb_sel[0] ^= 1
        w = list(deps)
        if pbb_r[bk] is not None:
            w.append(pbb_r[bk])
        th = P.add("tensor",
                   lambda e, ps=pbbs[bk], sr=src: e.transpose(ps, sr, sb_identb),
                   waits=w, inc=PE)
        te = P.add("vector", lambda e, ps=pbbs[bk]: dst_fn(e, ps), waits=[th],
                   inc=DV)
        pbb_r[bk] = te
        return te

    wrot = {"i": 0, "busy": [None, None]}

    def w_unit(src_ap, wcols, waits, q="scalar"):
        i = wrot["i"]
        wrot["i"] = (i + 1) % 2
        fslot = sb_wf[:, i * 1024:i * 1024 + wcols]
        h = sb_wb[:, i * 2048:i * 2048 + wcols]
        l = sb_wb[:, i * 2048 + 1024:i * 2048 + 1024 + wcols]
        w = list(waits)
        if wrot["busy"][i] is not None:
            w.append(wrot["busy"][i])
        t0 = b.dma(q, fslot, src_ap, waits=w, ch=b.WSEM[i])
        t1 = b.vec(lambda e, h=h, f=fslot: e.tensor_copy(h, f), waits=[t0])
        t2 = b.vec(lambda e, h=h, l=l, f=fslot: e.tensor_sub(l, f, h), waits=[t1])
        return t2, h, l, i

    def w_used(i, tok):
        wrot["busy"][i] = tok

    CS = b.CST
    cl = [b.dma("sync", sb_ident, ident, ch=CS),
          b.dma("sync", sb_identb, identb, ch=CS),
          b.dma("sync", sb_triu, triu, ch=CS),
          b.dma("sync", sb_trimT, trimaskT, ch=CS),
          b.dma("sync", sb_col, colpack, ch=CS),
          b.dma("sync", sb_erow, erow, ch=CS),
          b.dma("sync", sb_onesrow, onesrow, ch=CS),
          b.dma("sync", sb_below, below, ch=CS),
          b.dma("sync", sb_nw, norm_w.rearrange("(a p) -> p a", p=128),
                slow=True, ch=CS),
          b.dma("sync", sb_scal[0:1, 0:1], gate_s, ch=CS),
          b.dma("sync", sb_scal[0:1, 1:2], ltemp, ch=CS)]
    for dc in range(c.DC):
        cl.append(b.dma("sync", sb_wg[:, dc * 9:dc * 9 + 8],
                        Wg[dc * 128:(dc + 1) * 128, :], ch=CS))
    t_ts = b.sca(lambda e: e.activation(sb_scal[0:1, 2:3], sb_scal[0:1, 1:2],
                                        ACT.Exp), waits=cl)
    b.mm(1, pb[1][:, 0:1], sb_onesrow, sb_scal[0:1, 0:1], True, True, waits=cl)
    b.mm(1, pb[1][:, 1:2], sb_onesrow, sb_scal[0:1, 2:3], True, True, waits=[t_ts])
    t_gce = b.rd(1, "vector", lambda e: e.tensor_copy(sb_gcol, pb[1][:, 0:2]))

    wgo = [b.vec(lambda e, dc=dc: e.memset(sb_wg[:, dc * 9 + 8:dc * 9 + 9], 1.0),
                 waits=cl) for dc in range(c.DC)]
    for dc in range(c.DC):
        b.mm(2, pb[2][0:1, 0:8], ones_col, sb_wg[:, dc * 9:dc * 9 + 8],
             dc == 0, dc == c.DC - 1, waits=cl, inc=(dc == c.DC - 1))
    t_cs = b.rd(2, "vector", lambda e: e.tensor_scalar_mul(
        sb_csum[0:1, 0:8], pb[2][0:1, 0:8], -1.0 / c.D))
    b.mm(3, pb[3][:, 0:8], sb_onesrow, sb_csum[0:1, 0:8], True, True, waits=[t_cs])
    wg_fix = []
    for dc in range(c.DC):
        wg_fix.append(b.rd(3, "vector", lambda e, dc=dc: e.tensor_add(
            sb_wg[:, dc * 9:dc * 9 + 8], sb_wg[:, dc * 9:dc * 9 + 8], pb[3][:, 0:8])))

    t_m1 = b.vec(lambda e: e.memset(sb_slotcols, c.TOWN))
    t_si = b.dma("sync", slotimg, sb_slotcols, waits=[t_m1], ch=CS)
    t_m2 = b.vec(lambda e: e.memset(sb_tm[:, 0:2 * c.D], 0.0))
    t_zx = b.dma("sync", xnbuf[c.TOWN:c.TOWN + 1, :], sb_tm[0:1, 0:2 * c.D],
                 waits=[t_m2], ch=CS)
    t_m3 = b.vec(lambda e: e.memset(sb_fx[:, 0:c.SLAB], 0.0))
    for j in range(0, c.D, c.SLAB):
        w = min(c.SLAB, c.D - j)
        b.dma("sync", yslots[c.NSLOT:c.NSLOT + 1, j:j + w], sb_fx[0:1, 0:w],
              waits=[t_m3], ch=CS)
    tok_offs = b.vec(lambda e: e.memset(sb_offs, 0.0))
    t_ob = b.vec(lambda e: e.memset(sb_onesb, 1.0))
    cl.append(t_ob)

    xts_free = [None, None]
    tile_done = {}
    rr_readers = []
    sqmm_prev = []
    cnt_prev = []
    v6_prev, v10_prev = [], []
    for s in range(c.NSLAB_ALL):
        own = s < c.NSLAB_OWN
        buf = s % 2
        w = ([xts_free[buf]] if xts_free[buf] else []) + (cl if s < 2 else [])
        t_x = b.dma("sync", sb_xts[buf],
                    xT[:, s * c.DC * c.SLAB:(s + 1) * c.DC * c.SLAB],
                    waits=w, ch=b.XT[buf])
        tok_sq = None
        if own:
            t_sqa = b.sca(lambda e, bu=buf: e.activation(sb_sq, sb_xts[bu],
                                                         ACT.Square),
                          waits=[t_x] + ([t_zx, t_m2] if s == 0 else [])
                          + sqmm_prev)
            for dc in range(c.DC):
                b.mm(3, pb[3][0:1, 0:c.SLAB], ones_col,
                     sb_sq[:, dc * c.SLAB:(dc + 1) * c.SLAB], dc == 0,
                     dc == c.DC - 1, waits=[t_sqa], inc=(dc == c.DC - 1))
            sqmm_prev = [(PE, PE.n)]
            tok_sq = b.rd(3, "vector", lambda e: e.tensor_copy(
                sb_rr, pb[3][0:1, 0:c.SLAB]), waits=rr_readers)
            rr_readers = []
        rng = s // c.NSLAB_OWN
        last_tok = t_x
        for tt in range(c.TPS):
            tile = s * c.TPS + tt
            ncols = 9 if own else 8
            mm_w = [t_x] + (wg_fix + wgo if s == 0 else [])
            for dc in range(c.DC):
                b.mm(2, pb[2][:, 0:ncols],
                     sb_xts[buf][:, dc * c.SLAB + tt * 128:dc * c.SLAB + tt * 128 + 128],
                     sb_wg[:, dc * 9:dc * 9 + ncols], dc == 0, dc == c.DC - 1,
                     waits=mm_w, inc=(dc == c.DC - 1))
            u8 = pb[2][:, 0:8]
            ohsl = st_oh[:, tile * 16:tile * 16 + 16] if own else sb_v8[:, 32:48]
            t1 = b.rd(2, "vector", lambda e: e.reduce_max(sb_cc[:, 0:1], u8, AX.X,
                                                          negate=True))
            t2 = b.rd(2, "vector", lambda e, o=ohsl: e.tensor_scalar(
                o[:, 0:8], u8, sb_cc[:, 0:1], 0.0, ALU.add, ALU.is_equal),
                waits=[t1] + ([] if own else cnt_prev))
            t3 = b.rd(2, "vector", lambda e, o=ohsl: e.scalar_tensor_tensor(
                sb_v8[:, 0:8], o[:, 0:8], NEG, u8, ALU.mult, ALU.add), waits=[t2])
            t4 = b.vec(lambda e: e.reduce_max(sb_cc[:, 1:2], sb_v8[:, 0:8], AX.X,
                                              negate=True), waits=[t3])
            t5 = b.vec(lambda e, o=ohsl: e.tensor_scalar(
                o[:, 8:16], sb_v8[:, 0:8], sb_cc[:, 1:2], 0.0, ALU.add,
                ALU.is_equal), waits=[t4])
            start_cnt = (s % c.NSLAB_OWN) == 0 and tt == 0
            stop_cnt = (s % c.NSLAB_OWN) == c.NSLAB_OWN - 1 and tt == c.TPS - 1
            b.mm(4, pb[4][0:1, 0:16], ones_col, ohsl, start_cnt, stop_cnt,
                 waits=[t5])
            last_tok = (PE, PE.n)
            cnt_prev = [last_tok]
            if stop_cnt:
                tcv = b.rd(4, "vector", lambda e: e.tensor_copy(
                    sb_rows[0:1, 0:16], pb[4][0:1, 0:16]))
                b.dma("sync", sb_cnt8[rng:rng + 1, :], sb_rows[0:1, 0:16],
                      waits=[tcv])
                last_tok = tcv
            if own:
                cc0 = st_c[:, tile * 16:tile * 16 + 16]
                ta = b.vec(lambda e, o=ohsl: e.tensor_mul(sb_v8[:, 8:16], o[:, 0:8],
                                                          sb_erow), waits=[t5])
                b.vec(lambda e, cc=cc0: e.reduce_sum(cc[:, 3:4], sb_v8[:, 8:16],
                                                     AX.X), waits=[ta])
                tb_ = b.vec(lambda e, o=ohsl: e.tensor_mul(sb_v8[:, 8:16], o[:, 8:16],
                                                           sb_erow), waits=[ta])
                b.vec(lambda e, cc=cc0: e.reduce_sum(cc[:, 4:5], sb_v8[:, 8:16],
                                                     AX.X), waits=[tb_])
                tmc = b.rd(2, "vector", lambda e, cc=cc0: e.tensor_scalar_mul(
                    cc[:, 7:8], pb[2][:, 8:9], 1.0 / c.D))
                ttp = b.tp(5, pb[5][:, 0:1], sb_rr[0:1, tt * 128:tt * 128 + 128],
                           sb_ident[0:1, 0:1], waits=[tok_sq])
                rr_readers.append(ttp)
                v1 = b.rd(5, "vector", lambda e: e.tensor_scalar_mul(
                    sb_cc[:, 2:3], pb[5][:, 0:1], 1.0 / c.D))
                v2 = b.vec(lambda e, cc=cc0: e.tensor_mul(sb_cc[:, 3:4], cc[:, 7:8],
                                                          cc[:, 7:8]), waits=[tmc])
                v3 = b.vec(lambda e: e.tensor_sub(sb_cc[:, 4:5], sb_cc[:, 2:3],
                                                  sb_cc[:, 3:4]), waits=[v1, v2])
                v4 = b.vec(lambda e: e.tensor_scalar_add(sb_cc[:, 5:6],
                                                         sb_cc[:, 4:5], 1e-5),
                           waits=[v3])
                v5 = b.vec(lambda e: e.reciprocal(sb_cc[:, 6:7], sb_cc[:, 5:6]),
                           waits=[v4] + v6_prev)
                v6 = b.sca(lambda e, cc=cc0: e.activation(cc[:, 8:9], sb_cc[:, 6:7],
                                                          ACT.Sqrt), waits=[v5])
                v6_prev = [v6]
                v7 = b.vec(lambda e: e.tensor_mul(sb_cc[:, 7:8], sb_cc[:, 4:5],
                                                  sb_cc[:, 6:7]), waits=[v5])
                v8_ = b.vec(lambda e: e.tensor_scalar_add(sb_cc[:, 8:9],
                                                          sb_cc[:, 7:8], 1e-6),
                            waits=[v7])
                v9 = b.vec(lambda e: e.reciprocal(sb_cc[:, 9:10], sb_cc[:, 8:9]),
                           waits=[v8_] + v10_prev)
                v10 = b.sca(lambda e, cc=cc0: e.activation(
                    cc[:, 9:10], sb_cc[:, 9:10], ACT.Sqrt), waits=[v9])
                v10_prev = [v10]
                tl = b.rd(2, "vector", lambda e, cc=cc0: e.tensor_scalar_mul(
                    sb_v8[:, 16:24], u8, cc[:, 8:9]), waits=[v6, t5])
                tl2 = b.vec(lambda e: e.reduce_max(sb_cc[:, 10:11], sb_v8[:, 16:24],
                                                   AX.X, negate=True), waits=[tl])
                tl3 = b.sca(lambda e, cc=cc0: e.activation(
                    sb_v8[:, 24:32], sb_v8[:, 16:24], ACT.Exp, bias=sb_cc[:, 10:11],
                    accum_out=cc[:, 2:3]), waits=[tl2])
                tl4 = b.vec(lambda e, o=ohsl: e.tensor_mul(
                    sb_v8[:, 16:24], o[:, 0:8], sb_v8[:, 24:32]), waits=[tl3])
                b.vec(lambda e, cc=cc0: e.reduce_sum(cc[:, 0:1], sb_v8[:, 16:24],
                                                     AX.X), waits=[tl4])
                tl5 = b.vec(lambda e, o=ohsl: e.tensor_mul(
                    sb_v8[:, 16:24], o[:, 8:16], sb_v8[:, 24:32]), waits=[tl4])
                b.vec(lambda e, cc=cc0: e.reduce_sum(cc[:, 1:2], sb_v8[:, 16:24],
                                                     AX.X), waits=[tl5])
                b.mm(0, pb[0][:, 0:16], sb_triu, ohsl, True, False,
                     waits=[t5, tok_offs], inc=False)
                b.mm(0, pb[0][:, 0:16], sb_onesrow, sb_offs, False, True,
                     waits=[tok_offs])
                b.mm(1, pb[1][0:1, 0:16], ones_col, ohsl, True, True, waits=[t5])
                tok_offs = b.rd(1, "vector", lambda e: e.tensor_add(
                    sb_offs, sb_offs, pb[1][0:1, 0:16]))
                w1 = b.rd(0, "vector", lambda e, o=ohsl: e.tensor_mul(
                    sb_v8[:, 16:24], o[:, 0:8], pb[0][:, 0:8]))
                b.vec(lambda e, cc=cc0: e.reduce_sum(cc[:, 5:6], sb_v8[:, 16:24],
                                                     AX.X), waits=[w1])
                w2 = b.rd(0, "vector", lambda e, o=ohsl: e.tensor_mul(
                    sb_v8[:, 16:24], o[:, 8:16], pb[0][:, 8:16]), waits=[w1])
                tloc = b.vec(lambda e, cc=cc0: e.reduce_sum(
                    cc[:, 6:7], sb_v8[:, 16:24], AX.X), waits=[w2])
                tile_done[tile] = tloc
                last_tok = tloc
        xts_free[buf] = last_tok

    bar_r = P.barrier()
    b.mm(0, pb[0][0:1, 0:16], sb_below[:, 0:1], sb_cnt8, True, True, waits=bar_r)
    b.mm(1, pb[1][0:1, 0:16], sb_below[:, 1:2], sb_cnt8, True, True, waits=bar_r)
    tpv = b.rd(0, "vector", lambda e: e.tensor_copy(sb_rows[0:1, 0:16],
                                                    pb[0][0:1, 0:16]))
    ttv = b.rd(1, "vector", lambda e: e.tensor_copy(sb_rows[0:1, 16:32],
                                                    pb[1][0:1, 0:16]))
    tpg = b.vec(lambda e: e.tensor_add(sb_rows[0:1, 8:16], sb_rows[0:1, 8:16],
                                       sb_rows[0:1, 16:24]), waits=[tpv, ttv])
    b.mm(2, pb[2][:, 0:16], sb_onesrow, sb_rows[0:1, 0:16], True, True,
         waits=[tpg])
    tpbe = b.rd(2, "vector", lambda e: e.tensor_copy(sb_prefb[:, 0:16],
                                                     pb[2][:, 0:16]))
    b.mm(3, pb[3][:, 0:8], sb_onesrow, sb_cnt8[0:1, 0:8], True, True,
         waits=bar_r)
    tpc0 = b.rd(3, "vector", lambda e: e.tensor_copy(sb_prefb[:, 16:24],
                                                     pb[3][:, 0:8]))

    scat = []
    for tile in range(c.NTILE_OWN):
        cc0 = st_c[:, tile * 16:tile * 16 + 16]
        ohsl = st_oh[:, tile * 16:tile * 16 + 16]
        dep = [tpbe, tile_done[tile]]
        for kk in range(2):
            oh = ohsl[:, 8 * kk:8 * kk + 8]
            locw = cc0[:, 5 + kk:6 + kk]
            ek = cc0[:, 3 + kk:4 + kk]
            a1 = b.vec(lambda e, oh=oh, kk=kk: e.tensor_mul(
                sb_v8[:, 0:8], oh, sb_prefb[:, 8 * kk:8 * kk + 8]), waits=dep)
            a2 = b.vec(lambda e: e.reduce_sum(sb_cc[:, 0:1], sb_v8[:, 0:8], AX.X),
                       waits=[a1])
            a3 = b.vec(lambda e, locw=locw: e.tensor_add(sb_cc[:, 1:2], locw,
                                                         sb_cc[:, 0:1]), waits=[a2])
            a4 = b.vec(lambda e: e.tensor_scalar(sb_cc[:, 2:3], sb_cc[:, 1:2],
                                                 float(c.CGLOB), None, ALU.is_lt),
                       waits=[a3])
            if kk == 0:
                aL = b.vec(lambda e, locw=locw: e.tensor_copy(sb_cc[:, 15:16],
                                                              locw), waits=[a3])
            else:
                aL1 = b.vec(lambda e, oh=oh: e.tensor_mul(
                    sb_v8[:, 8:16], oh, sb_prefb[:, 16:24]), waits=[a3, tpc0])
                aL2 = b.vec(lambda e: e.reduce_sum(sb_cc[:, 14:15],
                                                   sb_v8[:, 8:16], AX.X),
                            waits=[aL1])
                aL = b.vec(lambda e, locw=locw: e.tensor_add(
                    sb_cc[:, 15:16], locw, sb_cc[:, 14:15]), waits=[aL2])
            a5 = b.vec(lambda e: e.tensor_scalar(
                sb_cc[:, 3:4], sb_cc[:, 15:16], float(c.CLOC), None, ALU.is_lt),
                waits=[aL])
            a6 = b.vec(lambda e: e.tensor_mul(sb_cc[:, 4:5], sb_cc[:, 2:3],
                                              sb_cc[:, 3:4]), waits=[a4, a5])
            b.vec(lambda e, cc=cc0, kk=kk: e.tensor_mul(
                cc[:, 12 + kk:13 + kk], cc[:, kk:kk + 1], sb_cc[:, 2:3]), waits=[a4])
            a8 = b.vec(lambda e, ek=ek: e.scalar_tensor_tensor(
                sb_cc[:, 5:6], ek, float(c.CLOC), sb_cc[:, 15:16], ALU.mult,
                ALU.add), waits=[a6])
            a9 = b.vec(lambda e, tile=tile, kk=kk: e.tensor_scalar_add(
                sb_cc[:, 6:7], p2_col, float(c.NSLOT + 256 * tile + kk)), waits=[a6])
            a6i = b.gpc(lambda e: e.tensor_copy(sb_i32[:, 3:4], sb_cc[:, 4:5]),
                        waits=[a6])
            a10 = b.vec(lambda e: e.select(sb_cc[:, 7:8], sb_i32[:, 3:4],
                                           sb_cc[:, 5:6], sb_cc[:, 6:7]),
                        waits=[a8, a9, a6i])
            a11 = b.vec(lambda e: e.memset(sb_cc[:, 8:9], float(c.NSLOT)),
                        waits=[a10])
            a12 = b.vec(lambda e: e.select(sb_cc[:, 9:10], sb_i32[:, 3:4],
                                           sb_cc[:, 5:6], sb_cc[:, 8:9]),
                        waits=[a11])
            b.gpc(lambda e, kk=kk, tile=tile: e.tensor_copy(
                st_yg[:, tile * 2 + kk:tile * 2 + kk + 1], sb_cc[:, 9:10]),
                waits=[a12])
            MAGIC = 12582912.0
            m1 = b.vec(lambda e: e.tensor_scalar_mul(sb_v8[:, 24:25],
                                                     sb_cc[:, 7:8], 1.0 / 128.0),
                       waits=[a10])
            m2 = b.vec(lambda e: e.tensor_scalar_add(sb_v8[:, 25:26],
                                                     sb_v8[:, 24:25], -0.49995),
                       waits=[m1])
            m3 = b.vec(lambda e: e.tensor_scalar_add(sb_v8[:, 26:27],
                                                     sb_v8[:, 25:26], MAGIC),
                       waits=[m2])
            m4 = b.vec(lambda e: e.tensor_scalar_add(sb_v8[:, 27:28],
                                                     sb_v8[:, 26:27], -MAGIC),
                       waits=[m3])
            m5 = b.vec(lambda e: e.tensor_scalar_mul(sb_v8[:, 28:29],
                                                     sb_cc[:, 7:8],
                                                     float(c.NT_IMG)),
                       waits=[a10])
            m6 = b.vec(lambda e: e.scalar_tensor_tensor(
                sb_v8[:, 29:30], sb_v8[:, 27:28], -(128.0 * c.NT_IMG - 1.0),
                sb_v8[:, 28:29], ALU.mult, ALU.add), waits=[m4, m5])
            m7 = b.gpc(lambda e: e.tensor_copy(sb_u32[:, 0:1], sb_v8[:, 29:30]),
                       waits=[m6] + scat[-1:])
            b5 = b.vec(lambda e, tile=tile: e.tensor_scalar_add(
                sb_cc[:, 10:11], p_col, float(128 * tile)), waits=[a10])
            b6 = b.gpc(lambda e: e.tensor_copy(sb_u32[:, 1:2], sb_cc[:, 10:11]),
                       waits=[b5] + scat[-1:])
            sc_t = b.gpd(lambda e: e.indirect_dma_start(
                out=slotimg.rearrange("p n -> (p n)").unsqueeze(1),
                out_offset=IndirectOffsetOnAxis(sb_u32[:, 0:1], 0),
                in_=sb_u32[:, 1:2], in_offset=None), waits=[m7, b6, t_si])
            scat.append(sc_t)
        g1 = b.vec(lambda e, cc=cc0: e.tensor_scalar_mul(sb_cc[:, 11:12],
                                                         cc[:, 2:3], 1e-9),
                   waits=[scat[-1]])
        g2 = b.vec(lambda e, cc=cc0: e.tensor_add(sb_cc[:, 12:13], cc[:, 12:13],
                                                  cc[:, 13:14]), waits=[scat[-1]])
        g3 = b.vec(lambda e: e.tensor_max(sb_cc[:, 13:14], sb_cc[:, 12:13],
                                          sb_cc[:, 11:12]), waits=[g1, g2])
        g4 = b.vec(lambda e: e.reciprocal(sb_cc[:, 14:15], sb_cc[:, 13:14]),
                   waits=[g3])
        b.vec(lambda e, cc=cc0: e.tensor_mul(cc[:, 10:11], cc[:, 12:13],
                                             sb_cc[:, 14:15]), waits=[g4])
        b.vec(lambda e, cc=cc0: e.tensor_mul(cc[:, 11:12], cc[:, 13:14],
                                             sb_cc[:, 14:15]), waits=[g4])

    bar_scat = P.barrier()
    t_slot = b.dma("sync", sb_slotcols, slotimg, waits=bar_scat, ch=CS)

    if upto == "R":
        sdump = DT("sdump", [128, NT_ * 48 + c.NT_IMG + 64], F32,
                   kind="ExternalOutput").ap()
        bar_d = P.barrier()
        b.dma("sync", sdump[:, 0:NT_ * 16], st_c, waits=bar_d)
        b.dma("sync", sdump[:, NT_ * 16:NT_ * 32], st_oh, waits=bar_d)
        b.dma("sync", sdump[:, NT_ * 32:NT_ * 34], st_yg.bitcast(F32), waits=bar_d)
        b.dma("sync", sdump[:, NT_ * 34:NT_ * 34 + c.NT_IMG],
              sb_slotcols.bitcast(F32), waits=[t_slot] + bar_d)
        b.dma("sync", sdump[:, NT_ * 34 + c.NT_IMG:NT_ * 34 + c.NT_IMG + 16],
              sb_prefb[:, 0:16], waits=bar_d)
        fin = P.barrier()
        for eng in ("sync", "scalar", "vector", "tensor", "gpsimd"):
            P.add(eng, lambda e: e.nop(), waits=fin)
        P.emit(nc)
        return nc

    def build_rows(stile, want_r3, waits):
        toks = []
        prev = list(waits)
        for j in range(c.TPS):
            cc0 = st_c[:, (stile + j) * 16:(stile + j) * 16 + 16]
            tmu = b.vec(lambda e, cc=cc0: e.tensor_mul(sb_cc[:, 0:1], cc[:, 7:8],
                                                       cc[:, 8:9]), waits=prev)
            b.tp(5, pb[5][0:1, 0:128], cc0[:, 8:9], sb_ident, waits=prev)
            te1 = b.rd(5, "vector", lambda e, j=j: e.tensor_copy(
                sb_r1[0:1, j * 128:(j + 1) * 128], pb[5][0:1, 0:128]))
            b.tp(5, pb[5][0:1, 0:128], sb_cc[:, 0:1], sb_ident, waits=[tmu])
            te2 = b.rd(5, "vector", lambda e, j=j: e.tensor_copy(
                sb_r2[0:1, j * 128:(j + 1) * 128], pb[5][0:1, 0:128]))
            toks += [te1, te2]
            prev = [te2]
            if want_r3:
                b.tp(5, pb[5][0:1, 0:128], cc0[:, 9:10], sb_ident, waits=[te2])
                te3 = b.rd(5, "vector", lambda e, j=j: e.tensor_copy(
                    sb_r3[0:1, j * 128:(j + 1) * 128], pb[5][0:1, 0:128]))
                toks.append(te3)
                prev = [te3]
        return toks

    xn_last = []
    s_stores = []
    a4_c1p, a4_mmp = [], []
    fxprev = []
    prev_ab = []
    for sq in range(c.NSEQ):
        t00 = sq * c.L
        h_tok = []
        xslab_free = None
        for sl in range(c.NQS):
            tok0 = t00 + sl * c.SLAB
            stile = tok0 // 128
            row_toks = build_rows(stile, True,
                                  [tile_done[stile + j] for j in range(c.TPS)])
            b.mm(0, pb[0][:, 0:c.SLAB], sb_onesrow, sb_r1[0:1, 0:c.SLAB], True,
                 True, waits=row_toks)
            b.mm(1, pb[1][:, 0:c.SLAB], sb_onesrow, sb_r2[0:1, 0:c.SLAB], True,
                 True, waits=row_toks)
            b.mm(2, pb[2][:, 0:c.SLAB], sb_onesrow, sb_r3[0:1, 0:c.SLAB], True,
                 True, waits=row_toks)
            t_x = b.dma("sync", sb_xts[0],
                        xT[:, (tok0 // c.SLAB) * c.DC * c.SLAB:
                           (tok0 // c.SLAB + 1) * c.DC * c.SLAB],
                        waits=[xslab_free] if xslab_free else bar_scat,
                        ch=b.XT[0])
            tm_toks = []
            c3p, c7p, tmp_ = [], [], []
            for dc in range(c.DC):
                xsl = sb_xts[0][:, dc * c.SLAB:(dc + 1) * c.SLAB]
                c1 = b.rd(0, "vector", lambda e, xsl=xsl: e.tensor_mul(
                    sb_fx, xsl, pb[0][:, 0:c.SLAB]), waits=[t_x] + c3p)
                c2 = b.rd(1, "vector", lambda e: e.tensor_sub(
                    sb_fx, sb_fx, pb[1][:, 0:c.SLAB]), waits=[c1])
                c3 = b.sca(lambda e: e.activation(sb_xn2[:, 0:c.SLAB], sb_fx, ACT.Copy),
                           waits=[c2] + tmp_)
                c4 = b.vec(lambda e: e.tensor_sub(sb_xn2[:, c.SLAB:], sb_fx,
                                                  sb_xn2[:, 0:c.SLAB]), waits=[c3])
                c5 = b.rd(2, "vector", lambda e: e.tensor_mul(
                    sb_fx2, sb_fx, pb[2][:, 0:c.SLAB]), waits=[c2] + c7p)
                c6 = b.vec(lambda e, dc=dc: e.tensor_scalar_mul(
                    sb_fx2, sb_fx2, sb_nw[:, dc:dc + 1]), waits=[c5])
                hh = sb_h[0][:, dc * c.L + sl * c.SLAB:dc * c.L + sl * c.SLAB + c.SLAB]
                hl = sb_h[1][:, dc * c.L + sl * c.SLAB:dc * c.L + sl * c.SLAB + c.SLAB]
                c7 = b.sca(lambda e, hh=hh: e.activation(hh, sb_fx2, ACT.Copy), waits=[c6])
                c8 = b.vec(lambda e, hh=hh, hl=hl: e.tensor_sub(hl, sb_fx2, hh),
                           waits=[c7])
                h_tok.append(c8)
                c3p, c7p = [c3], [c7]
                tmp_ = []
                for j in range(c.TPS):
                    for hilo in range(2):
                        dst = sb_tm[:, j * 2 * c.D + hilo * c.D + dc * 128:
                                    j * 2 * c.D + hilo * c.D + dc * 128 + 128]
                        te = bf16_tp(
                            (lambda e, ps, d=dst: e.tensor_copy(d, ps)),
                            sb_xn2[:, hilo * c.SLAB + j * 128:
                                   hilo * c.SLAB + j * 128 + 128],
                            [c3 if hilo == 0 else c4]
                            + ((xn_last[-c.TPS:] + s_stores[-c.TPS:])
                               if dc == 0 and j == 0 and hilo == 0 else []))
                        tmp_.append(te)
                        tm_toks.append(te)
            xslab_free = tm_toks[-1]
            for j in range(c.TPS):
                xn_last.append(b.dma("sync",
                                     xnbuf[tok0 + j * 128:tok0 + j * 128 + 128, :],
                                     sb_tm[:, j * 2 * c.D:(j + 1) * 2 * c.D],
                                     waits=tm_toks))
        bar_h = h_tok + xn_last[-c.TPS * c.NQS:]

        k_done = []
        for kvh in range(c.NKV):
            for slq in range(c.NQS):
                lastmm = {}
                for fc2 in range(c.HDC):
                    bank = fc2
                    col0 = (kvh * c.HDC + fc2) * 128
                    tw, wuh, wul, wsl = w_unit(
                        Wk[:, col0 * c.DC:(col0 + 128) * c.DC], c.DC * 128,
                        bar_h if (slq == 0 and fc2 == 0 and kvh == 0) else [])
                    for dc in range(c.DC):
                        wh_ = wuh[:, dc * 128:dc * 128 + 128]
                        wl_ = wul[:, dc * 128:dc * 128 + 128]
                        rh = sb_h[0][:, dc * c.L + slq * c.SLAB:dc * c.L + slq * c.SLAB + c.SLAB]
                        rl = sb_h[1][:, dc * c.L + slq * c.SLAB:dc * c.L + slq * c.SLAB + c.SLAB]
                        b.mm(bank, pb[bank][:, 0:c.SLAB], wh_, rh, dc == 0, False,
                             waits=([tw] + bar_h) if dc == 0 else [], inc=False)
                        b.mm(bank, pb[bank][:, 0:c.SLAB], wh_, rl, False, False,
                             inc=False)
                        t_l = b.mm(bank, pb[bank][:, 0:c.SLAB], wl_, rh, False,
                                   dc == c.DC - 1)
                    w_used(wsl, t_l)
                    lastmm[fc2] = t_l
                sq1 = b.rd(0, "scalar", lambda e: e.activation(
                    sb_fx, pb[0][:, 0:c.SLAB], ACT.Square), waits=fxprev)
                b.mm(4, pb[4][0:1, 0:c.SLAB], ones_col, sb_fx, True, c.HDC == 1,
                     waits=[sq1], inc=(c.HDC == 1))
                if c.HDC > 1:
                    sq2 = b.rd(1, "scalar", lambda e: e.activation(
                        sb_fx2, pb[1][:, 0:c.SLAB], ACT.Square))
                    b.mm(4, pb[4][0:1, 0:c.SLAB], ones_col, sb_fx2, False, True,
                         waits=[sq2])
                n1 = b.rd(4, "scalar", lambda e: e.activation(
                    sb_rr, pb[4][0:1, 0:c.SLAB], ACT.Sqrt))
                n2 = b.vec(lambda e: e.tensor_scalar_max(sb_rr, sb_rr, 1e-12),
                           waits=[n1])
                n3 = b.vec(lambda e: e.reciprocal(sb_rr, sb_rr), waits=[n2])
                b.mm(5, pb[5][:, 0:c.SLAB], sb_onesrow, sb_rr, True, True,
                     waits=[n3])
                be = b.rd(5, "vector", lambda e: e.tensor_copy(
                    sb_fx3, pb[5][:, 0:c.SLAB]))
                hhp = []
                for fc2 in range(c.HDC):
                    base_k = kvh * c.HDC * c.L + fc2 * c.L + slq * c.SLAB
                    pr = b.rd(fc2, "vector", lambda e, fc2=fc2: e.tensor_mul(
                        sb_fx, pb[fc2][:, 0:c.SLAB], sb_fx3), waits=[be] + hhp)
                    hh = b.sca(lambda e, base_k=base_k: e.activation(
                        sb_k[0][:, base_k:base_k + c.SLAB], sb_fx, ACT.Copy), waits=[pr])
                    hhp = [hh]
                    ll = b.vec(lambda e, base_k=base_k: e.tensor_sub(
                        sb_k[1][:, base_k:base_k + c.SLAB], sb_fx,
                        sb_k[0][:, base_k:base_k + c.SLAB]), waits=[hh])
                    k_done.append(ll)
                fxprev = [ll, be, hh]
        v_done = {}
        NVG = min(4, c.KT)
        for tg in range(0, c.KT, NVG):
            ng = min(NVG, c.KT - tg)
            lastv = {}
            VW = c.NKV * c.HD
            for dp in range(0, c.DC, 2):
                tw, wuh, wul, wsl = w_unit(Wv[:, dp * VW:(dp + 2) * VW], 2 * VW,
                                           bar_h if tg == 0 and dp == 0 else [])
                for d2 in range(2):
                    dc = dp + d2
                    wvh = wuh[:, d2 * VW:(d2 + 1) * VW]
                    wvl = wul[:, d2 * VW:(d2 + 1) * VW]
                    for ti in range(ng):
                        tt = tg + ti
                        bank = ti
                        l0 = sb_h[0][:, dc * c.L + tt * 128:dc * c.L + tt * 128 + 128]
                        l1 = sb_h[1][:, dc * c.L + tt * 128:dc * c.L + tt * 128 + 128]
                        b.mm(bank, pb[bank][:, 0:VW], l0, wvh, dc == 0,
                             False, waits=(([tw] + bar_h) if ti == 0 and d2 == 0
                                           else []), inc=False)
                        b.mm(bank, pb[bank][:, 0:VW], l0, wvl, False, False,
                             inc=False)
                        lastv[ti] = b.mm(bank, pb[bank][:, 0:VW], l1, wvh,
                                         False, dc == c.DC - 1)
                w_used(wsl, lastv[ng - 1])
            for ti in range(ng):
                tt = tg + ti
                vh = sb_vv[0][:, tt * c.NKV * c.HD:(tt + 1) * c.NKV * c.HD]
                vl = sb_vv[1][:, tt * c.NKV * c.HD:(tt + 1) * c.NKV * c.HD]
                e1 = b.rd(ti, "scalar", lambda e, vh=vh, ti=ti: e.activation(
                    vh, pb[ti][:, 0:c.NKV * c.HD], ACT.Copy))
                e2 = b.rd(ti, "vector", lambda e, vh=vh, vl=vl, ti=ti: e.tensor_sub(
                    vl, pb[ti][:, 0:c.NKV * c.HD], vh), waits=[e1])
                v_done[tt] = e2
        v_all = list(v_done.values())

        for kvh in range(c.NKV):
            for qh in range(kvh * c.NREP, (kvh + 1) * c.NREP):
                for qs in range(c.NQS):
                    for fc2 in range(c.HDC):
                        bank = fc2
                        col0 = (qh * c.HDC + fc2) * 128
                        tw, wuh, wul, wsl = w_unit(
                            Wq[:, col0 * c.DC:(col0 + 128) * c.DC],
                            c.DC * 128, [])
                        for dc in range(c.DC):
                            wh_ = wuh[:, dc * 128:dc * 128 + 128]
                            wl_ = wul[:, dc * 128:dc * 128 + 128]
                            rh = sb_h[0][:, dc * c.L + qs * c.SLAB:dc * c.L + qs * c.SLAB + c.SLAB]
                            rl = sb_h[1][:, dc * c.L + qs * c.SLAB:dc * c.L + qs * c.SLAB + c.SLAB]
                            b.mm(bank, pb[bank][:, 0:c.SLAB], wh_, rh, dc == 0,
                                 False, waits=([tw] + bar_h) if dc == 0 else [],
                                 inc=False)
                            b.mm(bank, pb[bank][:, 0:c.SLAB], wh_, rl, False,
                                 False, inc=False)
                            t_l = b.mm(bank, pb[bank][:, 0:c.SLAB], wl_, rh,
                                       False, dc == c.DC - 1)
                        w_used(wsl, t_l)
                    sq1 = b.rd(0, "scalar", lambda e: e.activation(
                        sb_fx, pb[0][:, 0:c.SLAB], ACT.Square), waits=fxprev)
                    b.mm(4, pb[4][0:1, 0:c.SLAB], ones_col, sb_fx, True,
                         c.HDC == 1, waits=[sq1], inc=(c.HDC == 1))
                    if c.HDC > 1:
                        sq2 = b.rd(1, "scalar", lambda e: e.activation(
                            sb_fx2, pb[1][:, 0:c.SLAB], ACT.Square))
                        b.mm(4, pb[4][0:1, 0:c.SLAB], ones_col, sb_fx2, False,
                             True, waits=[sq2])
                    n1 = b.rd(4, "scalar", lambda e: e.activation(
                        sb_rr, pb[4][0:1, 0:c.SLAB], ACT.Sqrt))
                    n2 = b.vec(lambda e: e.tensor_scalar_max(sb_rr, sb_rr, 1e-12),
                               waits=[n1])
                    n3 = b.vec(lambda e: e.reciprocal(sb_rr, sb_rr), waits=[n2])
                    n4 = b.vec(lambda e: e.tensor_scalar_mul(
                        sb_rr, sb_rr, sb_scal[0:1, 2:3]), waits=[n3, t_ts])
                    b.mm(5, pb[5][:, 0:c.SLAB], sb_onesrow, sb_rr, True, True,
                         waits=[n4])
                    be = b.rd(5, "vector", lambda e: e.tensor_copy(
                        sb_fx3, pb[5][:, 0:c.SLAB]))
                    qn_tok = []
                    hhp = []
                    for fc2 in range(c.HDC):
                        pr = b.rd(fc2, "vector", lambda e, fc2=fc2: e.tensor_mul(
                            sb_fx, pb[fc2][:, 0:c.SLAB], sb_fx3), waits=[be] + hhp)
                        hh = b.sca(lambda e, fc2=fc2: e.activation(
                            sb_qn[0][:, fc2 * c.SLAB:(fc2 + 1) * c.SLAB], sb_fx,
                            ACT.Copy), waits=[pr])
                        hhp = [hh]
                        ll = b.vec(lambda e, fc2=fc2: e.tensor_sub(
                            sb_qn[1][:, fc2 * c.SLAB:(fc2 + 1) * c.SLAB], sb_fx,
                            sb_qn[0][:, fc2 * c.SLAB:(fc2 + 1) * c.SLAB]),
                            waits=[hh])
                        qn_tok.append(ll)
                    fxprev = [ll, be, hh]
                    ktmax = (qs + 1) * c.TPS
                    pt_free = [None, None]
                    dn = None
                    lastpv = None
                    for kt in range(ktmax):
                        first = True
                        for fc2 in range(c.HDC):
                            kb = kvh * c.HDC * c.L + fc2 * c.L + kt * 128
                            kap_h = sb_k[0][:, kb:kb + 128]
                            kap_l = sb_k[1][:, kb:kb + 128]
                            qap_h = sb_qn[0][:, fc2 * c.SLAB:(fc2 + 1) * c.SLAB]
                            qap_l = sb_qn[1][:, fc2 * c.SLAB:(fc2 + 1) * c.SLAB]
                            b.mm(0, pb[0][:, 0:c.SLAB], kap_h, qap_h, first,
                                 False, waits=qn_tok + k_done[-1:], inc=False)
                            first = False
                            b.mm(0, pb[0][:, 0:c.SLAB], kap_h, qap_l, False,
                                 False, inc=False)
                            smm = b.mm(0, pb[0][:, 0:c.SLAB], kap_l, qap_h, False,
                                       fc2 == c.HDC - 1)
                        off = kt * 128 - qs * c.SLAB
                        if 0 <= off < c.SLAB:
                            smm = b.rd(0, "vector", lambda e, off=off: e.tensor_add(
                                pb[0][:, off:off + 128], pb[0][:, off:off + 128],
                                sb_trimT))
                            b.bk_w[0] = smm
                        ex = b.rd(0, "scalar", lambda e: e.activation(
                            sb_fx, pb[0][:, 0:c.SLAB], ACT.Exp),
                            waits=(pl_prev if kt > 0 else fxprev))
                        if 0 < off < c.SLAB:
                            ex = b.vec(lambda e, off=off: e.memset(
                                sb_fx[:, 0:off], 0.0), waits=[ex])
                        bs = 0
                        wpt = [x for x in (pt_free[bs],) if x]
                        ph = b.sca(lambda e, bs=bs: e.activation(
                            sb_pt[2 * bs], sb_fx, ACT.Copy), waits=[ex] + wpt)
                        pl = b.vec(lambda e, bs=bs: e.tensor_sub(
                            sb_pt[2 * bs + 1], sb_fx, sb_pt[2 * bs]), waits=[ph])
                        pl_prev = [pl]
                        b.mm(1, pb[1][0:1, 0:c.SLAB], sb_onesb, sb_pt[2 * bs],
                             kt == 0, False, waits=[pl, t_ob], inc=False)
                        dn = b.mm(1, pb[1][0:1, 0:c.SLAB], sb_onesb,
                                  sb_pt[2 * bs + 1], False, kt == ktmax - 1,
                                  waits=[pl])
                        for fc2 in range(c.HDC):
                            vb = kt * c.NKV * c.HD + kvh * c.HD + fc2 * 128
                            vap_h = sb_vv[0][:, vb:vb + 128]
                            vap_l = sb_vv[1][:, vb:vb + 128]
                            bank = 2 + fc2
                            b.mm(bank, pb[bank][:, 0:c.SLAB], vap_h,
                                 sb_pt[2 * bs], kt == 0, False,
                                 waits=[pl] + v_all[-1:], inc=False)
                            b.mm(bank, pb[bank][:, 0:c.SLAB], vap_h,
                                 sb_pt[2 * bs + 1], False, False, inc=False)
                            lastpv = b.mm(bank, pb[bank][:, 0:c.SLAB], vap_l,
                                          sb_pt[2 * bs], False, kt == ktmax - 1)
                        pt_free[bs] = lastpv
                    rd_ = b.rd(1, "vector", lambda e: e.reciprocal(
                        sb_rr, pb[1][0:1, 0:c.SLAB]))
                    b.mm(4, pb[4][:, 0:c.SLAB], sb_onesrow, sb_rr, True, True,
                         waits=[rd_])
                    be2 = b.rd(4, "vector", lambda e: e.tensor_copy(
                        sb_fx3, pb[4][:, 0:c.SLAB]))
                    hhp = []
                    for fc2 in range(c.HDC):
                        r_ch = qh * c.HDC + fc2
                        pr = b.rd(2 + fc2, "vector", lambda e, fc2=fc2:
                                  e.tensor_mul(sb_fx, pb[2 + fc2][:, 0:c.SLAB],
                                               sb_fx3), waits=[be2] + hhp)
                        hh = b.sca(lambda e: e.activation(
                            sb_ab[:, 0:c.SLAB], sb_fx, ACT.Copy),
                            waits=[pr] + prev_ab)
                        hhp = [hh]
                        ll = b.vec(lambda e: e.tensor_sub(
                            sb_ab[:, c.SLAB:], sb_fx, sb_ab[:, 0:c.SLAB]),
                            waits=[hh])
                        ab0 = ((sq * 2 + 0) * c.DC + r_ch) * 128
                        ab1 = ((sq * 2 + 1) * c.DC + r_ch) * 128
                        d1 = b.dma("sync",
                              aobuf[ab0:ab0 + 128, qs * c.SLAB:(qs + 1) * c.SLAB],
                              sb_ab[:, 0:c.SLAB], waits=[ll])
                        d2 = b.dma("sync",
                              aobuf[ab1:ab1 + 128, qs * c.SLAB:(qs + 1) * c.SLAB],
                              sb_ab[:, c.SLAB:], waits=[ll])
                        prev_ab = [d1, d2]
                    fxprev = [ll, be2, hh]

        bar_att = P.barrier()
        for sl in range(c.NQS):
            tok0 = t00 + sl * c.SLAB
            stile = tok0 // 128
            row_toks = build_rows(stile, False, bar_att)
            b.mm(2, pb[2][:, 0:c.SLAB], sb_onesrow, sb_r1[0:1, 0:c.SLAB], True,
                 True, waits=row_toks)
            b.mm(3, pb[3][:, 0:c.SLAB], sb_onesrow, sb_r2[0:1, 0:c.SLAB], True,
                 True, waits=row_toks)
            t_x = b.dma("sync", sb_xts[0],
                        xT[:, (tok0 // c.SLAB) * c.DC * c.SLAB:
                           (tok0 // c.SLAB + 1) * c.DC * c.SLAB],
                        waits=bar_att + a4_c1p, ch=b.XT[0])
            ao_toks = []
            for r in range(c.DC):
                ab0 = ((sq * 2 + 0) * c.DC + r) * 128
                ab1 = ((sq * 2 + 1) * c.DC + r) * 128
                ao_toks.append(b.dma("sync",
                                     sb_aoin[0][:, r * c.SLAB:(r + 1) * c.SLAB],
                                     aobuf[ab0:ab0 + 128, sl * c.SLAB:(sl + 1) * c.SLAB],
                                     waits=bar_att + a4_mmp))
                ao_toks.append(b.dma("sync",
                                     sb_aoin[1][:, r * c.SLAB:(r + 1) * c.SLAB],
                                     aobuf[ab1:ab1 + 128, sl * c.SLAB:(sl + 1) * c.SLAB],
                                     waits=bar_att))
            st_toks = []
            a4_trp = []
            for dout in range(c.DC):
                bank = dout % 2
                lastmm = None
                tw, wuh, wul, wsl = w_unit(
                    Wo[:, dout * c.DC * 128:(dout + 1) * c.DC * 128],
                    c.DC * 128, [])
                for r in range(c.DC):
                    woh = wuh[:, r * 128:r * 128 + 128]
                    wol = wul[:, r * 128:r * 128 + 128]
                    ah = sb_aoin[0][:, r * c.SLAB:(r + 1) * c.SLAB]
                    al = sb_aoin[1][:, r * c.SLAB:(r + 1) * c.SLAB]
                    b.mm(bank, pb[bank][:, 0:c.SLAB], woh, ah, r == 0, False,
                         waits=([tw] + ao_toks) if r == 0 else [], inc=False)
                    b.mm(bank, pb[bank][:, 0:c.SLAB], woh, al, False, False,
                         inc=False)
                    lastmm = b.mm(bank, pb[bank][:, 0:c.SLAB], wol, ah, False,
                                  r == c.DC - 1)
                w_used(wsl, lastmm)
                xsl = sb_xts[0][:, dout * c.SLAB:(dout + 1) * c.SLAB]
                c1 = b.rd(2, "vector", lambda e, xsl=xsl: e.tensor_mul(
                    sb_fx, xsl, pb[2][:, 0:c.SLAB]), waits=[t_x])
                c2 = b.rd(3, "vector", lambda e: e.tensor_sub(
                    sb_fx, sb_fx, pb[3][:, 0:c.SLAB]), waits=[c1])
                c3 = b.rd(bank, "vector", lambda e, bank=bank:
                          e.scalar_tensor_tensor(sb_fx2, pb[bank][:, 0:c.SLAB],
                                                 sb_gcol[:, 0:1], sb_fx, ALU.mult,
                                                 ALU.add), waits=[c2, t_gce] + a4_trp)
                a4_trp = []
                for j in range(c.TPS):
                    ttr = b.tp(5, pb[5][:, 0:128],
                               sb_fx2[:, j * 128:j * 128 + 128],
                               sb_ident, waits=[c3]
                               + (s_stores[-c.TPS:] if dout == 0 and j == 0
                                  else []))
                    a4_trp.append(ttr)
                    te = b.rd(5, "vector", lambda e, j=j, dout=dout: e.tensor_copy(
                        sb_st[:, j * c.D + dout * 128:j * c.D + dout * 128 + 128],
                        pb[5][:, 0:128]))
                    st_toks.append(te)
                a4_c1p = [c1]
                a4_mmp = [lastmm]
            for j in range(c.TPS):
                s_stores.append(
                    b.dma("sync", sbufd[tok0 + j * 128:tok0 + j * 128 + 128, :],
                          sb_st[:, j * c.D:(j + 1) * c.D], waits=st_toks))

    bar_a = P.barrier()
    fx3_readers = []
    m_prp, m_h1p = [], []
    y_busy = [None] * 4
    gt_free = [None, None]
    act_free = None
    nhs = len(c.HSLABS)
    g1_guard = []
    for ee in range(c.E):
        for hf in range(c.NHALF):
            tile0 = ee * c.TPE + hf * c.TPH
            xg_toks = []
            for tt in range(c.TPH):
                gb = tt % 2
                gsl = sb_gt[:, gb * 2 * c.D:(gb + 1) * 2 * c.D]
                wts = [t_slot] + (bar_a if tt < 2 else []) + \
                      ([gt_free[gb]] if gt_free[gb] else []) + \
                      (g1_guard if tt == 0 else [])
                tg = b.gpd(lambda e, gsl=gsl, ti=tile0 + tt: e.indirect_dma_start(
                    out=gsl, out_offset=None, in_=xnbuf,
                    in_offset=IndirectOffsetOnAxis(sb_slotcols[:, ti:ti + 1], 0)),
                    waits=wts, ch=b.GB[gb])
                lastte = None
                for dc in range(c.DC):
                    for hilo in range(2):
                        dst = sb_xg[hilo][:, dc * c.HSLOT + tt * 128:
                                          dc * c.HSLOT + tt * 128 + 128]
                        te = bf16_tp(
                            (lambda e, ps, d=dst: e.tensor_copy(d, ps)),
                            gsl[:, hilo * c.D + dc * 128:
                                hilo * c.D + dc * 128 + 128],
                            [tg])
                        lastte = te
                        xg_toks.append(te)
                gt_free[gb] = lastte
            xg_last = xg_toks[-1]
            act_toks = []
            for fp in range(c.FC):
                lastg1 = {}
                for gi, fch in enumerate((fp, fp + c.FC)):
                    u0_ = (ee * 2 * c.FC + fch) * c.DC * 128
                    tw, wuh, wul, wsl = w_unit(
                        W_in[:, u0_:u0_ + c.DC * 128], c.DC * 128, [])
                    for dc in range(c.DC):
                        wih = wuh[:, dc * 128:dc * 128 + 128]
                        wil = wul[:, dc * 128:dc * 128 + 128]
                        s0 = 0
                        for si, wdt in enumerate(c.HSLABS):
                            bank = gi * nhs + si
                            rhs_h = sb_xg[0][:, dc * c.HSLOT + s0:dc * c.HSLOT + s0 + wdt]
                            rhs_l = sb_xg[1][:, dc * c.HSLOT + s0:dc * c.HSLOT + s0 + wdt]
                            b.mm(bank, pb[bank][:, 0:wdt], wih, rhs_h, dc == 0,
                                 False, waits=([tw, xg_last] if dc == 0 and
                                               si == 0 else []), inc=False)
                            b.mm(bank, pb[bank][:, 0:wdt], wih, rhs_l, False,
                                 False, inc=False)
                            lastg1[(gi, si)] = b.mm(bank, pb[bank][:, 0:wdt], wil,
                                                    rhs_h, False, dc == c.DC - 1)
                            s0 += wdt
                    w_used(wsl, lastg1[(gi, nhs - 1)])
                s0 = 0
                for si, wdt in enumerate(c.HSLABS):
                    ga = b.rd(si, "scalar", lambda e, si=si, wdt=wdt: e.activation(
                        sb_fx[:, 0:wdt], pb[si][:, 0:wdt], ACT.Gelu),
                        waits=m_prp)
                    pr = b.rd(nhs + si, "vector", lambda e, si=si, wdt=wdt:
                              e.tensor_mul(sb_fx2[:, 0:wdt], sb_fx[:, 0:wdt],
                                           pb[nhs + si][:, 0:wdt]),
                              waits=[ga] + m_h1p)
                    m_prp = [pr]
                    ah = sb_act[0][:, fp * c.HSLOT + s0:fp * c.HSLOT + s0 + wdt]
                    al = sb_act[1][:, fp * c.HSLOT + s0:fp * c.HSLOT + s0 + wdt]
                    h1 = b.sca(lambda e, ah=ah, wdt=wdt: e.activation(
                        ah, sb_fx2[:, 0:wdt], ACT.Copy), waits=[pr])
                    m_h1p = [h1]
                    l1 = b.vec(lambda e, ah=ah, al=al, wdt=wdt: e.tensor_sub(
                        al, sb_fx2[:, 0:wdt], ah), waits=[h1])
                    act_toks.append(l1)
                    s0 += wdt
            g1_guard = [lastg1[(1, nhs - 1)]]
            if "mdump" in dbg:
                ti_ = ee * c.NHALF + hf
                g1_guard.append(b.dma(
                    "sync", xgdump[ti_ * 128:(ti_ + 1) * 128, :], sb_xg[0],
                    waits=[xg_last]))
                FW = c.FC * c.HSLOT
                b.dma("sync", actdump[ti_ * 128:(ti_ + 1) * 128, 0:FW],
                      sb_act[0], waits=act_toks[-1:])
                b.dma("sync", actdump[ti_ * 128:(ti_ + 1) * 128, FW:2 * FW],
                      sb_act[1], waits=act_toks[-1:])
            FH = max(1, c.FC // 2)
            for dout in range(c.DC):
                lastg2 = {}
                for fh in range(c.FC // FH):
                    u0_ = ((ee * (c.FC // FH) + fh) * c.DC + dout) * FH * 128
                    tw, wuh, wul, wsl = w_unit(
                        W_out[:, u0_:u0_ + FH * 128], FH * 128,
                        act_toks[-1:] if fh == 0 and dout == 0 else [])
                    if "mdump" in dbg and fh == 0 and dout == 0:
                        ti_ = ee * c.NHALF + hf
                        b.dma("sync", wbdump[ti_ * 128:(ti_ + 1) * 128, :],
                              sb_wb[:, wsl * 2048:wsl * 2048 + 2048],
                              waits=[tw])
                    for f8 in range(FH):
                        fc = fh * FH + f8
                        wgh = wuh[:, f8 * 128:f8 * 128 + 128]
                        wgl = wul[:, f8 * 128:f8 * 128 + 128]
                        s0 = 0
                        for si, wdt in enumerate(c.HSLABS):
                            bank = (dout * nhs + si) % 4
                            ath = sb_act[0][:, fc * c.HSLOT + s0:fc * c.HSLOT + s0 + wdt]
                            atl = sb_act[1][:, fc * c.HSLOT + s0:fc * c.HSLOT + s0 + wdt]
                            b.mm(bank, pb[bank][:, 0:wdt], wgh, ath, fc == 0,
                                 False, waits=(([tw] + (act_toks[-1:] if fh == 0
                                                        and dout == 0 else []))
                                               if f8 == 0 and si == 0 else []),
                                 inc=False)
                            b.mm(bank, pb[bank][:, 0:wdt], wgh, atl, False, False,
                                 inc=False)
                            lastg2[si] = b.mm(bank, pb[bank][:, 0:wdt], wgl, ath,
                                              False, fc == c.FC - 1)
                            s0 += wdt
                    w_used(wsl, lastg2[nhs - 1])
                s0 = 0
                for si, wdt in enumerate(c.HSLABS):
                    bank = (dout * nhs + si) % 4
                    tev = b.rd(bank, "vector", lambda e, wdt=wdt, bank=bank: e.tensor_copy(
                        sb_fx3[:, 0:wdt], pb[bank][:, 0:wdt]),
                        waits=fx3_readers)
                    fx3_readers = []
                    for j in range(wdt // 128):
                        tb = 4 + (j % 2)
                        ttp = b.tp(tb, pb[tb][:, 0:128],
                                   sb_fx3[:, j * 128:j * 128 + 128], sb_ident,
                                   waits=[tev])
                        fx3_readers.append(ttp)
                        ysl = sb_y[:, (j % 4) * 128:(j % 4) * 128 + 128]
                        te = b.rd(tb, "vector", lambda e, ysl=ysl, tb=tb:
                                  e.tensor_copy(ysl, pb[tb][:, 0:128]),
                            waits=[y_busy[j % 4]] if y_busy[j % 4] else [])
                        slot0 = ee * c.CLOC + hf * c.HSLOT + s0 + j * 128
                        y_busy[j % 4] = b.dma(
                            "sync", yslots[slot0:slot0 + 128,
                                           dout * 128:(dout + 1) * 128],
                            ysl, waits=[te], ch=b.YB[j % 4])
                    s0 += wdt

    bar_m = P.barrier()
    pv = {}
    for tile in range(c.NTILE_OWN):
        tok0 = tile * 128
        cc0 = st_c[:, tile * 16:tile * 16 + 16]
        tg0 = b.gpd(lambda e, tile=tile: e.indirect_dma_start(
            out=sb_fa, out_offset=None, in_=yslots,
            in_offset=IndirectOffsetOnAxis(st_yg[:, tile * 2:tile * 2 + 1], 0)),
            waits=bar_m + [pv["f1"]] if "f1" in pv else bar_m)
        tg1 = b.gpd(lambda e, tile=tile: e.indirect_dma_start(
            out=sb_fb, out_offset=None, in_=yslots,
            in_offset=IndirectOffsetOnAxis(st_yg[:, tile * 2 + 1:tile * 2 + 2], 0)),
            waits=[pv["f2"]] if "f2" in pv else bar_m)
        ts_ = b.dma("sync", sb_fs, sbufd[tok0:tok0 + 128, :],
                    waits=[pv["f4"]] if "f4" in pv else bar_m)
        tx_ = b.dma("scalar", sb_fo, own_x[tok0:tok0 + 128, :],
                    waits=[pv["f5"]] if "f5" in pv else bar_m)
        f1 = b.vec(lambda e, cc=cc0: e.tensor_scalar_mul(sb_fy, sb_fa,
                                                         cc[:, 10:11]),
                   waits=[tg0] + ([pv["f2"], pv["f3"]] if "f2" in pv else []))
        f2 = b.vec(lambda e, cc=cc0: e.scalar_tensor_tensor(
            sb_fy, sb_fb, cc[:, 11:12], sb_fy, ALU.mult, ALU.add),
            waits=[f1, tg1])
        f3 = b.sca(lambda e: e.activation(sb_fy2, sb_fy, ACT.Silu),
                   waits=[f2] + ([pv["dma"]] if "dma" in pv else []))
        pv["f3"] = f3
        f4 = b.vec(lambda e: e.tensor_mul(sb_fy2, sb_fy2, sb_fs), waits=[f3, ts_])
        f5 = b.vec(lambda e: e.tensor_add(sb_fy2, sb_fy2, sb_fo), waits=[f4, tx_])
        pv["dma"] = b.dma("sync", out_own[tok0:tok0 + 128, :], sb_fy2, waits=[f5])
        pv["f1"], pv["f2"], pv["f4"], pv["f5"] = f1, f2, f4, f5

    fin = P.barrier()
    for eng in ("sync", "scalar", "vector", "tensor", "gpsimd"):
        P.add(eng, lambda e: e.nop(), waits=fin)

    import os
    if os.environ.get("KBUILD_STATS"):
        for cc_ in P.chans:
            print("chan", cc_.name, "final", cc_.n)
    P.emit(nc)
    return nc


def make_consts(c):
    ident = np.eye(128, dtype=np.float32)
    identb = np.eye(128).astype(ml_dtypes.bfloat16)
    triu = np.triu(np.ones((128, 128), np.float32), 1)
    trimaskT = np.where(np.arange(128)[:, None] > np.arange(128)[None, :],
                        np.float32(NEG), np.float32(0.0))
    colpack = np.zeros((128, 4), np.float32)
    colpack[:, 0] = 1.0
    colpack[:, 1] = np.arange(128)
    colpack[:, 2] = 2 * np.arange(128)
    erow = np.tile(np.arange(8, dtype=np.float32)[None, :], (128, 1))
    onesrow = np.ones((1, 128), np.float32)
    return ident, identb, triu, trimaskT, colpack, erow, onesrow


def _pk(w, shape, perm):
    return np.ascontiguousarray(w.reshape(shape).transpose(perm).reshape(128, -1))


def stage_inputs(c, inputs):
    x = np.asarray(inputs["L0_attn_x"], dtype=np.float32)
    xf = np.ascontiguousarray(x.reshape(c.T, c.D))
    xTs = np.ascontiguousarray(xf.T)
    ident, identb, triu, trimaskT, colpack, erow, onesrow = make_consts(c)
    DC, HDC = c.DC, c.HDC
    wq = _pk(np.asarray(inputs["L0_attn_Wq"], dtype=np.float32),
             (DC, 128, c.NH, HDC, 128), (1, 2, 3, 0, 4))
    wk = _pk(np.asarray(inputs["L0_attn_Wk"], dtype=np.float32),
             (DC, 128, c.NKV, HDC, 128), (1, 2, 3, 0, 4))
    wv = _pk(np.asarray(inputs["L0_attn_Wv"], dtype=np.float32),
             (DC, 128, c.NKV * c.HD), (1, 0, 2))
    wo = _pk(np.asarray(inputs["L0_attn_Wo"], dtype=np.float32),
             (DC, 128, DC, 128), (1, 2, 0, 3))
    wg = np.ascontiguousarray(inputs["L0_ffn_Wg"], dtype=np.float32)
    win = _pk(np.asarray(inputs["L0_ffn_W_in"], dtype=np.float32),
              (c.E, DC, 128, 2 * c.FC, 128), (2, 0, 3, 1, 4))
    FH = max(1, c.FC // 2)
    wout = _pk(np.asarray(inputs["L0_ffn_W_out"], dtype=np.float32),
               (c.E, c.FC // FH, FH, 128, DC, 128), (3, 0, 1, 4, 2, 5))
    nw = np.ascontiguousarray(inputs["L0_attn_norm_w"], dtype=np.float32)
    gate = np.asarray(inputs["L0_attn_gate"], dtype=np.float32).reshape(1, 1)
    lt = np.asarray(inputs["L0_attn_log_temperature"],
                    dtype=np.float32).reshape(1, 1)
    in_maps = []
    for core in range(c.NCORE):
        xr = np.roll(xTs, -core * c.TOWN, axis=1) if core else xTs
        xr = _pk(xr, (c.DC, 128, c.T // c.SLAB, c.SLAB), (1, 2, 0, 3))
        belowv = np.zeros((c.NCORE, 2), np.float32)
        for r in range(c.NCORE):
            tru = (core + r) % c.NCORE
            belowv[r, 0] = 1.0 if tru < core else 0.0
            belowv[r, 1] = 1.0
        in_maps.append({
            "xT": np.ascontiguousarray(xr),
            "own_x": np.ascontiguousarray(xf[core * c.TOWN:(core + 1) * c.TOWN]),
            "Wq": wq, "Wk": wk, "Wv": wv, "Wo": wo, "Wg": wg,
            "W_in": win, "W_out": wout, "norm_w": nw,
            "gate_s": gate, "ltemp": lt, "below": belowv,
            "ident": ident, "identb": identb, "triu": triu,
            "trimaskT": trimaskT, "colpack": colpack, "erow": erow,
            "onesrow": onesrow,
        })
    return in_maps


def kernel(**inputs):
    c = Cfg(tiny=False)
    in_maps = stage_inputs(c, inputs)
    nc = build(c, dbg=True)
    res = run_bass_kernel_spmd(nc, in_maps, core_ids=list(range(c.NCORE)))
    out = np.concatenate([r["out"] for r in res.results], axis=0)
    return out.reshape(32, 2048, 1024)


# version 11
# speedup vs baseline: 1.0311x; 1.0213x over previous
import math
import numpy as np
import ml_dtypes
import concourse.bass as bass
import concourse.mybir as mybir
from concourse.bass_utils import run_bass_kernel_spmd
from concourse.bass_core_rust import IndirectOffsetOnAxis

F32 = mybir.dt.float32
BF16 = mybir.dt.bfloat16
I32 = mybir.dt.int32
U32 = mybir.dt.uint32
ALU = mybir.AluOpType
ACT = mybir.ActivationFunctionType
AX = mybir.AxisListType

NEG = -1.0e30


class Cfg:
    def __init__(self, tiny=False, mid=False):
        if mid:
            self.NCORE, self.D, self.L = 2, 1024, 512
            self.NSEQ, self.NH, self.NKV = 1, 4, 2
            self.E, self.F, self.SLAB = 8, 256, 256
        elif tiny:
            self.NCORE, self.D, self.L = 2, 256, 256
            self.NSEQ, self.NH, self.NKV = 2, 2, 1
            self.E, self.F, self.SLAB = 8, 128, 128
        else:
            self.NCORE, self.D, self.L = 8, 1024, 2048
            self.NSEQ, self.NH, self.NKV = 4, 4, 2
            self.E, self.F, self.SLAB = 8, 2048, 512
        self.DC = self.D // 128
        self.HD = self.D // self.NH
        self.HDC = self.HD // 128
        self.NREP = self.NH // self.NKV
        self.TOWN = self.NSEQ * self.L
        self.T = self.NCORE * self.TOWN
        self.NSLAB_OWN = self.TOWN // self.SLAB
        self.NSLAB_ALL = self.T // self.SLAB
        self.NTILE_OWN = self.TOWN // 128
        self.TPS = self.SLAB // 128
        self.KT = self.L // 128
        self.NQS = self.L // self.SLAB
        self.F2 = 2 * self.F
        self.FC = self.F // 128
        self.CGLOB = int(math.ceil(2 * self.T / self.E * 1.25))
        cl = int(math.ceil(2 * self.TOWN / self.E * 1.25))
        self.CLOC = ((cl + 127) // 128) * 128
        self.TPE = self.CLOC // 128
        self.NSLOT = self.E * self.CLOC
        self.NT_IMG = (self.NSLOT + 2 * self.TOWN) // 128
        self.NHALF = 2 if (not tiny or mid) else 1
        assert self.TPE % self.NHALF == 0
        self.TPH = self.TPE // self.NHALF
        self.HSLOT = self.TPH * 128
        self.HSLABS = []
        r = self.HSLOT
        while r > 0:
            w = min(self.SLAB, r)
            self.HSLABS.append(w)
            r -= w


class Ch:
    def __init__(self, name, engine, step):
        self.name, self.engine, self.step = name, engine, step
        self.is_dma = step == 16
        self.n = 0
        self.sem = None


class Prog:
    def __init__(self):
        self.ops = []
        self.chans = []

    def chan(self, name, engine, step=1):
        ch = Ch(name, engine, step)
        self.chans.append(ch)
        return ch

    def add(self, engine, fn, waits=(), inc=None):
        w = [(cc, v) for (cc, v) in waits if cc is not None and v > 0]
        self.ops.append((engine, fn, w, inc))
        if inc is not None:
            inc.n += inc.step
            return (inc, inc.n)
        return None

    def barrier(self):
        return [(cc, cc.n) for cc in self.chans]

    def emit(self, nc):
        for cc in self.chans:
            cc.sem = nc.alloc_semaphore(name="ch_" + cc.name)
        streams = {}
        for op in self.ops:
            streams.setdefault(op[0], []).append(op)

        def make_replay(ename, myops):
            def replay(eng):
                waited = {}
                for (_, fn, waits, inc) in myops:
                    for (cc, v) in waits:
                        if waited.get(cc.name, 0) >= v:
                            continue
                        eng.wait_ge(cc.sem, v)
                        waited[cc.name] = v
                    inst = fn(eng)
                    if inc is not None:
                        inst.then_inc(inc.sem, inc.step)
            return replay

        with nc.Block() as block:
            decs = {"sync": block.sync, "scalar": block.scalar,
                    "vector": block.vector, "tensor": block.tensor,
                    "gpsimd": block.gpsimd}
            for ename, myops in streams.items():
                decs[ename](make_replay(ename, myops))


class B:
    def __init__(self, cfg):
        self.c = cfg
        self.nc = bass.Bass("TRN2", target_bir_lowering=False,
                            detect_race_conditions=False)
        self.P = Prog()
        P = self.P
        self.LD = P.chan("ld", "sync", 16)
        self.SD = P.chan("sd", "scalar", 16)
        self.GD = P.chan("gd", "gpsimd", 16)
        self.WSEM = [P.chan("w0", "scalar", 16), P.chan("w1", "scalar", 16)]
        self.XT = [P.chan("xt0", "sync", 16), P.chan("xt1", "sync", 16)]
        self.CST = P.chan("cst", "sync", 16)
        self.YB = [P.chan(f"y{i}", "sync", 16) for i in range(4)]
        self.GB = [P.chan("g0", "gpsimd", 16), P.chan("g1", "gpsimd", 16)]
        self.PE = P.chan("pe", "tensor", 1)
        self.DV = P.chan("dv", "vector", 1)
        self.SC = P.chan("sc", "scalar", 1)
        self.GP = P.chan("gp", "gpsimd", 1)
        used = self.nc.SBUF_PARTITION_SIZE_BYTES - self.nc.sbuf_bytes_remaining
        self._sb_off = (used + 31) // 32 * 32
        self.bk_w = [None] * 8
        self.bk_r = [None] * 8
        self.n_mm = 0
        self.marks = []

    def mark(self, label):
        self.marks.append((self.n_mm, label))

    def dma(self, q, out, in_, waits=(), slow=False, ch=None):
        if ch is None:
            ch = {"sync": self.LD, "scalar": self.SD, "gpsimd": self.GD}[q]
        return self.P.add(
            q, lambda e, o=out, i=in_, s=slow: e.dma_start(
                out=o, in_=i, allow_slow_non_contiguous=s),
            waits=waits, inc=ch)

    def mm(self, bank_i, out, lhsT, rhs, start, stop, waits=(), inc=True):
        w = list(waits)
        if start and self.bk_r[bank_i] is not None:
            w.append(self.bk_r[bank_i])
        self.n_mm += 1
        tok = self.P.add(
            "tensor",
            lambda e, o=out, l=lhsT, r=rhs, s1=start, s2=stop:
                e.matmul(o, l, r, start=s1, stop=s2),
            waits=w, inc=self.PE if inc else None)
        if inc:
            self.bk_w[bank_i] = tok
        return tok

    def tp(self, bank_i, out, in_, ident, waits=()):
        w = list(waits)
        if self.bk_r[bank_i] is not None:
            w.append(self.bk_r[bank_i])
        self.n_mm += 1
        tok = self.P.add(
            "tensor", lambda e, o=out, i=in_, d=ident: e.transpose(o, i, d),
            waits=w, inc=self.PE)
        self.bk_w[bank_i] = tok
        return tok

    def rd(self, bank_i, engine, fn, waits=()):
        w = list(waits)
        if self.bk_w[bank_i] is not None:
            w.append(self.bk_w[bank_i])
        ch = self.DV if engine == "vector" else self.SC
        tok = self.P.add(engine, fn, waits=w, inc=ch)
        self.bk_r[bank_i] = tok
        return tok

    def rd2(self, bank_i, bank_j, engine, fn, waits=()):
        w = list(waits)
        for bi in (bank_i, bank_j):
            if self.bk_w[bi] is not None:
                w.append(self.bk_w[bi])
        ch = self.DV if engine == "vector" else self.SC
        tok = self.P.add(engine, fn, waits=w, inc=ch)
        self.bk_r[bank_i] = tok
        self.bk_r[bank_j] = tok
        return tok

    def vec(self, fn, waits=()):
        return self.P.add("vector", fn, waits=waits, inc=self.DV)

    def sca(self, fn, waits=()):
        return self.P.add("scalar", fn, waits=waits, inc=self.SC)

    def gpd(self, fn, waits=(), ch=None):
        return self.P.add("gpsimd", fn, waits=waits, inc=ch or self.GD)

    def gpc(self, fn, waits=()):
        return self.P.add("gpsimd", fn, waits=waits, inc=self.GP)

    def sb(self, name, shape, dtype, offset=None):
        nbytes = int(np.prod(shape[1:])) * mybir.dt.size(dtype)
        if offset is None:
            offset = self._sb_off
            self._sb_off = (offset + nbytes + 31) // 32 * 32
        return self.nc.alloc_sbuf_tensor_at(name, shape, dtype, offset=offset).ap()


def build(cfg: Cfg, dbg=False, upto='F'):
    b = B(cfg)
    c = cfg
    nc, P = b.nc, b.P
    PE, DV, SC = b.PE, b.DV, b.SC

    DT = nc.dram_tensor
    xT = DT("xT", [128, c.DC * c.T], F32, kind="ExternalInput").ap()
    own_x = DT("own_x", [c.TOWN, c.D], F32, kind="ExternalInput").ap()
    Wq = DT("Wq", [128, c.NH * c.HDC * c.DC * 128], F32, kind="ExternalInput").ap()
    Wk = DT("Wk", [128, c.NKV * c.HDC * c.DC * 128], F32, kind="ExternalInput").ap()
    Wv = DT("Wv", [128, c.DC * c.NKV * c.HD], F32, kind="ExternalInput").ap()
    Wo = DT("Wo", [128, c.DC * c.DC * 128], F32, kind="ExternalInput").ap()
    Wg = DT("Wg", [c.D, c.E], F32, kind="ExternalInput").ap()
    W_in = DT("W_in", [128, c.E * 2 * c.FC * c.DC * 128], F32,
              kind="ExternalInput").ap()
    W_out = DT("W_out", [128, c.E * c.FC * c.DC * 128], F32,
               kind="ExternalInput").ap()
    norm_w = DT("norm_w", [c.D], F32, kind="ExternalInput").ap()
    gate_s = DT("gate_s", [1, 1], F32, kind="ExternalInput").ap()
    ltemp = DT("ltemp", [1, 1], F32, kind="ExternalInput").ap()
    below = DT("below", [c.NCORE, 2], F32, kind="ExternalInput").ap()
    ident = DT("ident", [128, 128], F32, kind="ExternalInput").ap()
    identb = DT("identb", [128, 128], BF16, kind="ExternalInput").ap()
    triu = DT("triu", [128, 128], F32, kind="ExternalInput").ap()
    trimaskT = DT("trimaskT", [128, 128], F32, kind="ExternalInput").ap()
    colpack = DT("colpack", [128, 4], F32, kind="ExternalInput").ap()
    erow = DT("erow", [128, 8], F32, kind="ExternalInput").ap()
    onesrow = DT("onesrow", [1, 128], F32, kind="ExternalInput").ap()

    out_own = DT("out", [c.TOWN, c.D], F32, kind="ExternalOutput").ap()
    if dbg is True:
        dbg = {"xnbuf", "sbufd", "aobuf", "yslots", "slotimg"}
    elif not dbg:
        dbg = set()
    def kd(n):
        return "ExternalOutput" if n in dbg else "Internal"
    xnbuf = DT("xnbuf", [c.TOWN + 1, 2 * c.D], BF16, kind=kd("xnbuf")).ap()
    sbufd = DT("sbufd", [c.TOWN, c.D], F32, kind=kd("sbufd")).ap()
    aobuf = DT("aobuf", [c.NSEQ * 2 * c.DC * 128, c.L], BF16, kind=kd("aobuf")).ap()
    yslots = DT("yslots", [c.NSLOT + 1, c.D], F32, kind=kd("yslots")).ap()
    slotimg = DT("slotimg", [128, c.NT_IMG], U32, kind=kd("slotimg")).ap()
    if "mdump" in dbg:
        NTIL = c.E * c.NHALF
        xgdump = DT("xgdump", [NTIL * 128, c.DC * c.HSLOT], BF16,
                    kind="ExternalOutput").ap()
        actdump = DT("actdump", [NTIL * 128, 2 * c.FC * c.HSLOT], BF16,
                     kind="ExternalOutput").ap()
        wbdump = DT("wbdump", [NTIL * 128, 2048], BF16,
                    kind="ExternalOutput").ap()

    sb_ident = b.sb("c_id", [128, 128], F32)
    sb_identb = b.sb("c_idb", [128, 128], BF16)
    sb_triu = b.sb("c_triu", [128, 128], F32)
    sb_trimT = b.sb("c_trimT", [128, 128], F32)
    sb_col = b.sb("c_col", [128, 4], F32)
    ones_col, p_col, p2_col = sb_col[:, 0:1], sb_col[:, 1:2], sb_col[:, 2:3]
    sb_erow = b.sb("c_erow", [128, 8], F32)
    sb_onesrow = b.sb("c_onesrow", [1, 128], F32)
    sb_below = b.sb("c_below", [c.NCORE, 2], F32)
    sb_nw = b.sb("c_nw", [128, c.DC], F32)
    sb_scal = b.sb("c_scal", [1, 8], F32)
    sb_gcol = b.sb("c_gcol", [128, 2], F32)
    sb_wg = b.sb("c_wg", [128, c.DC * 9], F32)
    sb_csum = b.sb("c_csum", [1, 16], F32)

    NT_ = c.NTILE_OWN
    st_oh = b.sb("st_oh", [128, NT_ * 16], F32)
    st_c = b.sb("st_c", [128, NT_ * 16], F32)
    st_yg = b.sb("st_yg", [128, NT_ * 2], U32)
    sb_cnt8 = b.sb("st_cnt8", [c.NCORE, 16], F32)
    sb_offs = b.sb("st_offs", [1, 16], F32)
    sb_rows = b.sb("st_rows", [1, 64], F32)
    sb_prefb = b.sb("st_prefb", [128, 24], F32)
    sb_slotcols = b.sb("st_slotc", [128, c.NT_IMG], U32)
    sb_u32 = b.sb("st_u32", [128, 4], U32)
    sb_i32 = b.sb("st_i32", [128, 4], I32)
    sb_cc = b.sb("st_cc", [128, 16], F32)
    sb_v8 = b.sb("st_v8", [128, 64], F32)


    sb_fx = b.sb("sc_fx", [128, c.SLAB], F32)
    sb_fx2 = b.sb("sc_fx2", [128, c.SLAB], F32)
    sb_fx3 = b.sb("sc_fx3", [128, c.SLAB], F32)
    sb_fx4 = b.sb("sc_fx4", [128, c.SLAB], F32)
    sb_xn2 = b.sb("sc_xn2", [128, 2 * c.SLAB], BF16)
    sb_wf = b.sb("sc_wf", [128, 2 * 1024], F32)
    sb_wb = b.sb("sc_wb", [128, 2 * 2048], BF16)
    sb_r1 = b.sb("sc_r1", [1, c.SLAB], F32)
    sb_r2 = b.sb("sc_r2", [1, c.SLAB], F32)
    sb_r3 = b.sb("sc_r3", [1, c.SLAB], F32)
    sb_rr = sb_r3
    sb_pt = [b.sb(f"sc_pt{i}", [128, c.SLAB], BF16) for i in range(4)]
    sb_qn = [b.sb(f"sc_qn{i}", [128, c.HDC * c.SLAB], BF16) for i in range(2)]
    sb_onesb = b.sb("sc_onesb", [128, 1], BF16)
    sb_ab = b.sb("sc_ab", [128, 2 * c.SLAB], BF16)

    u0 = b._sb_off
    sb_sq = b.sb("u_sq", [128, c.DC * c.SLAB], F32, offset=u0)
    sb_tm = b.sb("u_tm", [128, c.TPS * 2 * c.D], BF16, offset=u0)
    sb_st = b.sb("u_st", [128, c.TPS * c.D], F32, offset=u0)
    ao_off = u0 + c.TPS * c.D * 4
    sb_aoin = [b.sb("u_ao0", [128, c.DC * c.SLAB], BF16, offset=ao_off),
               b.sb("u_ao1", [128, c.DC * c.SLAB], BF16,
                    offset=ao_off + c.DC * c.SLAB * 2)]
    usz = max(c.TPS * 2 * c.D * 2, c.TPS * c.D * 4 + 2 * c.DC * c.SLAB * 2,
              c.DC * c.SLAB * 4, 2 * c.KT * c.NKV * c.HD * 2,
              2 * 2 * c.D * 2 + 512 * 4)
    b._sb_off = (u0 + usz + 31) // 32 * 32

    base0 = b._sb_off
    LW = c.L
    sb_xts = [b.sb("sc_xts0", [128, c.DC * c.SLAB], F32, offset=base0),
              b.sb("sc_xts1", [128, c.DC * c.SLAB], F32,
                   offset=base0 + c.DC * c.SLAB * 4)]
    base = base0 + c.DC * c.SLAB * 4
    sb_h = [b.sb("rH_hi", [128, c.DC * LW], BF16, offset=base),
            b.sb("rH_lo", [128, c.DC * LW], BF16, offset=base + c.DC * LW * 2)]
    rK = base + 2 * (c.DC * LW * 2)
    sb_k = [b.sb("rK_hi", [128, c.NKV * c.HDC * LW], BF16, offset=rK),
            b.sb("rK_lo", [128, c.NKV * c.HDC * LW], BF16,
                 offset=rK + c.NKV * c.HDC * LW * 2)]
    sb_vv = [b.sb("rV_hi", [128, c.KT * c.NKV * c.HD], BF16, offset=u0),
             b.sb("rV_lo", [128, c.KT * c.NKV * c.HD], BF16,
                  offset=u0 + c.KT * c.NKV * c.HD * 2)]
    rEnd = rK + 2 * (c.NKV * c.HDC * LW * 2)

    mo = base0
    sb_xg = [b.sb("mo_xg_hi", [128, c.DC * c.HSLOT], BF16, offset=mo),
             b.sb("mo_xg_lo", [128, c.DC * c.HSLOT], BF16,
                  offset=mo + c.DC * c.HSLOT * 2)]
    mo2 = mo + 2 * c.DC * c.HSLOT * 2
    sb_act = [b.sb("mo_act_hi", [128, c.FC * c.HSLOT], BF16, offset=mo2),
              b.sb("mo_act_lo", [128, c.FC * c.HSLOT], BF16,
                   offset=mo2 + c.FC * c.HSLOT * 2)]
    mo3 = mo2 + 2 * c.FC * c.HSLOT * 2
    sb_gt = b.sb("mo_gt", [128, 2 * 2 * c.D], BF16, offset=u0)
    sb_y = b.sb("mo_y", [128, 512], F32, offset=u0 + 2 * 2 * c.D * 2)
    moEnd = mo3

    fo = base0
    sb_fa = b.sb("fi_a", [128, c.D], F32, offset=fo)
    sb_fb = b.sb("fi_b", [128, c.D], F32, offset=fo + c.D * 4)
    sb_fs = b.sb("fi_s", [128, c.D], F32, offset=fo + 2 * c.D * 4)
    sb_fo = b.sb("fi_o", [128, c.D], F32, offset=fo + 3 * c.D * 4)
    sb_fy = b.sb("fi_y", [128, c.D], F32, offset=fo + 4 * c.D * 4)
    sb_fy2 = b.sb("fi_y2", [128, c.D], F32, offset=fo + 5 * c.D * 4)

    peak = max(rEnd, moEnd, fo + 6 * c.D * 4)
    assert peak <= 229376, f"SBUF over budget: {peak}"

    pb = [nc.alloc_psum_tensor(f"pb{i}", [128, 512], F32).ap() for i in range(6)]
    pbbs = [nc.alloc_psum_tensor(f"pbb{i}", [128, 128], BF16).ap()
            for i in range(2)]
    pbb_r = [None, None]
    pbb_sel = [0]

    def bf16_tp(dst_fn, src, deps):
        bk = pbb_sel[0]
        pbb_sel[0] ^= 1
        w = list(deps)
        if pbb_r[bk] is not None:
            w.append(pbb_r[bk])
        b.n_mm += 1
        th = P.add("tensor",
                   lambda e, ps=pbbs[bk], sr=src: e.transpose(ps, sr, sb_identb),
                   waits=w, inc=PE)
        te = P.add("vector", lambda e, ps=pbbs[bk]: dst_fn(e, ps), waits=[th],
                   inc=DV)
        pbb_r[bk] = te
        return te

    wrot = {"i": 0, "busy": [None, None]}

    def w_unit(src_ap, wcols, waits, q="scalar"):
        i = wrot["i"]
        wrot["i"] = (i + 1) % 2
        fslot = sb_wf[:, i * 1024:i * 1024 + wcols]
        h = sb_wb[:, i * 2048:i * 2048 + wcols]
        l = sb_wb[:, i * 2048 + 1024:i * 2048 + 1024 + wcols]
        w = list(waits)
        if wrot["busy"][i] is not None:
            w.append(wrot["busy"][i])
        t0 = b.dma(q, fslot, src_ap, waits=w, ch=b.WSEM[i])
        t1 = b.vec(lambda e, h=h, f=fslot: e.tensor_copy(h, f), waits=[t0])
        t2 = b.vec(lambda e, h=h, l=l, f=fslot: e.tensor_sub(l, f, h), waits=[t1])
        return t2, h, l, i

    def w_used(i, tok):
        wrot["busy"][i] = tok

    CS = b.CST
    cl = [b.dma("sync", sb_ident, ident, ch=CS),
          b.dma("sync", sb_identb, identb, ch=CS),
          b.dma("sync", sb_triu, triu, ch=CS),
          b.dma("sync", sb_trimT, trimaskT, ch=CS),
          b.dma("sync", sb_col, colpack, ch=CS),
          b.dma("sync", sb_erow, erow, ch=CS),
          b.dma("sync", sb_onesrow, onesrow, ch=CS),
          b.dma("sync", sb_below, below, ch=CS),
          b.dma("sync", sb_nw, norm_w.rearrange("(a p) -> p a", p=128),
                slow=True, ch=CS),
          b.dma("sync", sb_scal[0:1, 0:1], gate_s, ch=CS),
          b.dma("sync", sb_scal[0:1, 1:2], ltemp, ch=CS)]
    for dc in range(c.DC):
        cl.append(b.dma("sync", sb_wg[:, dc * 9:dc * 9 + 8],
                        Wg[dc * 128:(dc + 1) * 128, :], ch=CS))
    t_ts = b.sca(lambda e: e.activation(sb_scal[0:1, 2:3], sb_scal[0:1, 1:2],
                                        ACT.Exp), waits=cl)
    b.mm(1, pb[1][:, 0:1], sb_onesrow, sb_scal[0:1, 0:1], True, True, waits=cl)
    b.mm(1, pb[1][:, 1:2], sb_onesrow, sb_scal[0:1, 2:3], True, True, waits=[t_ts])
    t_gce = b.rd(1, "vector", lambda e: e.tensor_copy(sb_gcol, pb[1][:, 0:2]))

    wgo = [b.vec(lambda e, dc=dc: e.memset(sb_wg[:, dc * 9 + 8:dc * 9 + 9], 1.0),
                 waits=cl) for dc in range(c.DC)]
    for dc in range(c.DC):
        b.mm(2, pb[2][0:1, 0:8], ones_col, sb_wg[:, dc * 9:dc * 9 + 8],
             dc == 0, dc == c.DC - 1, waits=cl, inc=(dc == c.DC - 1))
    t_cs = b.rd(2, "vector", lambda e: e.tensor_scalar_mul(
        sb_csum[0:1, 0:8], pb[2][0:1, 0:8], -1.0 / c.D))
    b.mm(3, pb[3][:, 0:8], sb_onesrow, sb_csum[0:1, 0:8], True, True, waits=[t_cs])
    wg_fix = []
    for dc in range(c.DC):
        wg_fix.append(b.rd(3, "vector", lambda e, dc=dc: e.tensor_add(
            sb_wg[:, dc * 9:dc * 9 + 8], sb_wg[:, dc * 9:dc * 9 + 8], pb[3][:, 0:8])))

    t_m1 = b.vec(lambda e: e.memset(sb_slotcols, c.TOWN))
    t_si = b.dma("sync", slotimg, sb_slotcols, waits=[t_m1], ch=CS)
    t_m2 = b.vec(lambda e: e.memset(sb_tm[:, 0:2 * c.D], 0.0))
    t_zx = b.dma("sync", xnbuf[c.TOWN:c.TOWN + 1, :], sb_tm[0:1, 0:2 * c.D],
                 waits=[t_m2], ch=CS)
    t_m3 = b.vec(lambda e: e.memset(sb_fx[:, 0:c.SLAB], 0.0))
    for j in range(0, c.D, c.SLAB):
        w = min(c.SLAB, c.D - j)
        b.dma("sync", yslots[c.NSLOT:c.NSLOT + 1, j:j + w], sb_fx[0:1, 0:w],
              waits=[t_m3], ch=CS)
    tok_offs = b.vec(lambda e: e.memset(sb_offs, 0.0))
    t_ob = b.vec(lambda e: e.memset(sb_onesb, 1.0))
    cl.append(t_ob)

    b.mark("R")
    xts_free = [None, None]
    tile_done = {}
    rr_readers = []
    sqmm_prev = []
    cnt_prev = []
    v6_prev, v10_prev = [], []
    for s in range(c.NSLAB_ALL):
        own = s < c.NSLAB_OWN
        buf = s % 2
        w = ([xts_free[buf]] if xts_free[buf] else []) + (cl if s < 2 else [])
        t_x = b.dma("sync", sb_xts[buf],
                    xT[:, s * c.DC * c.SLAB:(s + 1) * c.DC * c.SLAB],
                    waits=w, ch=b.XT[buf])
        tok_sq = None
        if own:
            t_sqa = b.sca(lambda e, bu=buf: e.activation(sb_sq, sb_xts[bu],
                                                         ACT.Square),
                          waits=[t_x] + ([t_zx, t_m2] if s == 0 else [])
                          + sqmm_prev)
            for dc in range(c.DC):
                b.mm(3, pb[3][0:1, 0:c.SLAB], ones_col,
                     sb_sq[:, dc * c.SLAB:(dc + 1) * c.SLAB], dc == 0,
                     dc == c.DC - 1, waits=[t_sqa], inc=(dc == c.DC - 1))
            sqmm_prev = [(PE, PE.n)]
            tok_sq = b.rd(3, "vector", lambda e: e.tensor_copy(
                sb_rr, pb[3][0:1, 0:c.SLAB]), waits=rr_readers)
            rr_readers = []
        rng = s // c.NSLAB_OWN
        last_tok = t_x
        for tt in range(c.TPS):
            tile = s * c.TPS + tt
            ncols = 9 if own else 8
            mm_w = [t_x] + (wg_fix + wgo if s == 0 else [])
            for dc in range(c.DC):
                b.mm(2, pb[2][:, 0:ncols],
                     sb_xts[buf][:, dc * c.SLAB + tt * 128:dc * c.SLAB + tt * 128 + 128],
                     sb_wg[:, dc * 9:dc * 9 + ncols], dc == 0, dc == c.DC - 1,
                     waits=mm_w, inc=(dc == c.DC - 1))
            u8 = pb[2][:, 0:8]
            ohsl = st_oh[:, tile * 16:tile * 16 + 16] if own else sb_v8[:, 32:48]
            t1 = b.rd(2, "vector", lambda e: e.reduce_max(sb_cc[:, 0:1], u8, AX.X,
                                                          negate=True))
            t2 = b.rd(2, "vector", lambda e, o=ohsl: e.tensor_scalar(
                o[:, 0:8], u8, sb_cc[:, 0:1], 0.0, ALU.add, ALU.is_equal),
                waits=[t1] + ([] if own else cnt_prev))
            t3 = b.rd(2, "vector", lambda e, o=ohsl: e.scalar_tensor_tensor(
                sb_v8[:, 0:8], o[:, 0:8], NEG, u8, ALU.mult, ALU.add), waits=[t2])
            t4 = b.vec(lambda e: e.reduce_max(sb_cc[:, 1:2], sb_v8[:, 0:8], AX.X,
                                              negate=True), waits=[t3])
            t5 = b.vec(lambda e, o=ohsl: e.tensor_scalar(
                o[:, 8:16], sb_v8[:, 0:8], sb_cc[:, 1:2], 0.0, ALU.add,
                ALU.is_equal), waits=[t4])
            start_cnt = (s % c.NSLAB_OWN) == 0 and tt == 0
            stop_cnt = (s % c.NSLAB_OWN) == c.NSLAB_OWN - 1 and tt == c.TPS - 1
            b.mm(4, pb[4][0:1, 0:16], ones_col, ohsl, start_cnt, stop_cnt,
                 waits=[t5])
            last_tok = (PE, PE.n)
            cnt_prev = [last_tok]
            if stop_cnt:
                tcv = b.rd(4, "vector", lambda e: e.tensor_copy(
                    sb_rows[0:1, 0:16], pb[4][0:1, 0:16]))
                b.dma("sync", sb_cnt8[rng:rng + 1, :], sb_rows[0:1, 0:16],
                      waits=[tcv])
                last_tok = tcv
            if own:
                cc0 = st_c[:, tile * 16:tile * 16 + 16]
                ta = b.vec(lambda e, o=ohsl: e.tensor_mul(sb_v8[:, 8:16], o[:, 0:8],
                                                          sb_erow), waits=[t5])
                b.vec(lambda e, cc=cc0: e.reduce_sum(cc[:, 3:4], sb_v8[:, 8:16],
                                                     AX.X), waits=[ta])
                tb_ = b.vec(lambda e, o=ohsl: e.tensor_mul(sb_v8[:, 8:16], o[:, 8:16],
                                                           sb_erow), waits=[ta])
                b.vec(lambda e, cc=cc0: e.reduce_sum(cc[:, 4:5], sb_v8[:, 8:16],
                                                     AX.X), waits=[tb_])
                tmc = b.rd(2, "vector", lambda e, cc=cc0: e.tensor_scalar_mul(
                    cc[:, 7:8], pb[2][:, 8:9], 1.0 / c.D))
                ttp = b.tp(5, pb[5][:, 0:1], sb_rr[0:1, tt * 128:tt * 128 + 128],
                           sb_ident[0:1, 0:1], waits=[tok_sq])
                rr_readers.append(ttp)
                v1 = b.rd(5, "vector", lambda e: e.tensor_scalar_mul(
                    sb_cc[:, 2:3], pb[5][:, 0:1], 1.0 / c.D))
                v2 = b.vec(lambda e, cc=cc0: e.tensor_mul(sb_cc[:, 3:4], cc[:, 7:8],
                                                          cc[:, 7:8]), waits=[tmc])
                v3 = b.vec(lambda e: e.tensor_sub(sb_cc[:, 4:5], sb_cc[:, 2:3],
                                                  sb_cc[:, 3:4]), waits=[v1, v2])
                v4 = b.vec(lambda e: e.tensor_scalar_add(sb_cc[:, 5:6],
                                                         sb_cc[:, 4:5], 1e-5),
                           waits=[v3])
                v5 = b.vec(lambda e: e.reciprocal(sb_cc[:, 6:7], sb_cc[:, 5:6]),
                           waits=[v4] + v6_prev)
                v6 = b.sca(lambda e, cc=cc0: e.activation(cc[:, 8:9], sb_cc[:, 6:7],
                                                          ACT.Sqrt), waits=[v5])
                v6_prev = [v6]
                v7 = b.vec(lambda e: e.tensor_mul(sb_cc[:, 7:8], sb_cc[:, 4:5],
                                                  sb_cc[:, 6:7]), waits=[v5])
                v8_ = b.vec(lambda e: e.tensor_scalar_add(sb_cc[:, 8:9],
                                                          sb_cc[:, 7:8], 1e-6),
                            waits=[v7])
                v9 = b.vec(lambda e: e.reciprocal(sb_cc[:, 9:10], sb_cc[:, 8:9]),
                           waits=[v8_] + v10_prev)
                v10 = b.sca(lambda e, cc=cc0: e.activation(
                    cc[:, 9:10], sb_cc[:, 9:10], ACT.Sqrt), waits=[v9])
                v10_prev = [v10]
                tl = b.rd(2, "vector", lambda e, cc=cc0: e.tensor_scalar_mul(
                    sb_v8[:, 16:24], u8, cc[:, 8:9]), waits=[v6, t5])
                tl2 = b.vec(lambda e: e.reduce_max(sb_cc[:, 10:11], sb_v8[:, 16:24],
                                                   AX.X, negate=True), waits=[tl])
                tl3 = b.sca(lambda e, cc=cc0: e.activation(
                    sb_v8[:, 24:32], sb_v8[:, 16:24], ACT.Exp, bias=sb_cc[:, 10:11],
                    accum_out=cc[:, 2:3]), waits=[tl2])
                tl4 = b.vec(lambda e, o=ohsl: e.tensor_mul(
                    sb_v8[:, 16:24], o[:, 0:8], sb_v8[:, 24:32]), waits=[tl3])
                b.vec(lambda e, cc=cc0: e.reduce_sum(cc[:, 0:1], sb_v8[:, 16:24],
                                                     AX.X), waits=[tl4])
                tl5 = b.vec(lambda e, o=ohsl: e.tensor_mul(
                    sb_v8[:, 16:24], o[:, 8:16], sb_v8[:, 24:32]), waits=[tl4])
                b.vec(lambda e, cc=cc0: e.reduce_sum(cc[:, 1:2], sb_v8[:, 16:24],
                                                     AX.X), waits=[tl5])
                b.mm(0, pb[0][:, 0:16], sb_triu, ohsl, True, False,
                     waits=[t5, tok_offs], inc=False)
                b.mm(0, pb[0][:, 0:16], sb_onesrow, sb_offs, False, True,
                     waits=[tok_offs])
                b.mm(1, pb[1][0:1, 0:16], ones_col, ohsl, True, True, waits=[t5])
                tok_offs = b.rd(1, "vector", lambda e: e.tensor_add(
                    sb_offs, sb_offs, pb[1][0:1, 0:16]))
                w1 = b.rd(0, "vector", lambda e, o=ohsl: e.tensor_mul(
                    sb_v8[:, 16:24], o[:, 0:8], pb[0][:, 0:8]))
                b.vec(lambda e, cc=cc0: e.reduce_sum(cc[:, 5:6], sb_v8[:, 16:24],
                                                     AX.X), waits=[w1])
                w2 = b.rd(0, "vector", lambda e, o=ohsl: e.tensor_mul(
                    sb_v8[:, 16:24], o[:, 8:16], pb[0][:, 8:16]), waits=[w1])
                tloc = b.vec(lambda e, cc=cc0: e.reduce_sum(
                    cc[:, 6:7], sb_v8[:, 16:24], AX.X), waits=[w2])
                tile_done[tile] = tloc
                last_tok = tloc
        xts_free[buf] = last_tok

    bar_r = P.barrier()
    b.mm(0, pb[0][0:1, 0:16], sb_below[:, 0:1], sb_cnt8, True, True, waits=bar_r)
    b.mm(1, pb[1][0:1, 0:16], sb_below[:, 1:2], sb_cnt8, True, True, waits=bar_r)
    tpv = b.rd(0, "vector", lambda e: e.tensor_copy(sb_rows[0:1, 0:16],
                                                    pb[0][0:1, 0:16]))
    ttv = b.rd(1, "vector", lambda e: e.tensor_copy(sb_rows[0:1, 16:32],
                                                    pb[1][0:1, 0:16]))
    tpg = b.vec(lambda e: e.tensor_add(sb_rows[0:1, 8:16], sb_rows[0:1, 8:16],
                                       sb_rows[0:1, 16:24]), waits=[tpv, ttv])
    b.mm(2, pb[2][:, 0:16], sb_onesrow, sb_rows[0:1, 0:16], True, True,
         waits=[tpg])
    tpbe = b.rd(2, "vector", lambda e: e.tensor_copy(sb_prefb[:, 0:16],
                                                     pb[2][:, 0:16]))
    b.mm(3, pb[3][:, 0:8], sb_onesrow, sb_cnt8[0:1, 0:8], True, True,
         waits=bar_r)
    tpc0 = b.rd(3, "vector", lambda e: e.tensor_copy(sb_prefb[:, 16:24],
                                                     pb[3][:, 0:8]))

    scat = []
    for tile in range(c.NTILE_OWN):
        cc0 = st_c[:, tile * 16:tile * 16 + 16]
        ohsl = st_oh[:, tile * 16:tile * 16 + 16]
        dep = [tpbe, tile_done[tile]]
        for kk in range(2):
            oh = ohsl[:, 8 * kk:8 * kk + 8]
            locw = cc0[:, 5 + kk:6 + kk]
            ek = cc0[:, 3 + kk:4 + kk]
            a1 = b.vec(lambda e, oh=oh, kk=kk: e.tensor_mul(
                sb_v8[:, 0:8], oh, sb_prefb[:, 8 * kk:8 * kk + 8]), waits=dep)
            a2 = b.vec(lambda e: e.reduce_sum(sb_cc[:, 0:1], sb_v8[:, 0:8], AX.X),
                       waits=[a1])
            a3 = b.vec(lambda e, locw=locw: e.tensor_add(sb_cc[:, 1:2], locw,
                                                         sb_cc[:, 0:1]), waits=[a2])
            a4 = b.vec(lambda e: e.tensor_scalar(sb_cc[:, 2:3], sb_cc[:, 1:2],
                                                 float(c.CGLOB), None, ALU.is_lt),
                       waits=[a3])
            if kk == 0:
                aL = b.vec(lambda e, locw=locw: e.tensor_copy(sb_cc[:, 15:16],
                                                              locw), waits=[a3])
            else:
                aL1 = b.vec(lambda e, oh=oh: e.tensor_mul(
                    sb_v8[:, 8:16], oh, sb_prefb[:, 16:24]), waits=[a3, tpc0])
                aL2 = b.vec(lambda e: e.reduce_sum(sb_cc[:, 14:15],
                                                   sb_v8[:, 8:16], AX.X),
                            waits=[aL1])
                aL = b.vec(lambda e, locw=locw: e.tensor_add(
                    sb_cc[:, 15:16], locw, sb_cc[:, 14:15]), waits=[aL2])
            a5 = b.vec(lambda e: e.tensor_scalar(
                sb_cc[:, 3:4], sb_cc[:, 15:16], float(c.CLOC), None, ALU.is_lt),
                waits=[aL])
            a6 = b.vec(lambda e: e.tensor_mul(sb_cc[:, 4:5], sb_cc[:, 2:3],
                                              sb_cc[:, 3:4]), waits=[a4, a5])
            b.vec(lambda e, cc=cc0, kk=kk: e.tensor_mul(
                cc[:, 12 + kk:13 + kk], cc[:, kk:kk + 1], sb_cc[:, 2:3]), waits=[a4])
            a8 = b.vec(lambda e, ek=ek: e.scalar_tensor_tensor(
                sb_cc[:, 5:6], ek, float(c.CLOC), sb_cc[:, 15:16], ALU.mult,
                ALU.add), waits=[a6])
            a9 = b.vec(lambda e, tile=tile, kk=kk: e.tensor_scalar_add(
                sb_cc[:, 6:7], p2_col, float(c.NSLOT + 256 * tile + kk)), waits=[a6])
            a6i = b.gpc(lambda e: e.tensor_copy(sb_i32[:, 3:4], sb_cc[:, 4:5]),
                        waits=[a6])
            a10 = b.vec(lambda e: e.select(sb_cc[:, 7:8], sb_i32[:, 3:4],
                                           sb_cc[:, 5:6], sb_cc[:, 6:7]),
                        waits=[a8, a9, a6i])
            a11 = b.vec(lambda e: e.memset(sb_cc[:, 8:9], float(c.NSLOT)),
                        waits=[a10])
            a12 = b.vec(lambda e: e.select(sb_cc[:, 9:10], sb_i32[:, 3:4],
                                           sb_cc[:, 5:6], sb_cc[:, 8:9]),
                        waits=[a11])
            b.gpc(lambda e, kk=kk, tile=tile: e.tensor_copy(
                st_yg[:, tile * 2 + kk:tile * 2 + kk + 1], sb_cc[:, 9:10]),
                waits=[a12])
            MAGIC = 12582912.0
            m1 = b.vec(lambda e: e.tensor_scalar_mul(sb_v8[:, 24:25],
                                                     sb_cc[:, 7:8], 1.0 / 128.0),
                       waits=[a10])
            m2 = b.vec(lambda e: e.tensor_scalar_add(sb_v8[:, 25:26],
                                                     sb_v8[:, 24:25], -0.49995),
                       waits=[m1])
            m3 = b.vec(lambda e: e.tensor_scalar_add(sb_v8[:, 26:27],
                                                     sb_v8[:, 25:26], MAGIC),
                       waits=[m2])
            m4 = b.vec(lambda e: e.tensor_scalar_add(sb_v8[:, 27:28],
                                                     sb_v8[:, 26:27], -MAGIC),
                       waits=[m3])
            m5 = b.vec(lambda e: e.tensor_scalar_mul(sb_v8[:, 28:29],
                                                     sb_cc[:, 7:8],
                                                     float(c.NT_IMG)),
                       waits=[a10])
            m6 = b.vec(lambda e: e.scalar_tensor_tensor(
                sb_v8[:, 29:30], sb_v8[:, 27:28], -(128.0 * c.NT_IMG - 1.0),
                sb_v8[:, 28:29], ALU.mult, ALU.add), waits=[m4, m5])
            m7 = b.gpc(lambda e: e.tensor_copy(sb_u32[:, 0:1], sb_v8[:, 29:30]),
                       waits=[m6] + scat[-1:])
            b5 = b.vec(lambda e, tile=tile: e.tensor_scalar_add(
                sb_cc[:, 10:11], p_col, float(128 * tile)), waits=[a10])
            b6 = b.gpc(lambda e: e.tensor_copy(sb_u32[:, 1:2], sb_cc[:, 10:11]),
                       waits=[b5] + scat[-1:])
            sc_t = b.gpd(lambda e: e.indirect_dma_start(
                out=slotimg.rearrange("p n -> (p n)").unsqueeze(1),
                out_offset=IndirectOffsetOnAxis(sb_u32[:, 0:1], 0),
                in_=sb_u32[:, 1:2], in_offset=None), waits=[m7, b6, t_si])
            scat.append(sc_t)
        g1 = b.vec(lambda e, cc=cc0: e.tensor_scalar_mul(sb_cc[:, 11:12],
                                                         cc[:, 2:3], 1e-9),
                   waits=[scat[-1]])
        g2 = b.vec(lambda e, cc=cc0: e.tensor_add(sb_cc[:, 12:13], cc[:, 12:13],
                                                  cc[:, 13:14]), waits=[scat[-1]])
        g3 = b.vec(lambda e: e.tensor_max(sb_cc[:, 13:14], sb_cc[:, 12:13],
                                          sb_cc[:, 11:12]), waits=[g1, g2])
        g4 = b.vec(lambda e: e.reciprocal(sb_cc[:, 14:15], sb_cc[:, 13:14]),
                   waits=[g3])
        b.vec(lambda e, cc=cc0: e.tensor_mul(cc[:, 10:11], cc[:, 12:13],
                                             sb_cc[:, 14:15]), waits=[g4])
        b.vec(lambda e, cc=cc0: e.tensor_mul(cc[:, 11:12], cc[:, 13:14],
                                             sb_cc[:, 14:15]), waits=[g4])

    b.mark("postR")
    bar_scat = P.barrier()
    t_slot = b.dma("sync", sb_slotcols, slotimg, waits=bar_scat, ch=CS)

    if upto == "R":
        sdump = DT("sdump", [128, NT_ * 48 + c.NT_IMG + 64], F32,
                   kind="ExternalOutput").ap()
        bar_d = P.barrier()
        b.dma("sync", sdump[:, 0:NT_ * 16], st_c, waits=bar_d)
        b.dma("sync", sdump[:, NT_ * 16:NT_ * 32], st_oh, waits=bar_d)
        b.dma("sync", sdump[:, NT_ * 32:NT_ * 34], st_yg.bitcast(F32), waits=bar_d)
        b.dma("sync", sdump[:, NT_ * 34:NT_ * 34 + c.NT_IMG],
              sb_slotcols.bitcast(F32), waits=[t_slot] + bar_d)
        b.dma("sync", sdump[:, NT_ * 34 + c.NT_IMG:NT_ * 34 + c.NT_IMG + 16],
              sb_prefb[:, 0:16], waits=bar_d)
        fin = P.barrier()
        for eng in ("sync", "scalar", "vector", "tensor", "gpsimd"):
            P.add(eng, lambda e: e.nop(), waits=fin)
        P.emit(nc)
        return nc

    def build_rows(stile, want_r3, waits):
        toks = []
        prev = list(waits)
        for j in range(c.TPS):
            cc0 = st_c[:, (stile + j) * 16:(stile + j) * 16 + 16]
            tmu = b.vec(lambda e, cc=cc0: e.tensor_mul(sb_cc[:, 0:1], cc[:, 7:8],
                                                       cc[:, 8:9]), waits=prev)
            b.tp(5, pb[5][0:1, 0:128], cc0[:, 8:9], sb_ident, waits=prev)
            te1 = b.rd(5, "vector", lambda e, j=j: e.tensor_copy(
                sb_r1[0:1, j * 128:(j + 1) * 128], pb[5][0:1, 0:128]))
            b.tp(5, pb[5][0:1, 0:128], sb_cc[:, 0:1], sb_ident, waits=[tmu])
            te2 = b.rd(5, "vector", lambda e, j=j: e.tensor_copy(
                sb_r2[0:1, j * 128:(j + 1) * 128], pb[5][0:1, 0:128]))
            toks += [te1, te2]
            prev = [te2]
            if want_r3:
                b.tp(5, pb[5][0:1, 0:128], cc0[:, 9:10], sb_ident, waits=[te2])
                te3 = b.rd(5, "vector", lambda e, j=j: e.tensor_copy(
                    sb_r3[0:1, j * 128:(j + 1) * 128], pb[5][0:1, 0:128]))
                toks.append(te3)
                prev = [te3]
        return toks

    xn_last = []
    s_stores = []
    a4_c1p, a4_mmp = [], []
    fxprev = []
    prev_ab = []
    for sq in range(c.NSEQ):
        t00 = sq * c.L
        b.mark(f"A1.{sq}")
        h_tok = []
        xslab_free = None
        for sl in range(c.NQS):
            tok0 = t00 + sl * c.SLAB
            stile = tok0 // 128
            row_toks = build_rows(stile, True,
                                  [tile_done[stile + j] for j in range(c.TPS)])
            b.mm(0, pb[0][:, 0:c.SLAB], sb_onesrow, sb_r1[0:1, 0:c.SLAB], True,
                 True, waits=row_toks)
            b.mm(1, pb[1][:, 0:c.SLAB], sb_onesrow, sb_r2[0:1, 0:c.SLAB], True,
                 True, waits=row_toks)
            b.mm(2, pb[2][:, 0:c.SLAB], sb_onesrow, sb_r3[0:1, 0:c.SLAB], True,
                 True, waits=row_toks)
            t_x = b.dma("sync", sb_xts[0],
                        xT[:, (tok0 // c.SLAB) * c.DC * c.SLAB:
                           (tok0 // c.SLAB + 1) * c.DC * c.SLAB],
                        waits=[xslab_free] if xslab_free else bar_scat,
                        ch=b.XT[0])
            tm_toks = []
            c3p, c7p, tmp_ = [], [], []
            for dc in range(c.DC):
                xsl = sb_xts[0][:, dc * c.SLAB:(dc + 1) * c.SLAB]
                c1 = b.rd(0, "vector", lambda e, xsl=xsl: e.tensor_mul(
                    sb_fx, xsl, pb[0][:, 0:c.SLAB]), waits=[t_x] + c3p)
                c2 = b.rd(1, "vector", lambda e: e.tensor_sub(
                    sb_fx, sb_fx, pb[1][:, 0:c.SLAB]), waits=[c1])
                c3 = b.sca(lambda e: e.activation(sb_xn2[:, 0:c.SLAB], sb_fx, ACT.Copy),
                           waits=[c2] + tmp_)
                c4 = b.vec(lambda e: e.tensor_sub(sb_xn2[:, c.SLAB:], sb_fx,
                                                  sb_xn2[:, 0:c.SLAB]), waits=[c3])
                c5 = b.rd(2, "vector", lambda e: e.tensor_mul(
                    sb_fx2, sb_fx, pb[2][:, 0:c.SLAB]), waits=[c2] + c7p)
                c6 = b.vec(lambda e, dc=dc: e.tensor_scalar_mul(
                    sb_fx2, sb_fx2, sb_nw[:, dc:dc + 1]), waits=[c5])
                hh = sb_h[0][:, dc * c.L + sl * c.SLAB:dc * c.L + sl * c.SLAB + c.SLAB]
                hl = sb_h[1][:, dc * c.L + sl * c.SLAB:dc * c.L + sl * c.SLAB + c.SLAB]
                c7 = b.sca(lambda e, hh=hh: e.activation(hh, sb_fx2, ACT.Copy), waits=[c6])
                c8 = b.vec(lambda e, hh=hh, hl=hl: e.tensor_sub(hl, sb_fx2, hh),
                           waits=[c7])
                h_tok.append(c8)
                c3p, c7p = [c3], [c7]
                tmp_ = []
                for j in range(c.TPS):
                    for hilo in range(2):
                        dst = sb_tm[:, j * 2 * c.D + hilo * c.D + dc * 128:
                                    j * 2 * c.D + hilo * c.D + dc * 128 + 128]
                        te = bf16_tp(
                            (lambda e, ps, d=dst: e.tensor_copy(d, ps)),
                            sb_xn2[:, hilo * c.SLAB + j * 128:
                                   hilo * c.SLAB + j * 128 + 128],
                            [c3 if hilo == 0 else c4]
                            + ((xn_last[-c.TPS:] + s_stores[-c.TPS:])
                               if dc == 0 and j == 0 and hilo == 0 else []))
                        tmp_.append(te)
                        tm_toks.append(te)
            xslab_free = tm_toks[-1]
            for j in range(c.TPS):
                xn_last.append(b.dma("sync",
                                     xnbuf[tok0 + j * 128:tok0 + j * 128 + 128, :],
                                     sb_tm[:, j * 2 * c.D:(j + 1) * 2 * c.D],
                                     waits=tm_toks))
        bar_h = h_tok + xn_last[-c.TPS * c.NQS:]

        b.mark(f"A2k.{sq}")
        k_done = []
        for kvh in range(c.NKV):
            for slq in range(c.NQS):
                lastmm = {}
                for fc2 in range(c.HDC):
                    bank = fc2
                    col0 = (kvh * c.HDC + fc2) * 128
                    tw, wuh, wul, wsl = w_unit(
                        Wk[:, col0 * c.DC:(col0 + 128) * c.DC], c.DC * 128,
                        bar_h if (slq == 0 and fc2 == 0 and kvh == 0) else [])
                    for dc in range(c.DC):
                        wh_ = wuh[:, dc * 128:dc * 128 + 128]
                        wl_ = wul[:, dc * 128:dc * 128 + 128]
                        rh = sb_h[0][:, dc * c.L + slq * c.SLAB:dc * c.L + slq * c.SLAB + c.SLAB]
                        rl = sb_h[1][:, dc * c.L + slq * c.SLAB:dc * c.L + slq * c.SLAB + c.SLAB]
                        b.mm(bank, pb[bank][:, 0:c.SLAB], wh_, rh, dc == 0, False,
                             waits=([tw] + bar_h) if dc == 0 else [], inc=False)
                        b.mm(bank, pb[bank][:, 0:c.SLAB], wh_, rl, False, False,
                             inc=False)
                        t_l = b.mm(bank, pb[bank][:, 0:c.SLAB], wl_, rh, False,
                                   dc == c.DC - 1)
                    w_used(wsl, t_l)
                    lastmm[fc2] = t_l
                sq1 = b.rd(0, "scalar", lambda e: e.activation(
                    sb_fx, pb[0][:, 0:c.SLAB], ACT.Square), waits=fxprev)
                b.mm(4, pb[4][0:1, 0:c.SLAB], ones_col, sb_fx, True, c.HDC == 1,
                     waits=[sq1], inc=(c.HDC == 1))
                if c.HDC > 1:
                    sq2 = b.rd(1, "scalar", lambda e: e.activation(
                        sb_fx2, pb[1][:, 0:c.SLAB], ACT.Square))
                    b.mm(4, pb[4][0:1, 0:c.SLAB], ones_col, sb_fx2, False, True,
                         waits=[sq2])
                n1 = b.rd(4, "scalar", lambda e: e.activation(
                    sb_rr, pb[4][0:1, 0:c.SLAB], ACT.Sqrt))
                n2 = b.vec(lambda e: e.tensor_scalar_max(sb_rr, sb_rr, 1e-12),
                           waits=[n1])
                n3 = b.vec(lambda e: e.reciprocal(sb_rr, sb_rr), waits=[n2])
                b.mm(5, pb[5][:, 0:c.SLAB], sb_onesrow, sb_rr, True, True,
                     waits=[n3])
                be = b.rd(5, "vector", lambda e: e.tensor_copy(
                    sb_fx3, pb[5][:, 0:c.SLAB]))
                hhp = []
                for fc2 in range(c.HDC):
                    base_k = kvh * c.HDC * c.L + fc2 * c.L + slq * c.SLAB
                    pr = b.rd(fc2, "vector", lambda e, fc2=fc2: e.tensor_mul(
                        sb_fx, pb[fc2][:, 0:c.SLAB], sb_fx3), waits=[be] + hhp)
                    hh = b.sca(lambda e, base_k=base_k: e.activation(
                        sb_k[0][:, base_k:base_k + c.SLAB], sb_fx, ACT.Copy), waits=[pr])
                    hhp = [hh]
                    ll = b.vec(lambda e, base_k=base_k: e.tensor_sub(
                        sb_k[1][:, base_k:base_k + c.SLAB], sb_fx,
                        sb_k[0][:, base_k:base_k + c.SLAB]), waits=[hh])
                    k_done.append(ll)
                fxprev = [ll, be, hh]
        b.mark(f"A2v.{sq}")
        v_done = {}
        NVG = min(4, c.KT)
        for tg in range(0, c.KT, NVG):
            ng = min(NVG, c.KT - tg)
            lastv = {}
            VW = c.NKV * c.HD
            for dp in range(0, c.DC, 2):
                tw, wuh, wul, wsl = w_unit(Wv[:, dp * VW:(dp + 2) * VW], 2 * VW,
                                           bar_h if tg == 0 and dp == 0 else [])
                for d2 in range(2):
                    dc = dp + d2
                    wvh = wuh[:, d2 * VW:(d2 + 1) * VW]
                    wvl = wul[:, d2 * VW:(d2 + 1) * VW]
                    for ti in range(ng):
                        tt = tg + ti
                        bank = ti
                        l0 = sb_h[0][:, dc * c.L + tt * 128:dc * c.L + tt * 128 + 128]
                        l1 = sb_h[1][:, dc * c.L + tt * 128:dc * c.L + tt * 128 + 128]
                        b.mm(bank, pb[bank][:, 0:VW], l0, wvh, dc == 0,
                             False, waits=(([tw] + bar_h) if ti == 0 and d2 == 0
                                           else []), inc=False)
                        b.mm(bank, pb[bank][:, 0:VW], l0, wvl, False, False,
                             inc=False)
                        lastv[ti] = b.mm(bank, pb[bank][:, 0:VW], l1, wvh,
                                         False, dc == c.DC - 1)
                w_used(wsl, lastv[ng - 1])
            for ti in range(ng):
                tt = tg + ti
                vh = sb_vv[0][:, tt * c.NKV * c.HD:(tt + 1) * c.NKV * c.HD]
                vl = sb_vv[1][:, tt * c.NKV * c.HD:(tt + 1) * c.NKV * c.HD]
                e1 = b.rd(ti, "scalar", lambda e, vh=vh, ti=ti: e.activation(
                    vh, pb[ti][:, 0:c.NKV * c.HD], ACT.Copy))
                e2 = b.rd(ti, "vector", lambda e, vh=vh, vl=vl, ti=ti: e.tensor_sub(
                    vl, pb[ti][:, 0:c.NKV * c.HD], vh), waits=[e1])
                v_done[tt] = e2
        v_all = list(v_done.values())

        for kvh in range(c.NKV):
            for qh in range(kvh * c.NREP, (kvh + 1) * c.NREP):
                for qs in range(c.NQS):
                    b.mark(f"A3.{sq}.{qh}.{qs}")
                    for fc2 in range(c.HDC):
                        bank = fc2
                        col0 = (qh * c.HDC + fc2) * 128
                        tw, wuh, wul, wsl = w_unit(
                            Wq[:, col0 * c.DC:(col0 + 128) * c.DC],
                            c.DC * 128, [])
                        for dc in range(c.DC):
                            wh_ = wuh[:, dc * 128:dc * 128 + 128]
                            wl_ = wul[:, dc * 128:dc * 128 + 128]
                            rh = sb_h[0][:, dc * c.L + qs * c.SLAB:dc * c.L + qs * c.SLAB + c.SLAB]
                            rl = sb_h[1][:, dc * c.L + qs * c.SLAB:dc * c.L + qs * c.SLAB + c.SLAB]
                            b.mm(bank, pb[bank][:, 0:c.SLAB], wh_, rh, dc == 0,
                                 False, waits=([tw] + bar_h) if dc == 0 else [],
                                 inc=False)
                            b.mm(bank, pb[bank][:, 0:c.SLAB], wh_, rl, False,
                                 False, inc=False)
                            t_l = b.mm(bank, pb[bank][:, 0:c.SLAB], wl_, rh,
                                       False, dc == c.DC - 1)
                        w_used(wsl, t_l)
                    sq1 = b.rd(0, "scalar", lambda e: e.activation(
                        sb_fx, pb[0][:, 0:c.SLAB], ACT.Square), waits=fxprev)
                    b.mm(4, pb[4][0:1, 0:c.SLAB], ones_col, sb_fx, True,
                         c.HDC == 1, waits=[sq1], inc=(c.HDC == 1))
                    if c.HDC > 1:
                        sq2 = b.rd(1, "scalar", lambda e: e.activation(
                            sb_fx2, pb[1][:, 0:c.SLAB], ACT.Square))
                        b.mm(4, pb[4][0:1, 0:c.SLAB], ones_col, sb_fx2, False,
                             True, waits=[sq2])
                    n1 = b.rd(4, "scalar", lambda e: e.activation(
                        sb_rr, pb[4][0:1, 0:c.SLAB], ACT.Sqrt))
                    n2 = b.vec(lambda e: e.tensor_scalar_max(sb_rr, sb_rr, 1e-12),
                               waits=[n1])
                    n3 = b.vec(lambda e: e.reciprocal(sb_rr, sb_rr), waits=[n2])
                    n4 = b.vec(lambda e: e.tensor_scalar_mul(
                        sb_rr, sb_rr, sb_scal[0:1, 2:3]), waits=[n3, t_ts])
                    b.mm(5, pb[5][:, 0:c.SLAB], sb_onesrow, sb_rr, True, True,
                         waits=[n4])
                    be = b.rd(5, "vector", lambda e: e.tensor_copy(
                        sb_fx3, pb[5][:, 0:c.SLAB]))
                    qn_tok = []
                    hhp = []
                    for fc2 in range(c.HDC):
                        pr = b.rd(fc2, "vector", lambda e, fc2=fc2: e.tensor_mul(
                            sb_fx, pb[fc2][:, 0:c.SLAB], sb_fx3), waits=[be] + hhp)
                        hh = b.sca(lambda e, fc2=fc2: e.activation(
                            sb_qn[0][:, fc2 * c.SLAB:(fc2 + 1) * c.SLAB], sb_fx,
                            ACT.Copy), waits=[pr])
                        hhp = [hh]
                        ll = b.vec(lambda e, fc2=fc2: e.tensor_sub(
                            sb_qn[1][:, fc2 * c.SLAB:(fc2 + 1) * c.SLAB], sb_fx,
                            sb_qn[0][:, fc2 * c.SLAB:(fc2 + 1) * c.SLAB]),
                            waits=[hh])
                        qn_tok.append(ll)
                    fxprev = [ll, be, hh]
                    ktmax = (qs + 1) * c.TPS
                    pt_free = [None, None]
                    dn = None
                    lastpv = None
                    pl_of = {}

                    def emit_pv(kt, kvh=kvh):
                        nonlocal lastpv, dn
                        bs = kt % 2
                        pl = pl_of[kt]
                        b.mm(1, pb[1][0:1, 0:c.SLAB], sb_onesb, sb_pt[2 * bs],
                             kt == 0, False, waits=[pl, t_ob], inc=False)
                        dn = b.mm(1, pb[1][0:1, 0:c.SLAB], sb_onesb,
                                  sb_pt[2 * bs + 1], False, kt == ktmax - 1,
                                  waits=[pl])
                        for fc2 in range(c.HDC):
                            vb = kt * c.NKV * c.HD + kvh * c.HD + fc2 * 128
                            vap_h = sb_vv[0][:, vb:vb + 128]
                            vap_l = sb_vv[1][:, vb:vb + 128]
                            bank = 2 + fc2
                            b.mm(bank, pb[bank][:, 0:c.SLAB], vap_h,
                                 sb_pt[2 * bs], kt == 0, False,
                                 waits=[pl] + (v_all[-1:] if kt == 0 else []),
                                 inc=False)
                            b.mm(bank, pb[bank][:, 0:c.SLAB], vap_h,
                                 sb_pt[2 * bs + 1], False, False, inc=False)
                            lastpv = b.mm(bank, pb[bank][:, 0:c.SLAB], vap_l,
                                          sb_pt[2 * bs], False, kt == ktmax - 1)
                        pt_free[bs] = lastpv

                    for kt in range(ktmax):
                        first = True
                        for fc2 in range(c.HDC):
                            kb = kvh * c.HDC * c.L + fc2 * c.L + kt * 128
                            kap_h = sb_k[0][:, kb:kb + 128]
                            kap_l = sb_k[1][:, kb:kb + 128]
                            qap_h = sb_qn[0][:, fc2 * c.SLAB:(fc2 + 1) * c.SLAB]
                            qap_l = sb_qn[1][:, fc2 * c.SLAB:(fc2 + 1) * c.SLAB]
                            b.mm(0, pb[0][:, 0:c.SLAB], kap_h, qap_h, first,
                                 False, waits=(qn_tok + k_done[-1:]) if kt == 0
                                 else [], inc=False)
                            first = False
                            b.mm(0, pb[0][:, 0:c.SLAB], kap_h, qap_l, False,
                                 False, inc=False)
                            smm = b.mm(0, pb[0][:, 0:c.SLAB], kap_l, qap_h, False,
                                       fc2 == c.HDC - 1)
                        off = kt * 128 - qs * c.SLAB
                        if 0 <= off < c.SLAB:
                            smm = b.rd(0, "vector", lambda e, off=off: e.tensor_add(
                                pb[0][:, off:off + 128], pb[0][:, off:off + 128],
                                sb_trimT))
                            b.bk_w[0] = smm
                        ex = b.rd(0, "scalar", lambda e: e.activation(
                            sb_fx, pb[0][:, 0:c.SLAB], ACT.Exp),
                            waits=(pl_prev if kt > 0 else fxprev))
                        if 0 < off < c.SLAB:
                            ex = b.vec(lambda e, off=off: e.memset(
                                sb_fx[:, 0:off], 0.0), waits=[ex])
                        bs = kt % 2
                        wpt = [x for x in (pt_free[bs],) if x]
                        ph = b.sca(lambda e, bs=bs: e.activation(
                            sb_pt[2 * bs], sb_fx, ACT.Copy), waits=[ex] + wpt)
                        pl = b.vec(lambda e, bs=bs: e.tensor_sub(
                            sb_pt[2 * bs + 1], sb_fx, sb_pt[2 * bs]), waits=[ph])
                        pl_prev = [pl]
                        pl_of[kt] = pl
                        if kt > 0:
                            emit_pv(kt - 1)
                    emit_pv(ktmax - 1)
                    rd_ = b.rd(1, "vector", lambda e: e.reciprocal(
                        sb_rr, pb[1][0:1, 0:c.SLAB]))
                    b.mm(4, pb[4][:, 0:c.SLAB], sb_onesrow, sb_rr, True, True,
                         waits=[rd_])
                    be2 = b.rd(4, "vector", lambda e: e.tensor_copy(
                        sb_fx3, pb[4][:, 0:c.SLAB]))
                    hhp = []
                    for fc2 in range(c.HDC):
                        r_ch = qh * c.HDC + fc2
                        pr = b.rd(2 + fc2, "vector", lambda e, fc2=fc2:
                                  e.tensor_mul(sb_fx, pb[2 + fc2][:, 0:c.SLAB],
                                               sb_fx3), waits=[be2] + hhp)
                        hh = b.sca(lambda e: e.activation(
                            sb_ab[:, 0:c.SLAB], sb_fx, ACT.Copy),
                            waits=[pr] + prev_ab)
                        hhp = [hh]
                        ll = b.vec(lambda e: e.tensor_sub(
                            sb_ab[:, c.SLAB:], sb_fx, sb_ab[:, 0:c.SLAB]),
                            waits=[hh])
                        ab0 = ((sq * 2 + 0) * c.DC + r_ch) * 128
                        ab1 = ((sq * 2 + 1) * c.DC + r_ch) * 128
                        d1 = b.dma("sync",
                              aobuf[ab0:ab0 + 128, qs * c.SLAB:(qs + 1) * c.SLAB],
                              sb_ab[:, 0:c.SLAB], waits=[ll])
                        d2 = b.dma("sync",
                              aobuf[ab1:ab1 + 128, qs * c.SLAB:(qs + 1) * c.SLAB],
                              sb_ab[:, c.SLAB:], waits=[ll])
                        prev_ab = [d1, d2]
                    fxprev = [ll, be2, hh]

        b.mark(f"A4.{sq}")
        bar_att = P.barrier()
        for sl in range(c.NQS):
            tok0 = t00 + sl * c.SLAB
            stile = tok0 // 128
            row_toks = build_rows(stile, False, bar_att)
            b.mm(2, pb[2][:, 0:c.SLAB], sb_onesrow, sb_r1[0:1, 0:c.SLAB], True,
                 True, waits=row_toks)
            b.mm(3, pb[3][:, 0:c.SLAB], sb_onesrow, sb_r2[0:1, 0:c.SLAB], True,
                 True, waits=row_toks)
            t_x = b.dma("sync", sb_xts[0],
                        xT[:, (tok0 // c.SLAB) * c.DC * c.SLAB:
                           (tok0 // c.SLAB + 1) * c.DC * c.SLAB],
                        waits=bar_att + a4_c1p, ch=b.XT[0])
            ao_toks = []
            for r in range(c.DC):
                ab0 = ((sq * 2 + 0) * c.DC + r) * 128
                ab1 = ((sq * 2 + 1) * c.DC + r) * 128
                ao_toks.append(b.dma("sync",
                                     sb_aoin[0][:, r * c.SLAB:(r + 1) * c.SLAB],
                                     aobuf[ab0:ab0 + 128, sl * c.SLAB:(sl + 1) * c.SLAB],
                                     waits=bar_att + a4_mmp))
                ao_toks.append(b.dma("sync",
                                     sb_aoin[1][:, r * c.SLAB:(r + 1) * c.SLAB],
                                     aobuf[ab1:ab1 + 128, sl * c.SLAB:(sl + 1) * c.SLAB],
                                     waits=bar_att))
            st_toks = []
            a4_trp = []
            for dout in range(c.DC):
                bank = dout % 2
                lastmm = None
                tw, wuh, wul, wsl = w_unit(
                    Wo[:, dout * c.DC * 128:(dout + 1) * c.DC * 128],
                    c.DC * 128, [])
                for r in range(c.DC):
                    woh = wuh[:, r * 128:r * 128 + 128]
                    wol = wul[:, r * 128:r * 128 + 128]
                    ah = sb_aoin[0][:, r * c.SLAB:(r + 1) * c.SLAB]
                    al = sb_aoin[1][:, r * c.SLAB:(r + 1) * c.SLAB]
                    b.mm(bank, pb[bank][:, 0:c.SLAB], woh, ah, r == 0, False,
                         waits=([tw] + ao_toks) if r == 0 else [], inc=False)
                    b.mm(bank, pb[bank][:, 0:c.SLAB], woh, al, False, False,
                         inc=False)
                    lastmm = b.mm(bank, pb[bank][:, 0:c.SLAB], wol, ah, False,
                                  r == c.DC - 1)
                w_used(wsl, lastmm)
                xsl = sb_xts[0][:, dout * c.SLAB:(dout + 1) * c.SLAB]
                c1 = b.rd(2, "vector", lambda e, xsl=xsl: e.tensor_mul(
                    sb_fx, xsl, pb[2][:, 0:c.SLAB]), waits=[t_x])
                c2 = b.rd(3, "vector", lambda e: e.tensor_sub(
                    sb_fx, sb_fx, pb[3][:, 0:c.SLAB]), waits=[c1])
                c3 = b.rd(bank, "vector", lambda e, bank=bank:
                          e.scalar_tensor_tensor(sb_fx2, pb[bank][:, 0:c.SLAB],
                                                 sb_gcol[:, 0:1], sb_fx, ALU.mult,
                                                 ALU.add), waits=[c2, t_gce] + a4_trp)
                a4_trp = []
                for j in range(c.TPS):
                    ttr = b.tp(5, pb[5][:, 0:128],
                               sb_fx2[:, j * 128:j * 128 + 128],
                               sb_ident, waits=[c3]
                               + (s_stores[-c.TPS:] if dout == 0 and j == 0
                                  else []))
                    a4_trp.append(ttr)
                    te = b.rd(5, "vector", lambda e, j=j, dout=dout: e.tensor_copy(
                        sb_st[:, j * c.D + dout * 128:j * c.D + dout * 128 + 128],
                        pb[5][:, 0:128]))
                    st_toks.append(te)
                a4_c1p = [c1]
                a4_mmp = [lastmm]
            for j in range(c.TPS):
                s_stores.append(
                    b.dma("sync", sbufd[tok0 + j * 128:tok0 + j * 128 + 128, :],
                          sb_st[:, j * c.D:(j + 1) * c.D], waits=st_toks))

    bar_a = P.barrier()
    fx3_readers = []
    m_prp, m_h1p = {0: [], 1: []}, {0: [], 1: []}
    y_busy = [None] * 4
    gt_free = [None, None]
    act_free = None
    nhs = len(c.HSLABS)
    g1_guard = []
    for ee in range(c.E):
        for hf in range(c.NHALF):
            b.mark(f"M.{ee}.{hf}")
            tile0 = ee * c.TPE + hf * c.TPH
            xg_toks = []
            for tt in range(c.TPH):
                gb = tt % 2
                gsl = sb_gt[:, gb * 2 * c.D:(gb + 1) * 2 * c.D]
                wts = [t_slot] + (bar_a if tt < 2 else []) + \
                      ([gt_free[gb]] if gt_free[gb] else []) + \
                      (g1_guard if tt == 0 else [])
                tg = b.gpd(lambda e, gsl=gsl, ti=tile0 + tt: e.indirect_dma_start(
                    out=gsl, out_offset=None, in_=xnbuf,
                    in_offset=IndirectOffsetOnAxis(sb_slotcols[:, ti:ti + 1], 0)),
                    waits=wts, ch=b.GB[gb])
                lastte = None
                for dc in range(c.DC):
                    for hilo in range(2):
                        dst = sb_xg[hilo][:, dc * c.HSLOT + tt * 128:
                                          dc * c.HSLOT + tt * 128 + 128]
                        te = bf16_tp(
                            (lambda e, ps, d=dst: e.tensor_copy(d, ps)),
                            gsl[:, hilo * c.D + dc * 128:
                                hilo * c.D + dc * 128 + 128],
                            [tg])
                        lastte = te
                        xg_toks.append(te)
                gt_free[gb] = lastte
            xg_last = xg_toks[-1]
            act_toks = []
            for fp in range(c.FC):
                lastg1 = {}
                for gi, fch in enumerate((fp, fp + c.FC)):
                    u0_ = (ee * 2 * c.FC + fch) * c.DC * 128
                    tw, wuh, wul, wsl = w_unit(
                        W_in[:, u0_:u0_ + c.DC * 128], c.DC * 128, [])
                    for dc in range(c.DC):
                        wih = wuh[:, dc * 128:dc * 128 + 128]
                        wil = wul[:, dc * 128:dc * 128 + 128]
                        s0 = 0
                        for si, wdt in enumerate(c.HSLABS):
                            bank = gi * nhs + si
                            rhs_h = sb_xg[0][:, dc * c.HSLOT + s0:dc * c.HSLOT + s0 + wdt]
                            rhs_l = sb_xg[1][:, dc * c.HSLOT + s0:dc * c.HSLOT + s0 + wdt]
                            b.mm(bank, pb[bank][:, 0:wdt], wih, rhs_h, dc == 0,
                                 False, waits=([tw, xg_last] if dc == 0 and
                                               si == 0 else []), inc=False)
                            b.mm(bank, pb[bank][:, 0:wdt], wih, rhs_l, False,
                                 False, inc=False)
                            lastg1[(gi, si)] = b.mm(bank, pb[bank][:, 0:wdt], wil,
                                                    rhs_h, False, dc == c.DC - 1)
                            s0 += wdt
                    w_used(wsl, lastg1[(gi, nhs - 1)])
                s0 = 0
                for si, wdt in enumerate(c.HSLABS):
                    par = (fp * nhs + si) % 2
                    gdst = (sb_fx, sb_fx3)[par]
                    pdst = (sb_fx2, sb_fx4)[par]
                    ga = b.rd(si, "scalar", lambda e, d=gdst, si=si, wdt=wdt:
                              e.activation(d[:, 0:wdt], pb[si][:, 0:wdt],
                                           ACT.Gelu), waits=m_prp[par])
                    pr = b.rd(nhs + si, "vector", lambda e, d=pdst, g=gdst,
                              si=si, wdt=wdt:
                              e.tensor_mul(d[:, 0:wdt], g[:, 0:wdt],
                                           pb[nhs + si][:, 0:wdt]),
                              waits=[ga] + m_h1p[par])
                    m_prp[par] = [pr]
                    ah = sb_act[0][:, fp * c.HSLOT + s0:fp * c.HSLOT + s0 + wdt]
                    al = sb_act[1][:, fp * c.HSLOT + s0:fp * c.HSLOT + s0 + wdt]
                    h1 = b.sca(lambda e, ah=ah, d=pdst, wdt=wdt: e.activation(
                        ah, d[:, 0:wdt], ACT.Copy), waits=[pr])
                    m_h1p[par] = [h1]
                    l1 = b.vec(lambda e, ah=ah, al=al, d=pdst, wdt=wdt:
                               e.tensor_sub(al, d[:, 0:wdt], ah), waits=[h1])
                    act_toks.append(l1)
                    s0 += wdt
            g1_guard = [lastg1[(1, nhs - 1)]]
            if "mdump" in dbg:
                ti_ = ee * c.NHALF + hf
                g1_guard.append(b.dma(
                    "sync", xgdump[ti_ * 128:(ti_ + 1) * 128, :], sb_xg[0],
                    waits=[xg_last]))
                FW = c.FC * c.HSLOT
                b.dma("sync", actdump[ti_ * 128:(ti_ + 1) * 128, 0:FW],
                      sb_act[0], waits=act_toks[-1:])
                b.dma("sync", actdump[ti_ * 128:(ti_ + 1) * 128, FW:2 * FW],
                      sb_act[1], waits=act_toks[-1:])
            FH = max(1, c.FC // 2)
            for dout in range(c.DC):
                lastg2 = {}
                for fh in range(c.FC // FH):
                    u0_ = ((ee * (c.FC // FH) + fh) * c.DC + dout) * FH * 128
                    tw, wuh, wul, wsl = w_unit(
                        W_out[:, u0_:u0_ + FH * 128], FH * 128,
                        act_toks[-1:] if fh == 0 and dout == 0 else [])
                    if "mdump" in dbg and fh == 0 and dout == 0:
                        ti_ = ee * c.NHALF + hf
                        b.dma("sync", wbdump[ti_ * 128:(ti_ + 1) * 128, :],
                              sb_wb[:, wsl * 2048:wsl * 2048 + 2048],
                              waits=[tw])
                    for f8 in range(FH):
                        fc = fh * FH + f8
                        wgh = wuh[:, f8 * 128:f8 * 128 + 128]
                        wgl = wul[:, f8 * 128:f8 * 128 + 128]
                        s0 = 0
                        for si, wdt in enumerate(c.HSLABS):
                            bank = (dout * nhs + si) % 4
                            ath = sb_act[0][:, fc * c.HSLOT + s0:fc * c.HSLOT + s0 + wdt]
                            atl = sb_act[1][:, fc * c.HSLOT + s0:fc * c.HSLOT + s0 + wdt]
                            b.mm(bank, pb[bank][:, 0:wdt], wgh, ath, fc == 0,
                                 False, waits=(([tw] + (act_toks[-1:] if fh == 0
                                                        and dout == 0 else []))
                                               if f8 == 0 and si == 0 else []),
                                 inc=False)
                            b.mm(bank, pb[bank][:, 0:wdt], wgh, atl, False, False,
                                 inc=False)
                            lastg2[si] = b.mm(bank, pb[bank][:, 0:wdt], wgl, ath,
                                              False, fc == c.FC - 1)
                            s0 += wdt
                    w_used(wsl, lastg2[nhs - 1])
                s0 = 0
                for si, wdt in enumerate(c.HSLABS):
                    bank = (dout * nhs + si) % 4
                    tev = b.rd(bank, "vector", lambda e, wdt=wdt, bank=bank: e.tensor_copy(
                        sb_fx3[:, 0:wdt], pb[bank][:, 0:wdt]),
                        waits=fx3_readers)
                    fx3_readers = []
                    for j in range(wdt // 128):
                        tb = 4 + (j % 2)
                        ttp = b.tp(tb, pb[tb][:, 0:128],
                                   sb_fx3[:, j * 128:j * 128 + 128], sb_ident,
                                   waits=[tev])
                        fx3_readers.append(ttp)
                        ysl = sb_y[:, (j % 4) * 128:(j % 4) * 128 + 128]
                        te = b.rd(tb, "vector", lambda e, ysl=ysl, tb=tb:
                                  e.tensor_copy(ysl, pb[tb][:, 0:128]),
                            waits=[y_busy[j % 4]] if y_busy[j % 4] else [])
                        slot0 = ee * c.CLOC + hf * c.HSLOT + s0 + j * 128
                        y_busy[j % 4] = b.dma(
                            "sync", yslots[slot0:slot0 + 128,
                                           dout * 128:(dout + 1) * 128],
                            ysl, waits=[te], ch=b.YB[j % 4])
                    s0 += wdt

    b.mark("F")
    bar_m = P.barrier()
    pv = {}
    for tile in range(c.NTILE_OWN):
        tok0 = tile * 128
        cc0 = st_c[:, tile * 16:tile * 16 + 16]
        tg0 = b.gpd(lambda e, tile=tile: e.indirect_dma_start(
            out=sb_fa, out_offset=None, in_=yslots,
            in_offset=IndirectOffsetOnAxis(st_yg[:, tile * 2:tile * 2 + 1], 0)),
            waits=bar_m + [pv["f1"]] if "f1" in pv else bar_m)
        tg1 = b.gpd(lambda e, tile=tile: e.indirect_dma_start(
            out=sb_fb, out_offset=None, in_=yslots,
            in_offset=IndirectOffsetOnAxis(st_yg[:, tile * 2 + 1:tile * 2 + 2], 0)),
            waits=[pv["f2"]] if "f2" in pv else bar_m)
        ts_ = b.dma("sync", sb_fs, sbufd[tok0:tok0 + 128, :],
                    waits=[pv["f4"]] if "f4" in pv else bar_m)
        tx_ = b.dma("scalar", sb_fo, own_x[tok0:tok0 + 128, :],
                    waits=[pv["f5"]] if "f5" in pv else bar_m)
        f1 = b.vec(lambda e, cc=cc0: e.tensor_scalar_mul(sb_fy, sb_fa,
                                                         cc[:, 10:11]),
                   waits=[tg0] + ([pv["f2"], pv["f3"]] if "f2" in pv else []))
        f2 = b.vec(lambda e, cc=cc0: e.scalar_tensor_tensor(
            sb_fy, sb_fb, cc[:, 11:12], sb_fy, ALU.mult, ALU.add),
            waits=[f1, tg1])
        f3 = b.sca(lambda e: e.activation(sb_fy2, sb_fy, ACT.Silu),
                   waits=[f2] + ([pv["dma"]] if "dma" in pv else []))
        pv["f3"] = f3
        f4 = b.vec(lambda e: e.tensor_mul(sb_fy2, sb_fy2, sb_fs), waits=[f3, ts_])
        f5 = b.vec(lambda e: e.tensor_add(sb_fy2, sb_fy2, sb_fo), waits=[f4, tx_])
        pv["dma"] = b.dma("sync", out_own[tok0:tok0 + 128, :], sb_fy2, waits=[f5])
        pv["f1"], pv["f2"], pv["f4"], pv["f5"] = f1, f2, f4, f5

    fin = P.barrier()
    for eng in ("sync", "scalar", "vector", "tensor", "gpsimd"):
        P.add(eng, lambda e: e.nop(), waits=fin)

    global LAST_MARKS
    LAST_MARKS = b.marks
    import os
    if os.environ.get("KBUILD_STATS"):
        for cc_ in P.chans:
            print("chan", cc_.name, "final", cc_.n)
    P.emit(nc)
    return nc


def make_consts(c):
    ident = np.eye(128, dtype=np.float32)
    identb = np.eye(128).astype(ml_dtypes.bfloat16)
    triu = np.triu(np.ones((128, 128), np.float32), 1)
    trimaskT = np.where(np.arange(128)[:, None] > np.arange(128)[None, :],
                        np.float32(NEG), np.float32(0.0))
    colpack = np.zeros((128, 4), np.float32)
    colpack[:, 0] = 1.0
    colpack[:, 1] = np.arange(128)
    colpack[:, 2] = 2 * np.arange(128)
    erow = np.tile(np.arange(8, dtype=np.float32)[None, :], (128, 1))
    onesrow = np.ones((1, 128), np.float32)
    return ident, identb, triu, trimaskT, colpack, erow, onesrow


LAST_MARKS = []


def _pk(w, shape, perm):
    return np.ascontiguousarray(w.reshape(shape).transpose(perm).reshape(128, -1))


def stage_inputs(c, inputs):
    x = np.asarray(inputs["L0_attn_x"], dtype=np.float32)
    xf = np.ascontiguousarray(x.reshape(c.T, c.D))
    xTs = np.ascontiguousarray(xf.T)
    ident, identb, triu, trimaskT, colpack, erow, onesrow = make_consts(c)
    DC, HDC = c.DC, c.HDC
    wq = _pk(np.asarray(inputs["L0_attn_Wq"], dtype=np.float32),
             (DC, 128, c.NH, HDC, 128), (1, 2, 3, 0, 4))
    wk = _pk(np.asarray(inputs["L0_attn_Wk"], dtype=np.float32),
             (DC, 128, c.NKV, HDC, 128), (1, 2, 3, 0, 4))
    wv = _pk(np.asarray(inputs["L0_attn_Wv"], dtype=np.float32),
             (DC, 128, c.NKV * c.HD), (1, 0, 2))
    wo = _pk(np.asarray(inputs["L0_attn_Wo"], dtype=np.float32),
             (DC, 128, DC, 128), (1, 2, 0, 3))
    wg = np.ascontiguousarray(inputs["L0_ffn_Wg"], dtype=np.float32)
    win = _pk(np.asarray(inputs["L0_ffn_W_in"], dtype=np.float32),
              (c.E, DC, 128, 2 * c.FC, 128), (2, 0, 3, 1, 4))
    FH = max(1, c.FC // 2)
    wout = _pk(np.asarray(inputs["L0_ffn_W_out"], dtype=np.float32),
               (c.E, c.FC // FH, FH, 128, DC, 128), (3, 0, 1, 4, 2, 5))
    nw = np.ascontiguousarray(inputs["L0_attn_norm_w"], dtype=np.float32)
    gate = np.asarray(inputs["L0_attn_gate"], dtype=np.float32).reshape(1, 1)
    lt = np.asarray(inputs["L0_attn_log_temperature"],
                    dtype=np.float32).reshape(1, 1)
    in_maps = []
    for core in range(c.NCORE):
        xr = np.roll(xTs, -core * c.TOWN, axis=1) if core else xTs
        xr = _pk(xr, (c.DC, 128, c.T // c.SLAB, c.SLAB), (1, 2, 0, 3))
        belowv = np.zeros((c.NCORE, 2), np.float32)
        for r in range(c.NCORE):
            tru = (core + r) % c.NCORE
            belowv[r, 0] = 1.0 if tru < core else 0.0
            belowv[r, 1] = 1.0
        in_maps.append({
            "xT": np.ascontiguousarray(xr),
            "own_x": np.ascontiguousarray(xf[core * c.TOWN:(core + 1) * c.TOWN]),
            "Wq": wq, "Wk": wk, "Wv": wv, "Wo": wo, "Wg": wg,
            "W_in": win, "W_out": wout, "norm_w": nw,
            "gate_s": gate, "ltemp": lt, "below": belowv,
            "ident": ident, "identb": identb, "triu": triu,
            "trimaskT": trimaskT, "colpack": colpack, "erow": erow,
            "onesrow": onesrow,
        })
    return in_maps


def kernel(**inputs):
    c = Cfg(tiny=False)
    in_maps = stage_inputs(c, inputs)
    nc = build(c, dbg=True)
    res = run_bass_kernel_spmd(nc, in_maps, core_ids=list(range(c.NCORE)))
    out = np.concatenate([r["out"] for r in res.results], axis=0)
    return out.reshape(32, 2048, 1024)


# version 12 (final)
# speedup vs baseline: 1.0850x; 1.0522x over previous
import math
import numpy as np
import ml_dtypes
import concourse.bass as bass
import concourse.mybir as mybir
from concourse.bass_utils import run_bass_kernel_spmd
from concourse.bass_core_rust import IndirectOffsetOnAxis

F32 = mybir.dt.float32
BF16 = mybir.dt.bfloat16
I32 = mybir.dt.int32
U32 = mybir.dt.uint32
ALU = mybir.AluOpType
ACT = mybir.ActivationFunctionType
AX = mybir.AxisListType

NEG = -1.0e30


class Cfg:
    def __init__(self, tiny=False, mid=False):
        if mid:
            self.NCORE, self.D, self.L = 2, 1024, 512
            self.NSEQ, self.NH, self.NKV = 1, 4, 2
            self.E, self.F, self.SLAB = 8, 256, 256
        elif tiny:
            self.NCORE, self.D, self.L = 2, 256, 256
            self.NSEQ, self.NH, self.NKV = 2, 2, 1
            self.E, self.F, self.SLAB = 8, 128, 128
        else:
            self.NCORE, self.D, self.L = 8, 1024, 2048
            self.NSEQ, self.NH, self.NKV = 4, 4, 2
            self.E, self.F, self.SLAB = 8, 2048, 512
        self.DC = self.D // 128
        self.HD = self.D // self.NH
        self.HDC = self.HD // 128
        self.NREP = self.NH // self.NKV
        self.TOWN = self.NSEQ * self.L
        self.T = self.NCORE * self.TOWN
        self.NSLAB_OWN = self.TOWN // self.SLAB
        self.NSLAB_ALL = self.T // self.SLAB
        self.NTILE_OWN = self.TOWN // 128
        self.TPS = self.SLAB // 128
        self.KT = self.L // 128
        self.NQS = self.L // self.SLAB
        self.F2 = 2 * self.F
        self.FC = self.F // 128
        self.CGLOB = int(math.ceil(2 * self.T / self.E * 1.25))
        cl = int(math.ceil(2 * self.TOWN / self.E * 1.25))
        self.CLOC = ((cl + 127) // 128) * 128
        self.TPE = self.CLOC // 128
        self.NSLOT = self.E * self.CLOC
        self.NT_IMG = (self.NSLOT + 2 * self.TOWN) // 128
        self.NHALF = 2 if (not tiny or mid) else 1
        assert self.TPE % self.NHALF == 0
        self.TPH = self.TPE // self.NHALF
        self.HSLOT = self.TPH * 128
        self.HSLABS = []
        r = self.HSLOT
        while r > 0:
            w = min(self.SLAB, r)
            self.HSLABS.append(w)
            r -= w


class Ch:
    def __init__(self, name, engine, step):
        self.name, self.engine, self.step = name, engine, step
        self.is_dma = step == 16
        self.n = 0
        self.sem = None


class Prog:
    def __init__(self):
        self.ops = []
        self.chans = []

    def chan(self, name, engine, step=1):
        ch = Ch(name, engine, step)
        self.chans.append(ch)
        return ch

    def add(self, engine, fn, waits=(), inc=None):
        w = [(cc, v) for (cc, v) in waits if cc is not None and v > 0]
        self.ops.append((engine, fn, w, inc))
        if inc is not None:
            inc.n += inc.step
            return (inc, inc.n)
        return None

    def barrier(self):
        return [(cc, cc.n) for cc in self.chans]

    def emit(self, nc):
        for cc in self.chans:
            cc.sem = nc.alloc_semaphore(name="ch_" + cc.name)
        streams = {}
        for op in self.ops:
            streams.setdefault(op[0], []).append(op)

        def make_replay(ename, myops):
            def replay(eng):
                waited = {}
                for (_, fn, waits, inc) in myops:
                    for (cc, v) in waits:
                        if waited.get(cc.name, 0) >= v:
                            continue
                        eng.wait_ge(cc.sem, v)
                        waited[cc.name] = v
                    inst = fn(eng)
                    if inc is not None:
                        inst.then_inc(inc.sem, inc.step)
            return replay

        with nc.Block() as block:
            decs = {"sync": block.sync, "scalar": block.scalar,
                    "vector": block.vector, "tensor": block.tensor,
                    "gpsimd": block.gpsimd}
            for ename, myops in streams.items():
                decs[ename](make_replay(ename, myops))


class B:
    def __init__(self, cfg):
        self.c = cfg
        self.nc = bass.Bass("TRN2", target_bir_lowering=False,
                            detect_race_conditions=False)
        self.P = Prog()
        P = self.P
        self.LD = P.chan("ld", "sync", 16)
        self.SD = P.chan("sd", "scalar", 16)
        self.GD = P.chan("gd", "gpsimd", 16)
        self.WSEM = [P.chan("w0", "scalar", 16), P.chan("w1", "scalar", 16)]
        self.XT = [P.chan("xt0", "sync", 16), P.chan("xt1", "sync", 16)]
        self.CST = P.chan("cst", "sync", 16)
        self.YB = [P.chan(f"y{i}", "sync", 16) for i in range(4)]
        self.GB = [P.chan("g0", "gpsimd", 16), P.chan("g1", "gpsimd", 16)]
        self.SS = P.chan("ss", "sync", 16)
        self.PE = P.chan("pe", "tensor", 1)
        self.DV = P.chan("dv", "vector", 1)
        self.SC = P.chan("sc", "scalar", 1)
        self.GP = P.chan("gp", "gpsimd", 1)
        used = self.nc.SBUF_PARTITION_SIZE_BYTES - self.nc.sbuf_bytes_remaining
        self._sb_off = (used + 31) // 32 * 32
        self.bk_w = [None] * 8
        self.bk_r = [None] * 8
        self.n_mm = 0
        self.marks = []

    def mark(self, label):
        self.marks.append((self.n_mm, label))

    def dma(self, q, out, in_, waits=(), slow=False, ch=None):
        if ch is None:
            ch = {"sync": self.LD, "scalar": self.SD, "gpsimd": self.GD}[q]
        return self.P.add(
            q, lambda e, o=out, i=in_, s=slow: e.dma_start(
                out=o, in_=i, allow_slow_non_contiguous=s),
            waits=waits, inc=ch)

    def mm(self, bank_i, out, lhsT, rhs, start, stop, waits=(), inc=True):
        w = list(waits)
        if start and self.bk_r[bank_i] is not None:
            w.append(self.bk_r[bank_i])
        self.n_mm += 1
        tok = self.P.add(
            "tensor",
            lambda e, o=out, l=lhsT, r=rhs, s1=start, s2=stop:
                e.matmul(o, l, r, start=s1, stop=s2),
            waits=w, inc=self.PE if inc else None)
        if inc:
            self.bk_w[bank_i] = tok
        return tok

    def tp(self, bank_i, out, in_, ident, waits=()):
        w = list(waits)
        if self.bk_r[bank_i] is not None:
            w.append(self.bk_r[bank_i])
        self.n_mm += 1
        tok = self.P.add(
            "tensor", lambda e, o=out, i=in_, d=ident: e.transpose(o, i, d),
            waits=w, inc=self.PE)
        self.bk_w[bank_i] = tok
        return tok

    def rd(self, bank_i, engine, fn, waits=()):
        w = list(waits)
        if self.bk_w[bank_i] is not None:
            w.append(self.bk_w[bank_i])
        ch = self.DV if engine == "vector" else self.SC
        tok = self.P.add(engine, fn, waits=w, inc=ch)
        self.bk_r[bank_i] = tok
        return tok

    def rd2(self, bank_i, bank_j, engine, fn, waits=()):
        w = list(waits)
        for bi in (bank_i, bank_j):
            if self.bk_w[bi] is not None:
                w.append(self.bk_w[bi])
        ch = self.DV if engine == "vector" else self.SC
        tok = self.P.add(engine, fn, waits=w, inc=ch)
        self.bk_r[bank_i] = tok
        self.bk_r[bank_j] = tok
        return tok

    def vec(self, fn, waits=()):
        return self.P.add("vector", fn, waits=waits, inc=self.DV)

    def sca(self, fn, waits=()):
        return self.P.add("scalar", fn, waits=waits, inc=self.SC)

    def gpd(self, fn, waits=(), ch=None):
        return self.P.add("gpsimd", fn, waits=waits, inc=ch or self.GD)

    def gpc(self, fn, waits=()):
        return self.P.add("gpsimd", fn, waits=waits, inc=self.GP)

    def sb(self, name, shape, dtype, offset=None):
        nbytes = int(np.prod(shape[1:])) * mybir.dt.size(dtype)
        if offset is None:
            offset = self._sb_off
            self._sb_off = (offset + nbytes + 31) // 32 * 32
        return self.nc.alloc_sbuf_tensor_at(name, shape, dtype, offset=offset).ap()


def build(cfg: Cfg, dbg=False, upto='F'):
    b = B(cfg)
    c = cfg
    nc, P = b.nc, b.P
    PE, DV, SC = b.PE, b.DV, b.SC

    DT = nc.dram_tensor
    xT = DT("xT", [128, c.DC * c.T], F32, kind="ExternalInput").ap()
    own_x = DT("own_x", [c.TOWN, c.D], F32, kind="ExternalInput").ap()
    Wq = DT("Wq", [128, c.NH * c.HDC * c.DC * 128], F32, kind="ExternalInput").ap()
    Wk = DT("Wk", [128, c.NKV * c.HDC * c.DC * 128], F32, kind="ExternalInput").ap()
    Wv = DT("Wv", [128, c.DC * c.NKV * c.HD], F32, kind="ExternalInput").ap()
    Wo = DT("Wo", [128, c.DC * c.DC * 128], F32, kind="ExternalInput").ap()
    Wg = DT("Wg", [c.D, c.E], F32, kind="ExternalInput").ap()
    W_in = DT("W_in", [128, c.E * 2 * c.FC * c.DC * 128], F32,
              kind="ExternalInput").ap()
    W_out = DT("W_out", [128, c.E * c.FC * c.DC * 128], F32,
               kind="ExternalInput").ap()
    norm_w = DT("norm_w", [c.D], F32, kind="ExternalInput").ap()
    gate_s = DT("gate_s", [1, 1], F32, kind="ExternalInput").ap()
    ltemp = DT("ltemp", [1, 1], F32, kind="ExternalInput").ap()
    below = DT("below", [c.NCORE, 2], F32, kind="ExternalInput").ap()
    ident = DT("ident", [128, 128], F32, kind="ExternalInput").ap()
    identb = DT("identb", [128, 128], BF16, kind="ExternalInput").ap()
    triu = DT("triu", [128, 128], F32, kind="ExternalInput").ap()
    trimaskT = DT("trimaskT", [128, 128], F32, kind="ExternalInput").ap()
    colpack = DT("colpack", [128, 4], F32, kind="ExternalInput").ap()
    erow = DT("erow", [128, 8], F32, kind="ExternalInput").ap()
    onesrow = DT("onesrow", [1, 128], F32, kind="ExternalInput").ap()

    out_own = DT("out", [c.TOWN, c.D], F32, kind="ExternalOutput").ap()
    if dbg is True:
        dbg = {"xnbuf", "sbufd", "aobuf", "yslots", "slotimg"}
    elif not dbg:
        dbg = set()
    def kd(n):
        return "ExternalOutput" if n in dbg else "Internal"
    xnbuf = DT("xnbuf", [c.TOWN + 1, 2 * c.D], BF16, kind=kd("xnbuf")).ap()
    sbufd = DT("sbufd", [c.TOWN, c.D], F32, kind=kd("sbufd")).ap()
    aobuf = DT("aobuf", [c.NSEQ * 2 * c.DC * 128, c.L], BF16, kind=kd("aobuf")).ap()
    yslots = DT("yslots", [c.NSLOT + 1, c.D], F32, kind=kd("yslots")).ap()
    slotimg = DT("slotimg", [128, c.NT_IMG], U32, kind=kd("slotimg")).ap()
    if "mdump" in dbg:
        NTIL = c.E * c.NHALF
        xgdump = DT("xgdump", [NTIL * 128, c.DC * c.HSLOT], BF16,
                    kind="ExternalOutput").ap()
        actdump = DT("actdump", [NTIL * 128, 2 * c.FC * c.HSLOT], BF16,
                     kind="ExternalOutput").ap()
        wbdump = DT("wbdump", [NTIL * 128, 2048], BF16,
                    kind="ExternalOutput").ap()

    sb_ident = b.sb("c_id", [128, 128], F32)
    sb_identb = b.sb("c_idb", [128, 128], BF16)
    sb_triu = b.sb("c_triu", [128, 128], F32)
    sb_trimT = b.sb("c_trimT", [128, 128], F32)
    sb_col = b.sb("c_col", [128, 4], F32)
    ones_col, p_col, p2_col = sb_col[:, 0:1], sb_col[:, 1:2], sb_col[:, 2:3]
    sb_erow = b.sb("c_erow", [128, 8], F32)
    sb_onesrow = b.sb("c_onesrow", [1, 128], F32)
    sb_below = b.sb("c_below", [c.NCORE, 2], F32)
    sb_nw = b.sb("c_nw", [128, c.DC], F32)
    sb_scal = b.sb("c_scal", [1, 8], F32)
    sb_gcol = b.sb("c_gcol", [128, 2], F32)
    sb_wg = b.sb("c_wg", [128, c.DC * 9], F32)
    sb_csum = b.sb("c_csum", [1, 16], F32)

    NT_ = c.NTILE_OWN
    st_oh = b.sb("st_oh", [128, NT_ * 16], F32)
    st_c = b.sb("st_c", [128, NT_ * 16], F32)
    st_yg = b.sb("st_yg", [128, NT_ * 2], U32)
    sb_cnt8 = b.sb("st_cnt8", [c.NCORE, 16], F32)
    sb_offs = b.sb("st_offs", [1, 16], F32)
    sb_rows = b.sb("st_rows", [1, 64], F32)
    sb_prefb = b.sb("st_prefb", [128, 24], F32)
    sb_slotcols = b.sb("st_slotc", [128, c.NT_IMG], U32)
    sb_u32 = b.sb("st_u32", [128, 4], U32)
    sb_i32 = b.sb("st_i32", [128, 4], I32)
    sb_cc = b.sb("st_cc", [128, 16], F32)
    sb_v8 = b.sb("st_v8", [128, 64], F32)


    sb_fx = b.sb("sc_fx", [128, c.SLAB], F32)
    sb_fx2 = b.sb("sc_fx2", [128, c.SLAB], F32)
    sb_fx3 = b.sb("sc_fx3", [128, c.SLAB], F32)
    sb_fx4 = b.sb("sc_fx4", [128, c.SLAB], F32)
    sb_xn2 = b.sb("sc_xn2", [128, 2 * c.SLAB], BF16)
    sb_wf = b.sb("sc_wf", [128, 2 * 1024], F32)
    sb_wb = b.sb("sc_wb", [128, 2 * 2048], BF16)
    sb_r1 = b.sb("sc_r1", [1, c.SLAB], F32)
    sb_r2 = b.sb("sc_r2", [1, c.SLAB], F32)
    sb_r3 = b.sb("sc_r3", [1, c.SLAB], F32)
    sb_rr = sb_r3
    sb_pt = [b.sb(f"sc_pt{i}", [128, c.SLAB], BF16) for i in range(4)]
    sb_qn = [b.sb(f"sc_qn{i}", [128, c.HDC * c.SLAB], BF16) for i in range(2)]
    sb_onesb = b.sb("sc_onesb", [128, 1], BF16)
    sb_ab = b.sb("sc_ab", [128, 2 * c.SLAB], BF16)

    u0 = b._sb_off
    sb_sq = b.sb("u_sq", [128, c.DC * c.SLAB], F32, offset=u0)
    sb_tm = b.sb("u_tm", [128, c.TPS * 2 * c.D], BF16, offset=u0)
    sb_st = b.sb("u_st", [128, c.TPS * c.D], F32, offset=u0)
    ao_off = u0 + c.TPS * c.D * 4
    sb_aoin = [b.sb("u_ao0", [128, c.DC * c.SLAB], BF16, offset=ao_off),
               b.sb("u_ao1", [128, c.DC * c.SLAB], BF16,
                    offset=ao_off + c.DC * c.SLAB * 2)]
    usz = max(c.TPS * 2 * c.D * 2, c.TPS * c.D * 4 + 2 * c.DC * c.SLAB * 2,
              c.DC * c.SLAB * 4, 2 * c.KT * c.NKV * c.HD * 2,
              2 * 2 * c.D * 2 + 512 * 4)
    b._sb_off = (u0 + usz + 31) // 32 * 32

    base0 = b._sb_off
    LW = c.L
    sb_xts = [b.sb("sc_xts0", [128, c.DC * c.SLAB], F32, offset=base0),
              b.sb("sc_xts1", [128, c.DC * c.SLAB], F32,
                   offset=base0 + c.DC * c.SLAB * 4)]
    base = base0 + c.DC * c.SLAB * 4
    sb_h = [b.sb("rH_hi", [128, c.DC * LW], BF16, offset=base),
            b.sb("rH_lo", [128, c.DC * LW], BF16, offset=base + c.DC * LW * 2)]
    rK = base + 2 * (c.DC * LW * 2)
    sb_k = [b.sb("rK_hi", [128, c.NKV * c.HDC * LW], BF16, offset=rK),
            b.sb("rK_lo", [128, c.NKV * c.HDC * LW], BF16,
                 offset=rK + c.NKV * c.HDC * LW * 2)]
    sb_vv = [b.sb("rV_hi", [128, c.KT * c.NKV * c.HD], BF16, offset=u0),
             b.sb("rV_lo", [128, c.KT * c.NKV * c.HD], BF16,
                  offset=u0 + c.KT * c.NKV * c.HD * 2)]
    rEnd = rK + 2 * (c.NKV * c.HDC * LW * 2)

    mo = base0
    sb_xg = [b.sb("mo_xg_hi", [128, c.DC * c.HSLOT], BF16, offset=mo),
             b.sb("mo_xg_lo", [128, c.DC * c.HSLOT], BF16,
                  offset=mo + c.DC * c.HSLOT * 2)]
    mo2 = mo + 2 * c.DC * c.HSLOT * 2
    sb_act = [b.sb("mo_act_hi", [128, c.FC * c.HSLOT], BF16, offset=mo2),
              b.sb("mo_act_lo", [128, c.FC * c.HSLOT], BF16,
                   offset=mo2 + c.FC * c.HSLOT * 2)]
    mo3 = mo2 + 2 * c.FC * c.HSLOT * 2
    sb_gt = b.sb("mo_gt", [128, 2 * 2 * c.D], BF16, offset=u0)
    sb_y = b.sb("mo_y", [128, 512], F32, offset=u0 + 2 * 2 * c.D * 2)
    moEnd = mo3

    fo = base0
    sb_fa = b.sb("fi_a", [128, c.D], F32, offset=fo)
    sb_fb = b.sb("fi_b", [128, c.D], F32, offset=fo + c.D * 4)
    sb_fs = b.sb("fi_s", [128, c.D], F32, offset=fo + 2 * c.D * 4)
    sb_fo = b.sb("fi_o", [128, c.D], F32, offset=fo + 3 * c.D * 4)
    sb_fy = b.sb("fi_y", [128, c.D], F32, offset=fo + 4 * c.D * 4)
    sb_fy2 = b.sb("fi_y2", [128, c.D], F32, offset=fo + 5 * c.D * 4)

    peak = max(rEnd, moEnd, fo + 6 * c.D * 4)
    assert peak <= 229376, f"SBUF over budget: {peak}"

    pb = [nc.alloc_psum_tensor(f"pb{i}", [128, 512], F32).ap() for i in range(6)]
    pbbs = [nc.alloc_psum_tensor(f"pbb{i}", [128, 128], BF16).ap()
            for i in range(2)]
    pbb_r = [None, None]
    pbb_sel = [0]

    def bf16_tp(dst_fn, src, deps):
        bk = pbb_sel[0]
        pbb_sel[0] ^= 1
        w = list(deps)
        if pbb_r[bk] is not None:
            w.append(pbb_r[bk])
        b.n_mm += 1
        th = P.add("tensor",
                   lambda e, ps=pbbs[bk], sr=src: e.transpose(ps, sr, sb_identb),
                   waits=w, inc=PE)
        te = P.add("vector", lambda e, ps=pbbs[bk]: dst_fn(e, ps), waits=[th],
                   inc=DV)
        pbb_r[bk] = te
        return te

    wrot = {"i": 0, "busy": [None, None]}

    def w_unit(src_ap, wcols, waits, q="scalar"):
        i = wrot["i"]
        wrot["i"] = (i + 1) % 2
        fslot = sb_wf[:, i * 1024:i * 1024 + wcols]
        h = sb_wb[:, i * 2048:i * 2048 + wcols]
        l = sb_wb[:, i * 2048 + 1024:i * 2048 + 1024 + wcols]
        w = list(waits)
        if wrot["busy"][i] is not None:
            w.append(wrot["busy"][i])
        t0 = b.dma(q, fslot, src_ap, waits=w, ch=b.WSEM[i])
        t1 = b.vec(lambda e, h=h, f=fslot: e.tensor_copy(h, f), waits=[t0])
        t2 = b.vec(lambda e, h=h, l=l, f=fslot: e.tensor_sub(l, f, h), waits=[t1])
        return t2, h, l, i

    def w_used(i, tok):
        wrot["busy"][i] = tok

    CS = b.CST
    cl = [b.dma("sync", sb_ident, ident, ch=CS),
          b.dma("sync", sb_identb, identb, ch=CS),
          b.dma("sync", sb_triu, triu, ch=CS),
          b.dma("sync", sb_trimT, trimaskT, ch=CS),
          b.dma("sync", sb_col, colpack, ch=CS),
          b.dma("sync", sb_erow, erow, ch=CS),
          b.dma("sync", sb_onesrow, onesrow, ch=CS),
          b.dma("sync", sb_below, below, ch=CS),
          b.dma("sync", sb_nw, norm_w.rearrange("(a p) -> p a", p=128),
                slow=True, ch=CS),
          b.dma("sync", sb_scal[0:1, 0:1], gate_s, ch=CS),
          b.dma("sync", sb_scal[0:1, 1:2], ltemp, ch=CS)]
    for dc in range(c.DC):
        cl.append(b.dma("sync", sb_wg[:, dc * 9:dc * 9 + 8],
                        Wg[dc * 128:(dc + 1) * 128, :], ch=CS))
    t_ts = b.sca(lambda e: e.activation(sb_scal[0:1, 2:3], sb_scal[0:1, 1:2],
                                        ACT.Exp), waits=cl)
    b.mm(1, pb[1][:, 0:1], sb_onesrow, sb_scal[0:1, 0:1], True, True, waits=cl)
    b.mm(1, pb[1][:, 1:2], sb_onesrow, sb_scal[0:1, 2:3], True, True, waits=[t_ts])
    t_gce = b.rd(1, "vector", lambda e: e.tensor_copy(sb_gcol, pb[1][:, 0:2]))

    wgo = [b.vec(lambda e, dc=dc: e.memset(sb_wg[:, dc * 9 + 8:dc * 9 + 9], 1.0),
                 waits=cl) for dc in range(c.DC)]
    for dc in range(c.DC):
        b.mm(2, pb[2][0:1, 0:8], ones_col, sb_wg[:, dc * 9:dc * 9 + 8],
             dc == 0, dc == c.DC - 1, waits=cl, inc=(dc == c.DC - 1))
    t_cs = b.rd(2, "vector", lambda e: e.tensor_scalar_mul(
        sb_csum[0:1, 0:8], pb[2][0:1, 0:8], -1.0 / c.D))
    b.mm(3, pb[3][:, 0:8], sb_onesrow, sb_csum[0:1, 0:8], True, True, waits=[t_cs])
    wg_fix = []
    for dc in range(c.DC):
        wg_fix.append(b.rd(3, "vector", lambda e, dc=dc: e.tensor_add(
            sb_wg[:, dc * 9:dc * 9 + 8], sb_wg[:, dc * 9:dc * 9 + 8], pb[3][:, 0:8])))

    t_m1 = b.vec(lambda e: e.memset(sb_slotcols, c.TOWN))
    t_si = b.dma("sync", slotimg, sb_slotcols, waits=[t_m1], ch=CS)
    t_m2 = b.vec(lambda e: e.memset(sb_tm[:, 0:2 * c.D], 0.0))
    t_zx = b.dma("sync", xnbuf[c.TOWN:c.TOWN + 1, :], sb_tm[0:1, 0:2 * c.D],
                 waits=[t_m2], ch=CS)
    t_m3 = b.vec(lambda e: e.memset(sb_fx[:, 0:c.SLAB], 0.0))
    for j in range(0, c.D, c.SLAB):
        w = min(c.SLAB, c.D - j)
        b.dma("sync", yslots[c.NSLOT:c.NSLOT + 1, j:j + w], sb_fx[0:1, 0:w],
              waits=[t_m3], ch=CS)
    tok_offs = b.vec(lambda e: e.memset(sb_offs, 0.0))
    t_ob = b.vec(lambda e: e.memset(sb_onesb, 1.0))
    cl.append(t_ob)

    b.mark("R")
    xts_free = [None, None]
    tile_done = {}
    rr_readers = []
    sqmm_prev = []
    cnt_prev = []
    v6_prev, v10_prev = [], []
    for s in range(c.NSLAB_ALL):
        own = s < c.NSLAB_OWN
        buf = s % 2
        w = ([xts_free[buf]] if xts_free[buf] else []) + (cl if s < 2 else [])
        t_x = b.dma("sync", sb_xts[buf],
                    xT[:, s * c.DC * c.SLAB:(s + 1) * c.DC * c.SLAB],
                    waits=w, ch=b.XT[buf])
        tok_sq = None
        if own:
            t_sqa = b.sca(lambda e, bu=buf: e.activation(sb_sq, sb_xts[bu],
                                                         ACT.Square),
                          waits=[t_x] + ([t_zx, t_m2] if s == 0 else [])
                          + sqmm_prev)
            for dc in range(c.DC):
                b.mm(3, pb[3][0:1, 0:c.SLAB], ones_col,
                     sb_sq[:, dc * c.SLAB:(dc + 1) * c.SLAB], dc == 0,
                     dc == c.DC - 1, waits=[t_sqa], inc=(dc == c.DC - 1))
            sqmm_prev = [(PE, PE.n)]
            tok_sq = b.rd(3, "vector", lambda e: e.tensor_copy(
                sb_rr, pb[3][0:1, 0:c.SLAB]), waits=rr_readers)
            rr_readers = []
        rng = s // c.NSLAB_OWN
        last_tok = t_x
        for tt in range(c.TPS):
            tile = s * c.TPS + tt
            ncols = 9 if own else 8
            mm_w = [t_x] + (wg_fix + wgo if s == 0 else [])
            for dc in range(c.DC):
                b.mm(2, pb[2][:, 0:ncols],
                     sb_xts[buf][:, dc * c.SLAB + tt * 128:dc * c.SLAB + tt * 128 + 128],
                     sb_wg[:, dc * 9:dc * 9 + ncols], dc == 0, dc == c.DC - 1,
                     waits=mm_w, inc=(dc == c.DC - 1))
            u8 = pb[2][:, 0:8]
            ohsl = st_oh[:, tile * 16:tile * 16 + 16] if own else sb_v8[:, 32:48]
            t1 = b.rd(2, "vector", lambda e: e.reduce_max(sb_cc[:, 0:1], u8, AX.X,
                                                          negate=True))
            t2 = b.rd(2, "vector", lambda e, o=ohsl: e.tensor_scalar(
                o[:, 0:8], u8, sb_cc[:, 0:1], 0.0, ALU.add, ALU.is_equal),
                waits=[t1] + ([] if own else cnt_prev))
            t3 = b.rd(2, "vector", lambda e, o=ohsl: e.scalar_tensor_tensor(
                sb_v8[:, 0:8], o[:, 0:8], NEG, u8, ALU.mult, ALU.add), waits=[t2])
            t4 = b.vec(lambda e: e.reduce_max(sb_cc[:, 1:2], sb_v8[:, 0:8], AX.X,
                                              negate=True), waits=[t3])
            t5 = b.vec(lambda e, o=ohsl: e.tensor_scalar(
                o[:, 8:16], sb_v8[:, 0:8], sb_cc[:, 1:2], 0.0, ALU.add,
                ALU.is_equal), waits=[t4])
            start_cnt = (s % c.NSLAB_OWN) == 0 and tt == 0
            stop_cnt = (s % c.NSLAB_OWN) == c.NSLAB_OWN - 1 and tt == c.TPS - 1
            b.mm(4, pb[4][0:1, 0:16], ones_col, ohsl, start_cnt, stop_cnt,
                 waits=[t5])
            last_tok = (PE, PE.n)
            cnt_prev = [last_tok]
            if stop_cnt:
                tcv = b.rd(4, "vector", lambda e: e.tensor_copy(
                    sb_rows[0:1, 0:16], pb[4][0:1, 0:16]))
                b.dma("sync", sb_cnt8[rng:rng + 1, :], sb_rows[0:1, 0:16],
                      waits=[tcv])
                last_tok = tcv
            if own:
                cc0 = st_c[:, tile * 16:tile * 16 + 16]
                ta = b.vec(lambda e, o=ohsl: e.tensor_mul(sb_v8[:, 8:16], o[:, 0:8],
                                                          sb_erow), waits=[t5])
                b.vec(lambda e, cc=cc0: e.reduce_sum(cc[:, 3:4], sb_v8[:, 8:16],
                                                     AX.X), waits=[ta])
                tb_ = b.vec(lambda e, o=ohsl: e.tensor_mul(sb_v8[:, 8:16], o[:, 8:16],
                                                           sb_erow), waits=[ta])
                b.vec(lambda e, cc=cc0: e.reduce_sum(cc[:, 4:5], sb_v8[:, 8:16],
                                                     AX.X), waits=[tb_])
                tmc = b.rd(2, "vector", lambda e, cc=cc0: e.tensor_scalar_mul(
                    cc[:, 7:8], pb[2][:, 8:9], 1.0 / c.D))
                ttp = b.tp(5, pb[5][:, 0:1], sb_rr[0:1, tt * 128:tt * 128 + 128],
                           sb_ident[0:1, 0:1], waits=[tok_sq])
                rr_readers.append(ttp)
                v1 = b.rd(5, "vector", lambda e: e.tensor_scalar_mul(
                    sb_cc[:, 2:3], pb[5][:, 0:1], 1.0 / c.D))
                v2 = b.vec(lambda e, cc=cc0: e.tensor_mul(sb_cc[:, 3:4], cc[:, 7:8],
                                                          cc[:, 7:8]), waits=[tmc])
                v3 = b.vec(lambda e: e.tensor_sub(sb_cc[:, 4:5], sb_cc[:, 2:3],
                                                  sb_cc[:, 3:4]), waits=[v1, v2])
                v4 = b.vec(lambda e: e.tensor_scalar_add(sb_cc[:, 5:6],
                                                         sb_cc[:, 4:5], 1e-5),
                           waits=[v3])
                v5 = b.vec(lambda e: e.reciprocal(sb_cc[:, 6:7], sb_cc[:, 5:6]),
                           waits=[v4] + v6_prev)
                v6 = b.sca(lambda e, cc=cc0: e.activation(cc[:, 8:9], sb_cc[:, 6:7],
                                                          ACT.Sqrt), waits=[v5])
                v6_prev = [v6]
                v7 = b.vec(lambda e: e.tensor_mul(sb_cc[:, 7:8], sb_cc[:, 4:5],
                                                  sb_cc[:, 6:7]), waits=[v5])
                v8_ = b.vec(lambda e: e.tensor_scalar_add(sb_cc[:, 8:9],
                                                          sb_cc[:, 7:8], 1e-6),
                            waits=[v7])
                v9 = b.vec(lambda e: e.reciprocal(sb_cc[:, 9:10], sb_cc[:, 8:9]),
                           waits=[v8_] + v10_prev)
                v10 = b.sca(lambda e, cc=cc0: e.activation(
                    cc[:, 9:10], sb_cc[:, 9:10], ACT.Sqrt), waits=[v9])
                v10_prev = [v10]
                tl = b.rd(2, "vector", lambda e, cc=cc0: e.tensor_scalar_mul(
                    sb_v8[:, 16:24], u8, cc[:, 8:9]), waits=[v6, t5])
                tl2 = b.vec(lambda e: e.reduce_max(sb_cc[:, 10:11], sb_v8[:, 16:24],
                                                   AX.X, negate=True), waits=[tl])
                tl3 = b.sca(lambda e, cc=cc0: e.activation(
                    sb_v8[:, 24:32], sb_v8[:, 16:24], ACT.Exp, bias=sb_cc[:, 10:11],
                    accum_out=cc[:, 2:3]), waits=[tl2])
                tl4 = b.vec(lambda e, o=ohsl: e.tensor_mul(
                    sb_v8[:, 16:24], o[:, 0:8], sb_v8[:, 24:32]), waits=[tl3])
                b.vec(lambda e, cc=cc0: e.reduce_sum(cc[:, 0:1], sb_v8[:, 16:24],
                                                     AX.X), waits=[tl4])
                tl5 = b.vec(lambda e, o=ohsl: e.tensor_mul(
                    sb_v8[:, 16:24], o[:, 8:16], sb_v8[:, 24:32]), waits=[tl4])
                b.vec(lambda e, cc=cc0: e.reduce_sum(cc[:, 1:2], sb_v8[:, 16:24],
                                                     AX.X), waits=[tl5])
                b.mm(0, pb[0][:, 0:16], sb_triu, ohsl, True, False,
                     waits=[t5, tok_offs], inc=False)
                b.mm(0, pb[0][:, 0:16], sb_onesrow, sb_offs, False, True,
                     waits=[tok_offs])
                b.mm(1, pb[1][0:1, 0:16], ones_col, ohsl, True, True, waits=[t5])
                tok_offs = b.rd(1, "vector", lambda e: e.tensor_add(
                    sb_offs, sb_offs, pb[1][0:1, 0:16]))
                w1 = b.rd(0, "vector", lambda e, o=ohsl: e.tensor_mul(
                    sb_v8[:, 16:24], o[:, 0:8], pb[0][:, 0:8]))
                b.vec(lambda e, cc=cc0: e.reduce_sum(cc[:, 5:6], sb_v8[:, 16:24],
                                                     AX.X), waits=[w1])
                w2 = b.rd(0, "vector", lambda e, o=ohsl: e.tensor_mul(
                    sb_v8[:, 16:24], o[:, 8:16], pb[0][:, 8:16]), waits=[w1])
                tloc = b.vec(lambda e, cc=cc0: e.reduce_sum(
                    cc[:, 6:7], sb_v8[:, 16:24], AX.X), waits=[w2])
                tile_done[tile] = tloc
                last_tok = tloc
        xts_free[buf] = last_tok

    bar_r = P.barrier()
    b.mm(0, pb[0][0:1, 0:16], sb_below[:, 0:1], sb_cnt8, True, True, waits=bar_r)
    b.mm(1, pb[1][0:1, 0:16], sb_below[:, 1:2], sb_cnt8, True, True, waits=bar_r)
    tpv = b.rd(0, "vector", lambda e: e.tensor_copy(sb_rows[0:1, 0:16],
                                                    pb[0][0:1, 0:16]))
    ttv = b.rd(1, "vector", lambda e: e.tensor_copy(sb_rows[0:1, 16:32],
                                                    pb[1][0:1, 0:16]))
    tpg = b.vec(lambda e: e.tensor_add(sb_rows[0:1, 8:16], sb_rows[0:1, 8:16],
                                       sb_rows[0:1, 16:24]), waits=[tpv, ttv])
    b.mm(2, pb[2][:, 0:16], sb_onesrow, sb_rows[0:1, 0:16], True, True,
         waits=[tpg])
    tpbe = b.rd(2, "vector", lambda e: e.tensor_copy(sb_prefb[:, 0:16],
                                                     pb[2][:, 0:16]))
    b.mm(3, pb[3][:, 0:8], sb_onesrow, sb_cnt8[0:1, 0:8], True, True,
         waits=bar_r)
    tpc0 = b.rd(3, "vector", lambda e: e.tensor_copy(sb_prefb[:, 16:24],
                                                     pb[3][:, 0:8]))

    scat = []
    for tile in range(c.NTILE_OWN):
        cc0 = st_c[:, tile * 16:tile * 16 + 16]
        ohsl = st_oh[:, tile * 16:tile * 16 + 16]
        dep = [tpbe, tile_done[tile]]
        for kk in range(2):
            oh = ohsl[:, 8 * kk:8 * kk + 8]
            locw = cc0[:, 5 + kk:6 + kk]
            ek = cc0[:, 3 + kk:4 + kk]
            a1 = b.vec(lambda e, oh=oh, kk=kk: e.tensor_mul(
                sb_v8[:, 0:8], oh, sb_prefb[:, 8 * kk:8 * kk + 8]), waits=dep)
            a2 = b.vec(lambda e: e.reduce_sum(sb_cc[:, 0:1], sb_v8[:, 0:8], AX.X),
                       waits=[a1])
            a3 = b.vec(lambda e, locw=locw: e.tensor_add(sb_cc[:, 1:2], locw,
                                                         sb_cc[:, 0:1]), waits=[a2])
            a4 = b.vec(lambda e: e.tensor_scalar(sb_cc[:, 2:3], sb_cc[:, 1:2],
                                                 float(c.CGLOB), None, ALU.is_lt),
                       waits=[a3])
            if kk == 0:
                aL = b.vec(lambda e, locw=locw: e.tensor_copy(sb_cc[:, 15:16],
                                                              locw), waits=[a3])
            else:
                aL1 = b.vec(lambda e, oh=oh: e.tensor_mul(
                    sb_v8[:, 8:16], oh, sb_prefb[:, 16:24]), waits=[a3, tpc0])
                aL2 = b.vec(lambda e: e.reduce_sum(sb_cc[:, 14:15],
                                                   sb_v8[:, 8:16], AX.X),
                            waits=[aL1])
                aL = b.vec(lambda e, locw=locw: e.tensor_add(
                    sb_cc[:, 15:16], locw, sb_cc[:, 14:15]), waits=[aL2])
            a5 = b.vec(lambda e: e.tensor_scalar(
                sb_cc[:, 3:4], sb_cc[:, 15:16], float(c.CLOC), None, ALU.is_lt),
                waits=[aL])
            a6 = b.vec(lambda e: e.tensor_mul(sb_cc[:, 4:5], sb_cc[:, 2:3],
                                              sb_cc[:, 3:4]), waits=[a4, a5])
            b.vec(lambda e, cc=cc0, kk=kk: e.tensor_mul(
                cc[:, 12 + kk:13 + kk], cc[:, kk:kk + 1], sb_cc[:, 2:3]), waits=[a4])
            a8 = b.vec(lambda e, ek=ek: e.scalar_tensor_tensor(
                sb_cc[:, 5:6], ek, float(c.CLOC), sb_cc[:, 15:16], ALU.mult,
                ALU.add), waits=[a6])
            a9 = b.vec(lambda e, tile=tile, kk=kk: e.tensor_scalar_add(
                sb_cc[:, 6:7], p2_col, float(c.NSLOT + 256 * tile + kk)), waits=[a6])
            a6i = b.gpc(lambda e: e.tensor_copy(sb_i32[:, 3:4], sb_cc[:, 4:5]),
                        waits=[a6])
            a10 = b.vec(lambda e: e.select(sb_cc[:, 7:8], sb_i32[:, 3:4],
                                           sb_cc[:, 5:6], sb_cc[:, 6:7]),
                        waits=[a8, a9, a6i])
            a11 = b.vec(lambda e: e.memset(sb_cc[:, 8:9], float(c.NSLOT)),
                        waits=[a10])
            a12 = b.vec(lambda e: e.select(sb_cc[:, 9:10], sb_i32[:, 3:4],
                                           sb_cc[:, 5:6], sb_cc[:, 8:9]),
                        waits=[a11])
            b.gpc(lambda e, kk=kk, tile=tile: e.tensor_copy(
                st_yg[:, tile * 2 + kk:tile * 2 + kk + 1], sb_cc[:, 9:10]),
                waits=[a12])
            MAGIC = 12582912.0
            m1 = b.vec(lambda e: e.tensor_scalar_mul(sb_v8[:, 24:25],
                                                     sb_cc[:, 7:8], 1.0 / 128.0),
                       waits=[a10])
            m2 = b.vec(lambda e: e.tensor_scalar_add(sb_v8[:, 25:26],
                                                     sb_v8[:, 24:25], -0.49995),
                       waits=[m1])
            m3 = b.vec(lambda e: e.tensor_scalar_add(sb_v8[:, 26:27],
                                                     sb_v8[:, 25:26], MAGIC),
                       waits=[m2])
            m4 = b.vec(lambda e: e.tensor_scalar_add(sb_v8[:, 27:28],
                                                     sb_v8[:, 26:27], -MAGIC),
                       waits=[m3])
            m5 = b.vec(lambda e: e.tensor_scalar_mul(sb_v8[:, 28:29],
                                                     sb_cc[:, 7:8],
                                                     float(c.NT_IMG)),
                       waits=[a10])
            m6 = b.vec(lambda e: e.scalar_tensor_tensor(
                sb_v8[:, 29:30], sb_v8[:, 27:28], -(128.0 * c.NT_IMG - 1.0),
                sb_v8[:, 28:29], ALU.mult, ALU.add), waits=[m4, m5])
            m7 = b.gpc(lambda e: e.tensor_copy(sb_u32[:, 0:1], sb_v8[:, 29:30]),
                       waits=[m6] + scat[-1:])
            b5 = b.vec(lambda e, tile=tile: e.tensor_scalar_add(
                sb_cc[:, 10:11], p_col, float(128 * tile)), waits=[a10])
            b6 = b.gpc(lambda e: e.tensor_copy(sb_u32[:, 1:2], sb_cc[:, 10:11]),
                       waits=[b5] + scat[-1:])
            sc_t = b.gpd(lambda e: e.indirect_dma_start(
                out=slotimg.rearrange("p n -> (p n)").unsqueeze(1),
                out_offset=IndirectOffsetOnAxis(sb_u32[:, 0:1], 0),
                in_=sb_u32[:, 1:2], in_offset=None), waits=[m7, b6, t_si])
            scat.append(sc_t)
        g1 = b.vec(lambda e, cc=cc0: e.tensor_scalar_mul(sb_cc[:, 11:12],
                                                         cc[:, 2:3], 1e-9),
                   waits=[scat[-1]])
        g2 = b.vec(lambda e, cc=cc0: e.tensor_add(sb_cc[:, 12:13], cc[:, 12:13],
                                                  cc[:, 13:14]), waits=[scat[-1]])
        g3 = b.vec(lambda e: e.tensor_max(sb_cc[:, 13:14], sb_cc[:, 12:13],
                                          sb_cc[:, 11:12]), waits=[g1, g2])
        g4 = b.vec(lambda e: e.reciprocal(sb_cc[:, 14:15], sb_cc[:, 13:14]),
                   waits=[g3])
        b.vec(lambda e, cc=cc0: e.tensor_mul(cc[:, 10:11], cc[:, 12:13],
                                             sb_cc[:, 14:15]), waits=[g4])
        b.vec(lambda e, cc=cc0: e.tensor_mul(cc[:, 11:12], cc[:, 13:14],
                                             sb_cc[:, 14:15]), waits=[g4])

    b.mark("postR")
    bar_scat = P.barrier()
    t_slot = b.dma("sync", sb_slotcols, slotimg, waits=bar_scat, ch=CS)

    if upto == "R":
        sdump = DT("sdump", [128, NT_ * 48 + c.NT_IMG + 64], F32,
                   kind="ExternalOutput").ap()
        bar_d = P.barrier()
        b.dma("sync", sdump[:, 0:NT_ * 16], st_c, waits=bar_d)
        b.dma("sync", sdump[:, NT_ * 16:NT_ * 32], st_oh, waits=bar_d)
        b.dma("sync", sdump[:, NT_ * 32:NT_ * 34], st_yg.bitcast(F32), waits=bar_d)
        b.dma("sync", sdump[:, NT_ * 34:NT_ * 34 + c.NT_IMG],
              sb_slotcols.bitcast(F32), waits=[t_slot] + bar_d)
        b.dma("sync", sdump[:, NT_ * 34 + c.NT_IMG:NT_ * 34 + c.NT_IMG + 16],
              sb_prefb[:, 0:16], waits=bar_d)
        fin = P.barrier()
        for eng in ("sync", "scalar", "vector", "tensor", "gpsimd"):
            P.add(eng, lambda e: e.nop(), waits=fin)
        P.emit(nc)
        return nc

    def build_rows(stile, want_r3, waits):
        toks = []
        prev = list(waits)
        for j in range(c.TPS):
            cc0 = st_c[:, (stile + j) * 16:(stile + j) * 16 + 16]
            tmu = b.vec(lambda e, cc=cc0: e.tensor_mul(sb_cc[:, 0:1], cc[:, 7:8],
                                                       cc[:, 8:9]), waits=prev)
            b.tp(5, pb[5][0:1, 0:128], cc0[:, 8:9], sb_ident, waits=prev)
            te1 = b.rd(5, "vector", lambda e, j=j: e.tensor_copy(
                sb_r1[0:1, j * 128:(j + 1) * 128], pb[5][0:1, 0:128]))
            b.tp(5, pb[5][0:1, 0:128], sb_cc[:, 0:1], sb_ident, waits=[tmu])
            te2 = b.rd(5, "vector", lambda e, j=j: e.tensor_copy(
                sb_r2[0:1, j * 128:(j + 1) * 128], pb[5][0:1, 0:128]))
            toks += [te1, te2]
            prev = [te2]
            if want_r3:
                b.tp(5, pb[5][0:1, 0:128], cc0[:, 9:10], sb_ident, waits=[te2])
                te3 = b.rd(5, "vector", lambda e, j=j: e.tensor_copy(
                    sb_r3[0:1, j * 128:(j + 1) * 128], pb[5][0:1, 0:128]))
                toks.append(te3)
                prev = [te3]
        return toks

    xn_last = []
    s_stores = []
    a4_c1p, a4_mmp = [], []
    fxprev = []
    prev_ab = []
    for sq in range(c.NSEQ):
        t00 = sq * c.L
        b.mark(f"A1.{sq}")
        h_tok = []
        xslab_free = None
        for sl in range(c.NQS):
            tok0 = t00 + sl * c.SLAB
            stile = tok0 // 128
            row_toks = build_rows(stile, True,
                                  [tile_done[stile + j] for j in range(c.TPS)])
            b.mm(0, pb[0][:, 0:c.SLAB], sb_onesrow, sb_r1[0:1, 0:c.SLAB], True,
                 True, waits=row_toks)
            b.mm(1, pb[1][:, 0:c.SLAB], sb_onesrow, sb_r2[0:1, 0:c.SLAB], True,
                 True, waits=row_toks)
            b.mm(2, pb[2][:, 0:c.SLAB], sb_onesrow, sb_r3[0:1, 0:c.SLAB], True,
                 True, waits=row_toks)
            t_x = b.dma("sync", sb_xts[0],
                        xT[:, (tok0 // c.SLAB) * c.DC * c.SLAB:
                           (tok0 // c.SLAB + 1) * c.DC * c.SLAB],
                        waits=[xslab_free] if xslab_free else bar_scat,
                        ch=b.XT[0])
            tm_toks = []
            c3p, c7p, tmp_ = [], [], []
            for dc in range(c.DC):
                xsl = sb_xts[0][:, dc * c.SLAB:(dc + 1) * c.SLAB]
                c1 = b.rd(0, "vector", lambda e, xsl=xsl: e.tensor_mul(
                    sb_fx, xsl, pb[0][:, 0:c.SLAB]), waits=[t_x] + c3p)
                c2 = b.rd(1, "vector", lambda e: e.tensor_sub(
                    sb_fx, sb_fx, pb[1][:, 0:c.SLAB]), waits=[c1])
                c3 = b.sca(lambda e: e.activation(sb_xn2[:, 0:c.SLAB], sb_fx, ACT.Copy),
                           waits=[c2] + tmp_)
                c4 = b.vec(lambda e: e.tensor_sub(sb_xn2[:, c.SLAB:], sb_fx,
                                                  sb_xn2[:, 0:c.SLAB]), waits=[c3])
                c5 = b.rd(2, "vector", lambda e: e.tensor_mul(
                    sb_fx2, sb_fx, pb[2][:, 0:c.SLAB]), waits=[c2] + c7p)
                c6 = b.vec(lambda e, dc=dc: e.tensor_scalar_mul(
                    sb_fx2, sb_fx2, sb_nw[:, dc:dc + 1]), waits=[c5])
                hh = sb_h[0][:, dc * c.L + sl * c.SLAB:dc * c.L + sl * c.SLAB + c.SLAB]
                hl = sb_h[1][:, dc * c.L + sl * c.SLAB:dc * c.L + sl * c.SLAB + c.SLAB]
                c7 = b.sca(lambda e, hh=hh: e.activation(hh, sb_fx2, ACT.Copy), waits=[c6])
                c8 = b.vec(lambda e, hh=hh, hl=hl: e.tensor_sub(hl, sb_fx2, hh),
                           waits=[c7])
                h_tok.append(c8)
                c3p, c7p = [c3], [c7]
                tmp_ = []
                for j in range(c.TPS):
                    for hilo in range(2):
                        dst = sb_tm[:, j * 2 * c.D + hilo * c.D + dc * 128:
                                    j * 2 * c.D + hilo * c.D + dc * 128 + 128]
                        te = bf16_tp(
                            (lambda e, ps, d=dst: e.tensor_copy(d, ps)),
                            sb_xn2[:, hilo * c.SLAB + j * 128:
                                   hilo * c.SLAB + j * 128 + 128],
                            [c3 if hilo == 0 else c4]
                            + ((xn_last[-c.TPS:] + s_stores[-c.TPS:])
                               if dc == 0 and j == 0 and hilo == 0 else []))
                        tmp_.append(te)
                        tm_toks.append(te)
            xslab_free = tm_toks[-1]
            for j in range(c.TPS):
                xn_last.append(b.dma("sync",
                                     xnbuf[tok0 + j * 128:tok0 + j * 128 + 128, :],
                                     sb_tm[:, j * 2 * c.D:(j + 1) * 2 * c.D],
                                     waits=tm_toks))
        bar_h = h_tok + xn_last[-c.TPS * c.NQS:]

        b.mark(f"A2k.{sq}")
        k_done = []
        for kvh in range(c.NKV):
            for slq in range(c.NQS):
                lastmm = {}
                for fc2 in range(c.HDC):
                    bank = fc2
                    col0 = (kvh * c.HDC + fc2) * 128
                    tw, wuh, wul, wsl = w_unit(
                        Wk[:, col0 * c.DC:(col0 + 128) * c.DC], c.DC * 128,
                        bar_h if (slq == 0 and fc2 == 0 and kvh == 0) else [])
                    for dc in range(c.DC):
                        wh_ = wuh[:, dc * 128:dc * 128 + 128]
                        wl_ = wul[:, dc * 128:dc * 128 + 128]
                        rh = sb_h[0][:, dc * c.L + slq * c.SLAB:dc * c.L + slq * c.SLAB + c.SLAB]
                        rl = sb_h[1][:, dc * c.L + slq * c.SLAB:dc * c.L + slq * c.SLAB + c.SLAB]
                        b.mm(bank, pb[bank][:, 0:c.SLAB], wh_, rh, dc == 0, False,
                             waits=([tw] + bar_h) if dc == 0 else [], inc=False)
                        b.mm(bank, pb[bank][:, 0:c.SLAB], wh_, rl, False, False,
                             inc=False)
                        t_l = b.mm(bank, pb[bank][:, 0:c.SLAB], wl_, rh, False,
                                   dc == c.DC - 1)
                    w_used(wsl, t_l)
                    lastmm[fc2] = t_l
                sq1 = b.rd(0, "scalar", lambda e: e.activation(
                    sb_fx, pb[0][:, 0:c.SLAB], ACT.Square), waits=fxprev)
                b.mm(4, pb[4][0:1, 0:c.SLAB], ones_col, sb_fx, True, c.HDC == 1,
                     waits=[sq1], inc=(c.HDC == 1))
                if c.HDC > 1:
                    sq2 = b.rd(1, "scalar", lambda e: e.activation(
                        sb_fx2, pb[1][:, 0:c.SLAB], ACT.Square))
                    b.mm(4, pb[4][0:1, 0:c.SLAB], ones_col, sb_fx2, False, True,
                         waits=[sq2])
                n1 = b.rd(4, "scalar", lambda e: e.activation(
                    sb_rr, pb[4][0:1, 0:c.SLAB], ACT.Sqrt))
                n2 = b.vec(lambda e: e.tensor_scalar_max(sb_rr, sb_rr, 1e-12),
                           waits=[n1])
                n3 = b.vec(lambda e: e.reciprocal(sb_rr, sb_rr), waits=[n2])
                b.mm(5, pb[5][:, 0:c.SLAB], sb_onesrow, sb_rr, True, True,
                     waits=[n3])
                be = b.rd(5, "vector", lambda e: e.tensor_copy(
                    sb_fx3, pb[5][:, 0:c.SLAB]))
                hhp = []
                for fc2 in range(c.HDC):
                    base_k = kvh * c.HDC * c.L + fc2 * c.L + slq * c.SLAB
                    pr = b.rd(fc2, "vector", lambda e, fc2=fc2: e.tensor_mul(
                        sb_fx, pb[fc2][:, 0:c.SLAB], sb_fx3), waits=[be] + hhp)
                    hh = b.sca(lambda e, base_k=base_k: e.activation(
                        sb_k[0][:, base_k:base_k + c.SLAB], sb_fx, ACT.Copy), waits=[pr])
                    hhp = [hh]
                    ll = b.vec(lambda e, base_k=base_k: e.tensor_sub(
                        sb_k[1][:, base_k:base_k + c.SLAB], sb_fx,
                        sb_k[0][:, base_k:base_k + c.SLAB]), waits=[hh])
                    k_done.append(ll)
                fxprev = [ll, be, hh]
        b.mark(f"A2v.{sq}")
        v_done = {}
        NVG = min(4, c.KT)
        for tg in range(0, c.KT, NVG):
            ng = min(NVG, c.KT - tg)
            lastv = {}
            VW = c.NKV * c.HD
            for dp in range(0, c.DC, 2):
                tw, wuh, wul, wsl = w_unit(Wv[:, dp * VW:(dp + 2) * VW], 2 * VW,
                                           bar_h if tg == 0 and dp == 0 else [])
                for d2 in range(2):
                    dc = dp + d2
                    wvh = wuh[:, d2 * VW:(d2 + 1) * VW]
                    wvl = wul[:, d2 * VW:(d2 + 1) * VW]
                    for ti in range(ng):
                        tt = tg + ti
                        bank = ti
                        l0 = sb_h[0][:, dc * c.L + tt * 128:dc * c.L + tt * 128 + 128]
                        l1 = sb_h[1][:, dc * c.L + tt * 128:dc * c.L + tt * 128 + 128]
                        b.mm(bank, pb[bank][:, 0:VW], l0, wvh, dc == 0,
                             False, waits=(([tw] + bar_h) if ti == 0 and d2 == 0
                                           else []), inc=False)
                        b.mm(bank, pb[bank][:, 0:VW], l0, wvl, False, False,
                             inc=False)
                        lastv[ti] = b.mm(bank, pb[bank][:, 0:VW], l1, wvh,
                                         False, dc == c.DC - 1)
                w_used(wsl, lastv[ng - 1])
            for ti in range(ng):
                tt = tg + ti
                vh = sb_vv[0][:, tt * c.NKV * c.HD:(tt + 1) * c.NKV * c.HD]
                vl = sb_vv[1][:, tt * c.NKV * c.HD:(tt + 1) * c.NKV * c.HD]
                e1 = b.rd(ti, "scalar", lambda e, vh=vh, ti=ti: e.activation(
                    vh, pb[ti][:, 0:c.NKV * c.HD], ACT.Copy))
                e2 = b.rd(ti, "vector", lambda e, vh=vh, vl=vl, ti=ti: e.tensor_sub(
                    vl, pb[ti][:, 0:c.NKV * c.HD], vh), waits=[e1])
                v_done[tt] = e2
        v_all = list(v_done.values())

        for kvh in range(c.NKV):
            for qh in range(kvh * c.NREP, (kvh + 1) * c.NREP):
                for qs in range(c.NQS):
                    b.mark(f"A3.{sq}.{qh}.{qs}")
                    for fc2 in range(c.HDC):
                        bank = fc2
                        col0 = (qh * c.HDC + fc2) * 128
                        tw, wuh, wul, wsl = w_unit(
                            Wq[:, col0 * c.DC:(col0 + 128) * c.DC],
                            c.DC * 128, [])
                        for dc in range(c.DC):
                            wh_ = wuh[:, dc * 128:dc * 128 + 128]
                            wl_ = wul[:, dc * 128:dc * 128 + 128]
                            rh = sb_h[0][:, dc * c.L + qs * c.SLAB:dc * c.L + qs * c.SLAB + c.SLAB]
                            rl = sb_h[1][:, dc * c.L + qs * c.SLAB:dc * c.L + qs * c.SLAB + c.SLAB]
                            b.mm(bank, pb[bank][:, 0:c.SLAB], wh_, rh, dc == 0,
                                 False, waits=([tw] + bar_h) if dc == 0 else [],
                                 inc=False)
                            b.mm(bank, pb[bank][:, 0:c.SLAB], wh_, rl, False,
                                 False, inc=False)
                            t_l = b.mm(bank, pb[bank][:, 0:c.SLAB], wl_, rh,
                                       False, dc == c.DC - 1)
                        w_used(wsl, t_l)
                    sq1 = b.rd(0, "scalar", lambda e: e.activation(
                        sb_fx, pb[0][:, 0:c.SLAB], ACT.Square), waits=fxprev)
                    b.mm(4, pb[4][0:1, 0:c.SLAB], ones_col, sb_fx, True,
                         c.HDC == 1, waits=[sq1], inc=(c.HDC == 1))
                    if c.HDC > 1:
                        sq2 = b.rd(1, "scalar", lambda e: e.activation(
                            sb_fx2, pb[1][:, 0:c.SLAB], ACT.Square))
                        b.mm(4, pb[4][0:1, 0:c.SLAB], ones_col, sb_fx2, False,
                             True, waits=[sq2])
                    n1 = b.rd(4, "scalar", lambda e: e.activation(
                        sb_rr, pb[4][0:1, 0:c.SLAB], ACT.Sqrt))
                    n2 = b.vec(lambda e: e.tensor_scalar_max(sb_rr, sb_rr, 1e-12),
                               waits=[n1])
                    n3 = b.vec(lambda e: e.reciprocal(sb_rr, sb_rr), waits=[n2])
                    n4 = b.vec(lambda e: e.tensor_scalar_mul(
                        sb_rr, sb_rr, sb_scal[0:1, 2:3]), waits=[n3, t_ts])
                    b.mm(5, pb[5][:, 0:c.SLAB], sb_onesrow, sb_rr, True, True,
                         waits=[n4])
                    be = b.rd(5, "vector", lambda e: e.tensor_copy(
                        sb_fx3, pb[5][:, 0:c.SLAB]))
                    qn_tok = []
                    hhp = []
                    for fc2 in range(c.HDC):
                        pr = b.rd(fc2, "vector", lambda e, fc2=fc2: e.tensor_mul(
                            sb_fx, pb[fc2][:, 0:c.SLAB], sb_fx3), waits=[be] + hhp)
                        hh = b.sca(lambda e, fc2=fc2: e.activation(
                            sb_qn[0][:, fc2 * c.SLAB:(fc2 + 1) * c.SLAB], sb_fx,
                            ACT.Copy), waits=[pr])
                        hhp = [hh]
                        ll = b.vec(lambda e, fc2=fc2: e.tensor_sub(
                            sb_qn[1][:, fc2 * c.SLAB:(fc2 + 1) * c.SLAB], sb_fx,
                            sb_qn[0][:, fc2 * c.SLAB:(fc2 + 1) * c.SLAB]),
                            waits=[hh])
                        qn_tok.append(ll)
                    fxprev = [ll, be, hh]
                    ktmax = (qs + 1) * c.TPS
                    pt_free = [None, None]
                    dn = None
                    lastpv = None
                    pl_of = {}

                    def emit_pv(kt, kvh=kvh):
                        nonlocal lastpv, dn
                        bs = kt % 2
                        pl = pl_of[kt]
                        b.mm(1, pb[1][0:1, 0:c.SLAB], sb_onesb, sb_pt[2 * bs],
                             kt == 0, False, waits=[pl, t_ob], inc=False)
                        dn = b.mm(1, pb[1][0:1, 0:c.SLAB], sb_onesb,
                                  sb_pt[2 * bs + 1], False, kt == ktmax - 1,
                                  waits=[pl])
                        for fc2 in range(c.HDC):
                            vb = kt * c.NKV * c.HD + kvh * c.HD + fc2 * 128
                            vap_h = sb_vv[0][:, vb:vb + 128]
                            vap_l = sb_vv[1][:, vb:vb + 128]
                            bank = 2 + fc2
                            b.mm(bank, pb[bank][:, 0:c.SLAB], vap_h,
                                 sb_pt[2 * bs], kt == 0, False,
                                 waits=[pl] + (v_all[-1:] if kt == 0 else []),
                                 inc=False)
                            b.mm(bank, pb[bank][:, 0:c.SLAB], vap_h,
                                 sb_pt[2 * bs + 1], False, False, inc=False)
                            lastpv = b.mm(bank, pb[bank][:, 0:c.SLAB], vap_l,
                                          sb_pt[2 * bs], False, kt == ktmax - 1)
                        pt_free[bs] = lastpv

                    for kt in range(ktmax):
                        first = True
                        for fc2 in range(c.HDC):
                            kb = kvh * c.HDC * c.L + fc2 * c.L + kt * 128
                            kap_h = sb_k[0][:, kb:kb + 128]
                            kap_l = sb_k[1][:, kb:kb + 128]
                            qap_h = sb_qn[0][:, fc2 * c.SLAB:(fc2 + 1) * c.SLAB]
                            qap_l = sb_qn[1][:, fc2 * c.SLAB:(fc2 + 1) * c.SLAB]
                            b.mm(0, pb[0][:, 0:c.SLAB], kap_h, qap_h, first,
                                 False, waits=(qn_tok + k_done[-1:]) if kt == 0
                                 else [], inc=False)
                            first = False
                            b.mm(0, pb[0][:, 0:c.SLAB], kap_h, qap_l, False,
                                 False, inc=False)
                            smm = b.mm(0, pb[0][:, 0:c.SLAB], kap_l, qap_h, False,
                                       fc2 == c.HDC - 1)
                        off = kt * 128 - qs * c.SLAB
                        if 0 <= off < c.SLAB:
                            smm = b.rd(0, "vector", lambda e, off=off: e.tensor_add(
                                pb[0][:, off:off + 128], pb[0][:, off:off + 128],
                                sb_trimT))
                            b.bk_w[0] = smm
                        ex = b.rd(0, "scalar", lambda e: e.activation(
                            sb_fx, pb[0][:, 0:c.SLAB], ACT.Exp),
                            waits=(pl_prev if kt > 0 else fxprev))
                        if 0 < off < c.SLAB:
                            ex = b.vec(lambda e, off=off: e.memset(
                                sb_fx[:, 0:off], 0.0), waits=[ex])
                        bs = kt % 2
                        wpt = [x for x in (pt_free[bs],) if x]
                        ph = b.sca(lambda e, bs=bs: e.activation(
                            sb_pt[2 * bs], sb_fx, ACT.Copy), waits=[ex] + wpt)
                        pl = b.vec(lambda e, bs=bs: e.tensor_sub(
                            sb_pt[2 * bs + 1], sb_fx, sb_pt[2 * bs]), waits=[ph])
                        pl_prev = [pl]
                        pl_of[kt] = pl
                        if kt > 0:
                            emit_pv(kt - 1)
                    emit_pv(ktmax - 1)
                    rd_ = b.rd(1, "vector", lambda e: e.reciprocal(
                        sb_rr, pb[1][0:1, 0:c.SLAB]))
                    b.mm(4, pb[4][:, 0:c.SLAB], sb_onesrow, sb_rr, True, True,
                         waits=[rd_])
                    be2 = b.rd(4, "vector", lambda e: e.tensor_copy(
                        sb_fx3, pb[4][:, 0:c.SLAB]))
                    hhp = []
                    for fc2 in range(c.HDC):
                        r_ch = qh * c.HDC + fc2
                        pr = b.rd(2 + fc2, "vector", lambda e, fc2=fc2:
                                  e.tensor_mul(sb_fx, pb[2 + fc2][:, 0:c.SLAB],
                                               sb_fx3), waits=[be2] + hhp)
                        hh = b.sca(lambda e: e.activation(
                            sb_ab[:, 0:c.SLAB], sb_fx, ACT.Copy),
                            waits=[pr] + prev_ab)
                        hhp = [hh]
                        ll = b.vec(lambda e: e.tensor_sub(
                            sb_ab[:, c.SLAB:], sb_fx, sb_ab[:, 0:c.SLAB]),
                            waits=[hh])
                        ab0 = ((sq * 2 + 0) * c.DC + r_ch) * 128
                        ab1 = ((sq * 2 + 1) * c.DC + r_ch) * 128
                        d1 = b.dma("sync",
                              aobuf[ab0:ab0 + 128, qs * c.SLAB:(qs + 1) * c.SLAB],
                              sb_ab[:, 0:c.SLAB], waits=[ll])
                        d2 = b.dma("sync",
                              aobuf[ab1:ab1 + 128, qs * c.SLAB:(qs + 1) * c.SLAB],
                              sb_ab[:, c.SLAB:], waits=[ll])
                        prev_ab = [d1, d2]
                    fxprev = [ll, be2, hh]

        b.mark(f"A4.{sq}")
        bar_att = P.barrier()
        for sl in range(c.NQS):
            tok0 = t00 + sl * c.SLAB
            stile = tok0 // 128
            row_toks = build_rows(stile, False, bar_att)
            b.mm(2, pb[2][:, 0:c.SLAB], sb_onesrow, sb_r1[0:1, 0:c.SLAB], True,
                 True, waits=row_toks)
            b.mm(3, pb[3][:, 0:c.SLAB], sb_onesrow, sb_r2[0:1, 0:c.SLAB], True,
                 True, waits=row_toks)
            t_x = b.dma("sync", sb_xts[0],
                        xT[:, (tok0 // c.SLAB) * c.DC * c.SLAB:
                           (tok0 // c.SLAB + 1) * c.DC * c.SLAB],
                        waits=bar_att + a4_c1p, ch=b.XT[0])
            ao_toks = []
            for r in range(c.DC):
                ab0 = ((sq * 2 + 0) * c.DC + r) * 128
                ab1 = ((sq * 2 + 1) * c.DC + r) * 128
                ao_toks.append(b.dma("sync",
                                     sb_aoin[0][:, r * c.SLAB:(r + 1) * c.SLAB],
                                     aobuf[ab0:ab0 + 128, sl * c.SLAB:(sl + 1) * c.SLAB],
                                     waits=bar_att + a4_mmp))
                ao_toks.append(b.dma("sync",
                                     sb_aoin[1][:, r * c.SLAB:(r + 1) * c.SLAB],
                                     aobuf[ab1:ab1 + 128, sl * c.SLAB:(sl + 1) * c.SLAB],
                                     waits=bar_att))
            st_toks = []
            a4_trp = []
            for dout in range(c.DC):
                bank = dout % 2
                lastmm = None
                tw, wuh, wul, wsl = w_unit(
                    Wo[:, dout * c.DC * 128:(dout + 1) * c.DC * 128],
                    c.DC * 128, [])
                for r in range(c.DC):
                    woh = wuh[:, r * 128:r * 128 + 128]
                    wol = wul[:, r * 128:r * 128 + 128]
                    ah = sb_aoin[0][:, r * c.SLAB:(r + 1) * c.SLAB]
                    al = sb_aoin[1][:, r * c.SLAB:(r + 1) * c.SLAB]
                    b.mm(bank, pb[bank][:, 0:c.SLAB], woh, ah, r == 0, False,
                         waits=([tw] + ao_toks) if r == 0 else [], inc=False)
                    b.mm(bank, pb[bank][:, 0:c.SLAB], woh, al, False, False,
                         inc=False)
                    lastmm = b.mm(bank, pb[bank][:, 0:c.SLAB], wol, ah, False,
                                  r == c.DC - 1)
                w_used(wsl, lastmm)
                xsl = sb_xts[0][:, dout * c.SLAB:(dout + 1) * c.SLAB]
                c1 = b.rd(2, "vector", lambda e, xsl=xsl: e.tensor_mul(
                    sb_fx, xsl, pb[2][:, 0:c.SLAB]), waits=[t_x])
                c2 = b.rd(3, "vector", lambda e: e.tensor_sub(
                    sb_fx, sb_fx, pb[3][:, 0:c.SLAB]), waits=[c1])
                c3 = b.rd(bank, "vector", lambda e, bank=bank:
                          e.scalar_tensor_tensor(sb_fx2, pb[bank][:, 0:c.SLAB],
                                                 sb_gcol[:, 0:1], sb_fx, ALU.mult,
                                                 ALU.add), waits=[c2, t_gce] + a4_trp)
                a4_trp = []
                for j in range(c.TPS):
                    ttr = b.tp(5, pb[5][:, 0:128],
                               sb_fx2[:, j * 128:j * 128 + 128],
                               sb_ident, waits=[c3]
                               + (s_stores[-c.TPS:] if dout == 0 and j == 0
                                  else []))
                    a4_trp.append(ttr)
                    te = b.rd(5, "vector", lambda e, j=j, dout=dout: e.tensor_copy(
                        sb_st[:, j * c.D + dout * 128:j * c.D + dout * 128 + 128],
                        pb[5][:, 0:128]))
                    st_toks.append(te)
                a4_c1p = [c1]
                a4_mmp = [lastmm]
            for j in range(c.TPS):
                s_stores.append(
                    b.dma("sync", sbufd[tok0 + j * 128:tok0 + j * 128 + 128, :],
                          sb_st[:, j * c.D:(j + 1) * c.D], waits=st_toks,
                          ch=b.SS))

    bar_a = P.barrier()
    fx3_readers = []
    m_prp, m_h1p = {0: [], 1: []}, {0: [], 1: []}
    y_busy = [None] * 4
    gt_free = [None, None]
    act_free = None
    nhs = len(c.HSLABS)
    g1_guard = []
    for ee in range(c.E):
        for hf in range(c.NHALF):
            b.mark(f"M.{ee}.{hf}")
            tile0 = ee * c.TPE + hf * c.TPH
            xg_toks = []
            for tt in range(c.TPH):
                gb = tt % 2
                gsl = sb_gt[:, gb * 2 * c.D:(gb + 1) * 2 * c.D]
                wts = [t_slot] + (bar_a if tt < 2 else []) + \
                      ([gt_free[gb]] if gt_free[gb] else []) + \
                      (g1_guard if tt == 0 else [])
                tg = b.gpd(lambda e, gsl=gsl, ti=tile0 + tt: e.indirect_dma_start(
                    out=gsl, out_offset=None, in_=xnbuf,
                    in_offset=IndirectOffsetOnAxis(sb_slotcols[:, ti:ti + 1], 0)),
                    waits=wts, ch=b.GB[gb])
                lastte = None
                for dc in range(c.DC):
                    for hilo in range(2):
                        dst = sb_xg[hilo][:, dc * c.HSLOT + tt * 128:
                                          dc * c.HSLOT + tt * 128 + 128]
                        te = bf16_tp(
                            (lambda e, ps, d=dst: e.tensor_copy(d, ps)),
                            gsl[:, hilo * c.D + dc * 128:
                                hilo * c.D + dc * 128 + 128],
                            [tg])
                        lastte = te
                        xg_toks.append(te)
                gt_free[gb] = lastte
            xg_last = xg_toks[-1]
            act_toks = []
            for fp in range(c.FC):
                lastg1 = {}
                for gi, fch in enumerate((fp, fp + c.FC)):
                    u0_ = (ee * 2 * c.FC + fch) * c.DC * 128
                    tw, wuh, wul, wsl = w_unit(
                        W_in[:, u0_:u0_ + c.DC * 128], c.DC * 128, [])
                    for dc in range(c.DC):
                        wih = wuh[:, dc * 128:dc * 128 + 128]
                        wil = wul[:, dc * 128:dc * 128 + 128]
                        s0 = 0
                        for si, wdt in enumerate(c.HSLABS):
                            bank = gi * nhs + si
                            rhs_h = sb_xg[0][:, dc * c.HSLOT + s0:dc * c.HSLOT + s0 + wdt]
                            rhs_l = sb_xg[1][:, dc * c.HSLOT + s0:dc * c.HSLOT + s0 + wdt]
                            b.mm(bank, pb[bank][:, 0:wdt], wih, rhs_h, dc == 0,
                                 False, waits=([tw, xg_last] if dc == 0 and
                                               si == 0 else []), inc=False)
                            b.mm(bank, pb[bank][:, 0:wdt], wih, rhs_l, False,
                                 False, inc=False)
                            lastg1[(gi, si)] = b.mm(bank, pb[bank][:, 0:wdt], wil,
                                                    rhs_h, False, dc == c.DC - 1)
                            s0 += wdt
                    w_used(wsl, lastg1[(gi, nhs - 1)])
                s0 = 0
                for si, wdt in enumerate(c.HSLABS):
                    par = (fp * nhs + si) % 2
                    gdst = (sb_fx, sb_fx3)[par]
                    pdst = (sb_fx2, sb_fx4)[par]
                    ga = b.rd(si, "scalar", lambda e, d=gdst, si=si, wdt=wdt:
                              e.activation(d[:, 0:wdt], pb[si][:, 0:wdt],
                                           ACT.Gelu), waits=m_prp[par])
                    pr = b.rd(nhs + si, "vector", lambda e, d=pdst, g=gdst,
                              si=si, wdt=wdt:
                              e.tensor_mul(d[:, 0:wdt], g[:, 0:wdt],
                                           pb[nhs + si][:, 0:wdt]),
                              waits=[ga] + m_h1p[par])
                    m_prp[par] = [pr]
                    ah = sb_act[0][:, fp * c.HSLOT + s0:fp * c.HSLOT + s0 + wdt]
                    al = sb_act[1][:, fp * c.HSLOT + s0:fp * c.HSLOT + s0 + wdt]
                    h1 = b.sca(lambda e, ah=ah, d=pdst, wdt=wdt: e.activation(
                        ah, d[:, 0:wdt], ACT.Copy), waits=[pr])
                    m_h1p[par] = [h1]
                    l1 = b.vec(lambda e, ah=ah, al=al, d=pdst, wdt=wdt:
                               e.tensor_sub(al, d[:, 0:wdt], ah), waits=[h1])
                    act_toks.append(l1)
                    s0 += wdt
            g1_guard = [lastg1[(1, nhs - 1)]]
            if "mdump" in dbg:
                ti_ = ee * c.NHALF + hf
                g1_guard.append(b.dma(
                    "sync", xgdump[ti_ * 128:(ti_ + 1) * 128, :], sb_xg[0],
                    waits=[xg_last]))
                FW = c.FC * c.HSLOT
                b.dma("sync", actdump[ti_ * 128:(ti_ + 1) * 128, 0:FW],
                      sb_act[0], waits=act_toks[-1:])
                b.dma("sync", actdump[ti_ * 128:(ti_ + 1) * 128, FW:2 * FW],
                      sb_act[1], waits=act_toks[-1:])
            FH = max(1, c.FC // 2)
            for dout in range(c.DC):
                lastg2 = {}
                for fh in range(c.FC // FH):
                    u0_ = ((ee * (c.FC // FH) + fh) * c.DC + dout) * FH * 128
                    tw, wuh, wul, wsl = w_unit(
                        W_out[:, u0_:u0_ + FH * 128], FH * 128,
                        act_toks[-1:] if fh == 0 and dout == 0 else [])
                    if "mdump" in dbg and fh == 0 and dout == 0:
                        ti_ = ee * c.NHALF + hf
                        b.dma("sync", wbdump[ti_ * 128:(ti_ + 1) * 128, :],
                              sb_wb[:, wsl * 2048:wsl * 2048 + 2048],
                              waits=[tw])
                    for f8 in range(FH):
                        fc = fh * FH + f8
                        wgh = wuh[:, f8 * 128:f8 * 128 + 128]
                        wgl = wul[:, f8 * 128:f8 * 128 + 128]
                        s0 = 0
                        for si, wdt in enumerate(c.HSLABS):
                            bank = (dout * nhs + si) % 4
                            ath = sb_act[0][:, fc * c.HSLOT + s0:fc * c.HSLOT + s0 + wdt]
                            atl = sb_act[1][:, fc * c.HSLOT + s0:fc * c.HSLOT + s0 + wdt]
                            b.mm(bank, pb[bank][:, 0:wdt], wgh, ath, fc == 0,
                                 False, waits=(([tw] + (act_toks[-1:] if fh == 0
                                                        and dout == 0 else []))
                                               if f8 == 0 and si == 0 else []),
                                 inc=False)
                            b.mm(bank, pb[bank][:, 0:wdt], wgh, atl, False, False,
                                 inc=False)
                            lastg2[si] = b.mm(bank, pb[bank][:, 0:wdt], wgl, ath,
                                              False, fc == c.FC - 1)
                            s0 += wdt
                    w_used(wsl, lastg2[nhs - 1])
                s0 = 0
                for si, wdt in enumerate(c.HSLABS):
                    bank = (dout * nhs + si) % 4
                    tev = b.rd(bank, "vector", lambda e, wdt=wdt, bank=bank: e.tensor_copy(
                        sb_fx3[:, 0:wdt], pb[bank][:, 0:wdt]),
                        waits=fx3_readers)
                    fx3_readers = []
                    for j in range(wdt // 128):
                        tb = 4 + (j % 2)
                        ttp = b.tp(tb, pb[tb][:, 0:128],
                                   sb_fx3[:, j * 128:j * 128 + 128], sb_ident,
                                   waits=[tev])
                        fx3_readers.append(ttp)
                        ysl = sb_y[:, (j % 4) * 128:(j % 4) * 128 + 128]
                        te = b.rd(tb, "vector", lambda e, ysl=ysl, tb=tb:
                                  e.tensor_copy(ysl, pb[tb][:, 0:128]),
                            waits=[y_busy[j % 4]] if y_busy[j % 4] else [])
                        slot0 = ee * c.CLOC + hf * c.HSLOT + s0 + j * 128
                        y_busy[j % 4] = b.dma(
                            "sync", yslots[slot0:slot0 + 128,
                                           dout * 128:(dout + 1) * 128],
                            ysl, waits=[te], ch=b.YB[j % 4])
                    s0 += wdt

    b.mark("F")
    bar_m = P.barrier()
    pv = {}
    for tile in range(c.NTILE_OWN):
        tok0 = tile * 128
        cc0 = st_c[:, tile * 16:tile * 16 + 16]
        tg0 = b.gpd(lambda e, tile=tile: e.indirect_dma_start(
            out=sb_fa, out_offset=None, in_=yslots,
            in_offset=IndirectOffsetOnAxis(st_yg[:, tile * 2:tile * 2 + 1], 0)),
            waits=bar_m + [pv["f1"]] if "f1" in pv else bar_m)
        tg1 = b.gpd(lambda e, tile=tile: e.indirect_dma_start(
            out=sb_fb, out_offset=None, in_=yslots,
            in_offset=IndirectOffsetOnAxis(st_yg[:, tile * 2 + 1:tile * 2 + 2], 0)),
            waits=[pv["f2"]] if "f2" in pv else bar_m)
        ts_ = b.dma("sync", sb_fs, sbufd[tok0:tok0 + 128, :],
                    waits=[pv["f4"]] if "f4" in pv else bar_m)
        tx_ = b.dma("scalar", sb_fo, own_x[tok0:tok0 + 128, :],
                    waits=[pv["f5"]] if "f5" in pv else bar_m)
        f1 = b.vec(lambda e, cc=cc0: e.tensor_scalar_mul(sb_fy, sb_fa,
                                                         cc[:, 10:11]),
                   waits=[tg0] + ([pv["f2"], pv["f3"]] if "f2" in pv else []))
        f2 = b.vec(lambda e, cc=cc0: e.scalar_tensor_tensor(
            sb_fy, sb_fb, cc[:, 11:12], sb_fy, ALU.mult, ALU.add),
            waits=[f1, tg1])
        f3 = b.sca(lambda e: e.activation(sb_fy2, sb_fy, ACT.Silu),
                   waits=[f2] + ([pv["dma"]] if "dma" in pv else []))
        pv["f3"] = f3
        f4 = b.vec(lambda e: e.tensor_mul(sb_fy2, sb_fy2, sb_fs), waits=[f3, ts_])
        f5 = b.vec(lambda e: e.tensor_add(sb_fy2, sb_fy2, sb_fo), waits=[f4, tx_])
        pv["dma"] = b.dma("sync", out_own[tok0:tok0 + 128, :], sb_fy2, waits=[f5])
        pv["f1"], pv["f2"], pv["f4"], pv["f5"] = f1, f2, f4, f5

    fin = P.barrier()
    for eng in ("sync", "scalar", "vector", "tensor", "gpsimd"):
        P.add(eng, lambda e: e.nop(), waits=fin)

    global LAST_MARKS
    LAST_MARKS = b.marks
    import os
    if os.environ.get("KBUILD_STATS"):
        for cc_ in P.chans:
            print("chan", cc_.name, "final", cc_.n)
    P.emit(nc)
    return nc


def make_consts(c):
    ident = np.eye(128, dtype=np.float32)
    identb = np.eye(128).astype(ml_dtypes.bfloat16)
    triu = np.triu(np.ones((128, 128), np.float32), 1)
    trimaskT = np.where(np.arange(128)[:, None] > np.arange(128)[None, :],
                        np.float32(NEG), np.float32(0.0))
    colpack = np.zeros((128, 4), np.float32)
    colpack[:, 0] = 1.0
    colpack[:, 1] = np.arange(128)
    colpack[:, 2] = 2 * np.arange(128)
    erow = np.tile(np.arange(8, dtype=np.float32)[None, :], (128, 1))
    onesrow = np.ones((1, 128), np.float32)
    return ident, identb, triu, trimaskT, colpack, erow, onesrow


LAST_MARKS = []


def _pk(w, shape, perm):
    return np.ascontiguousarray(w.reshape(shape).transpose(perm).reshape(128, -1))


def stage_inputs(c, inputs):
    x = np.asarray(inputs["L0_attn_x"], dtype=np.float32)
    xf = np.ascontiguousarray(x.reshape(c.T, c.D))
    xTs = np.ascontiguousarray(xf.T)
    ident, identb, triu, trimaskT, colpack, erow, onesrow = make_consts(c)
    DC, HDC = c.DC, c.HDC
    wq = _pk(np.asarray(inputs["L0_attn_Wq"], dtype=np.float32),
             (DC, 128, c.NH, HDC, 128), (1, 2, 3, 0, 4))
    wk = _pk(np.asarray(inputs["L0_attn_Wk"], dtype=np.float32),
             (DC, 128, c.NKV, HDC, 128), (1, 2, 3, 0, 4))
    wv = _pk(np.asarray(inputs["L0_attn_Wv"], dtype=np.float32),
             (DC, 128, c.NKV * c.HD), (1, 0, 2))
    wo = _pk(np.asarray(inputs["L0_attn_Wo"], dtype=np.float32),
             (DC, 128, DC, 128), (1, 2, 0, 3))
    wg = np.ascontiguousarray(inputs["L0_ffn_Wg"], dtype=np.float32)
    win = _pk(np.asarray(inputs["L0_ffn_W_in"], dtype=np.float32),
              (c.E, DC, 128, 2 * c.FC, 128), (2, 0, 3, 1, 4))
    FH = max(1, c.FC // 2)
    wout = _pk(np.asarray(inputs["L0_ffn_W_out"], dtype=np.float32),
               (c.E, c.FC // FH, FH, 128, DC, 128), (3, 0, 1, 4, 2, 5))
    nw = np.ascontiguousarray(inputs["L0_attn_norm_w"], dtype=np.float32)
    gate = np.asarray(inputs["L0_attn_gate"], dtype=np.float32).reshape(1, 1)
    lt = np.asarray(inputs["L0_attn_log_temperature"],
                    dtype=np.float32).reshape(1, 1)
    in_maps = []
    for core in range(c.NCORE):
        xr = np.roll(xTs, -core * c.TOWN, axis=1) if core else xTs
        xr = _pk(xr, (c.DC, 128, c.T // c.SLAB, c.SLAB), (1, 2, 0, 3))
        belowv = np.zeros((c.NCORE, 2), np.float32)
        for r in range(c.NCORE):
            tru = (core + r) % c.NCORE
            belowv[r, 0] = 1.0 if tru < core else 0.0
            belowv[r, 1] = 1.0
        in_maps.append({
            "xT": np.ascontiguousarray(xr),
            "own_x": np.ascontiguousarray(xf[core * c.TOWN:(core + 1) * c.TOWN]),
            "Wq": wq, "Wk": wk, "Wv": wv, "Wo": wo, "Wg": wg,
            "W_in": win, "W_out": wout, "norm_w": nw,
            "gate_s": gate, "ltemp": lt, "below": belowv,
            "ident": ident, "identb": identb, "triu": triu,
            "trimaskT": trimaskT, "colpack": colpack, "erow": erow,
            "onesrow": onesrow,
        })
    return in_maps


def kernel(**inputs):
    c = Cfg(tiny=False)
    in_maps = stage_inputs(c, inputs)
    nc = build(c, dbg=True)
    res = run_bass_kernel_spmd(nc, in_maps, core_ids=list(range(c.NCORE)))
    out = np.concatenate([r["out"] for r in res.results], axis=0)
    return out.reshape(32, 2048, 1024)
